# version 06
# baseline (speedup 1.0000x reference)
import numpy as np
import ml_dtypes
from contextlib import ExitStack
import concourse.bass as bass
import concourse.mybir as mybir
from concourse.bass import IndirectOffsetOnAxis
from concourse.bass_utils import run_bass_kernel_spmd

F32 = mybir.dt.float32
BF16 = mybir.dt.bfloat16
U32 = mybir.dt.uint32
I32 = mybir.dt.int32
AF = mybir.ActivationFunctionType
ALU = mybir.AluOpType
AXX = mybir.AxisListType.X
BF = ml_dtypes.bfloat16

H = 2048
T = 1024
NH = 16
C = 64
NCH = 16
NDC = 16
NTC = 8
EPS = 1e-5
LNDK = float(np.log(128.0 ** -0.5))
NKV = 4
GRP = 4
NE = 16
NBIS = 30
MAGIC = 0x41FA5A5A
FLAGVAL = np.frombuffer(np.int32(MAGIC).tobytes(), dtype=BF).copy()

ENGINES = ("sync", "gpsimd", "tensor", "scalar", "vector")


class Prog:

    def __init__(self):
        self.ops = {e: [] for e in ENGINES}
        self.tag = {e: 0 for e in ENGINES}
        self.dma = {e: 0 for e in ENGINES}
        self.lastwait = {e: {} for e in ENGINES}

    def op(self, e, fn, tag=False):
        if tag:
            self.tag[e] += 1

            def wrapped(eng, sems, fn=fn, e=e):
                r = fn(eng, sems)
                assert r is not None, "tagged op must return an instruction"
                r.then_inc(sems[e], 1)
            self.ops[e].append(wrapped)
        else:
            self.ops[e].append(fn)

    def dmao(self, e, out, in_):
        self.dma[e] += 16

        def fn(eng, sems, out=out, in_=in_, e=e):
            eng.dma_start(out=out, in_=in_).then_inc(sems[e + "_dma"], 16)
        self.ops[e].append(fn)

    def scat(self, e, out, oidx, in_):
        self.dma[e] += 16

        def fn(eng, sems, out=out, oidx=oidx, in_=in_, e=e):
            eng.indirect_dma_start(
                out=out, out_offset=IndirectOffsetOnAxis(ap=oidx, axis=0),
                in_=in_, in_offset=None).then_inc(sems[e + "_dma"], 16)
        self.ops[e].append(fn)

    def gath(self, e, out, in_, iidx):
        self.dma[e] += 16

        def fn(eng, sems, out=out, in_=in_, iidx=iidx, e=e):
            eng.indirect_dma_start(
                out=out, out_offset=None,
                in_=in_, in_offset=IndirectOffsetOnAxis(ap=iidx, axis=0)).then_inc(sems[e + "_dma"], 16)
        self.ops[e].append(fn)

    def mark(self, prod, dma=False):
        return self.dma[prod] if dma else self.tag[prod]

    def w_at(self, waiter, prod, val, dma=False):
        key = prod + ("_dma" if dma else "")
        if val <= 0 or self.lastwait[waiter].get(key, -1) >= val:
            return
        self.lastwait[waiter][key] = val
        self.ops[waiter].append(lambda eng, sems, key=key, val=val: eng.wait_ge(sems[key], val))

    def w(self, waiter, prod, dma=False):
        key = prod + ("_dma" if dma else "")
        val = self.dma[prod] if dma else self.tag[prod]
        if val <= 0 or self.lastwait[waiter].get(key, -1) >= val:
            return
        self.lastwait[waiter][key] = val
        self.ops[waiter].append(lambda eng, sems, key=key, val=val: eng.wait_ge(sems[key], val))

    def barrier(self):
        for wtr in ENGINES:
            for prod in ENGINES:
                if prod != wtr:
                    self.w(wtr, prod)
                self.w(wtr, prod, dma=True)

    def wv(self, waiter, key, val):
        self.ops[waiter].append(lambda eng, sems, key=key, val=val: eng.wait_ge(sems[key], val))

    def spin_flags(self, e, flag_aps, expect_ap):
        def fn(eng, sems):
            rc = eng.alloc_register(name="r_spin_c")
            ra = eng.alloc_register(name="r_spin_a")
            rexp = eng.alloc_register(name="r_spin_e")
            eng.load(rexp, expect_ap)
            regs = [eng.alloc_register(name=f"r_spin_f{i}") for i in range(len(flag_aps))]
            eng.reg_mov(rc, 1)
            with eng.While(rc):
                for rf, ap in zip(regs, flag_aps):
                    eng.load(rf, ap)
                eng.reg_mov(rc, 0)
                for rf in regs:
                    eng.reg_alu(ra, rf, rexp, ALU.not_equal)
                    eng.reg_alu(rc, rc, ra, ALU.bitwise_or)
            for r in regs + [ra, rc, rexp]:
                eng.free_register(r)
        self.ops[e].append(fn)

    def build(self, nc, ctx):
        semnames = list(ENGINES) + [e + "_dma" for e in ENGINES]
        sems = {s: ctx.enter_context(nc.semaphore(name="sem_" + s)) for s in semnames}
        blk = ctx.enter_context(nc.Block())

        def body(name):
            def f(eng):
                for fn in self.ops[name]:
                    fn(eng, sems)
            return f
        blk.sync(body("sync"))
        blk.gpsimd(body("gpsimd"))
        blk.tensor(body("tensor"))
        blk.scalar(body("scalar"))
        blk.vector(body("vector"))


def chunk(t, j, w=128):
    return t[j * w:(j + 1) * w, :]


def emit_ln_finish(p, psx, psx2, mb, rstdb, scr, rms_only=False, epsc=None):
    p.w("vector", "tensor")
    for half in range(2):
        sl = slice(half * 512, (half + 1) * 512)
        if rms_only:
            p.op("vector", lambda e, s, half=half, sl=sl: e.tensor_scalar(
                out=rstdb[:, sl], in0=psx2[half][:, 0:512], scalar1=1.0 / H,
                scalar2=None, op0=ALU.mult), tag=(half == 1))
        else:
            p.op("vector", lambda e, s, half=half, sl=sl: e.tensor_scalar(
                out=mb[:, sl], in0=psx[half][:, 0:512], scalar1=1.0 / H,
                scalar2=None, op0=ALU.mult))
            p.op("vector", lambda e, s, half=half, sl=sl: e.tensor_scalar(
                out=rstdb[:, sl], in0=psx2[half][:, 0:512], scalar1=1.0 / H,
                scalar2=None, op0=ALU.mult))
            p.op("vector", lambda e, s, sl=sl: e.scalar_tensor_tensor(
                out=scr[:, sl], in0=mb[:, sl], scalar=-1.0, in1=mb[:, sl],
                op0=ALU.mult, op1=ALU.mult))
            p.op("vector", lambda e, s, sl=sl: e.tensor_tensor(
                out=rstdb[:, sl], in0=rstdb[:, sl], in1=scr[:, sl], op=ALU.add),
                tag=(half == 1))
    p.w("scalar", "vector")
    p.op("scalar", lambda e, s: e.activation(out=rstdb[:], in_=rstdb[:], func=AF.Ln, bias=epsc[:, 0:1]))
    p.op("scalar", lambda e, s: e.activation(out=rstdb[:], in_=rstdb[:], func=AF.Exp, scale=-0.5), tag=True)



def build_L1(dbg=False):
    nc = bass.Bass("TRN2", target_bir_lowering=False)
    x_in = nc.dram_tensor("x_fm", [H, T], F32, kind="ExternalInput").ap()
    wq_p = nc.dram_tensor("wq_p", [NDC * 128, H], F32, kind="ExternalInput").ap()
    wf_p = nc.dram_tensor("wf_p", [NDC * 128, H], F32, kind="ExternalInput").ap()
    wi_p = nc.dram_tensor("wi_p", [NDC * 128, H], F32, kind="ExternalInput").ap()
    wo_p = nc.dram_tensor("wo_p", [NDC * 128, H], F32, kind="ExternalInput").ap()
    gwt = nc.dram_tensor("gw", [128, NDC], F32, kind="ExternalInput").ap()
    rw_in = nc.dram_tensor("rw", [128, NDC * 16], F32, kind="ExternalInput").ap()
    rbias = nc.dram_tensor("rb", [1, 16], F32, kind="ExternalInput").ap()
    u512c = nc.dram_tensor("u512", [128, 512], BF16, kind="ExternalInput").ap()
    identc = nc.dram_tensor("ident", [128, 128], BF16, kind="ExternalInput").ap()
    onesc = nc.dram_tensor("ones128", [128, 128], BF16, kind="ExternalInput").ap()
    roleflag = nc.dram_tensor("roleflag", [128, 1], F32, kind="ExternalInput").ap()
    exidx = nc.dram_tensor("exidx", [128, 1], U32, kind="ExternalInput").ap()
    exgidx = nc.dram_tensor("exgidx", [128, 1], U32, kind="ExternalInput").ap()
    exfidx = nc.dram_tensor("exfidx", [2, 1], U32, kind="ExternalInput").ap()
    noncein = nc.dram_tensor("noncef", [2, 2048], BF16, kind="ExternalInput").ap()
    noncei_in = nc.dram_tensor("noncei", [1, 1], I32, kind="ExternalInput").ap()

    x1_out = nc.dram_tensor("x1", [H, T], F32, kind="ExternalOutput").ap()
    xln2_out = nc.dram_tensor("xln2", [H, T], BF16, kind="ExternalOutput").ap()
    p1_out = nc.dram_tensor("p1", [T, 16], F32, kind="ExternalOutput").ap()

    exch = nc.dram_tensor("exch_l1", [258, 2048], BF16, kind="Internal",
                          addr_space="Shared").ap()
    if dbg:
        d_e1 = nc.dram_tensor("d_e1", [128, 1024], F32, kind="ExternalOutput").ap()
        d_qi = nc.dram_tensor("d_qi", [128, 1024], BF16, kind="ExternalOutput").ap()
        d_es = nc.dram_tensor("d_es", [128, 16], F32, kind="ExternalOutput").ap()
        d_qb = nc.dram_tensor("d_qb", [128, 128], BF16, kind="ExternalOutput").ap()
        d_qe = nc.dram_tensor("d_qe", [128, 256], BF16, kind="ExternalOutput").ap()
        d_gt = nc.dram_tensor("d_gt", [128, NTC * H], BF16, kind="ExternalOutput").ap()
        d_vt = nc.dram_tensor("d_vt", [128, NTC * H], BF16, kind="ExternalOutput").ap()
        d_ob = nc.dram_tensor("d_ob", [128, NDC * T], BF16, kind="ExternalOutput").ap()
        d_q = nc.dram_tensor("d_q", [128, NDC * T], BF16, kind="ExternalOutput").ap()

    p = Prog()
    ctx = ExitStack()
    def sb(name, shape, dt):
        return ctx.enter_context(nc.sbuf_tensor("sb_" + name, shape, dt))

    def ps_(name, shape, dt):
        return ctx.enter_context(nc.psum_tensor("ps_" + name, shape, dt))

    qbuf = sb("qbuf", [128, NDC * T], BF16)
    gtok = sb("gtok", [128, NTC * H], BF16)
    xv = sb("xv", [128, NDC * T], BF16)
    xs0 = sb("xs0", [128, T], F32)
    xs1 = sb("xs1", [128, T], F32)
    sqs = sb("sqs", [128, T], BF16)
    bfs = sb("bfs", [128, T], BF16)
    mb = sb("mb", [128, T], F32)
    rstdb = sb("rstdb", [128, T], F32)
    scr = sb("scr", [128, T], F32)
    wt0 = sb("wt0", [128, H], BF16)
    wt1 = sb("wt1", [128, H], BF16)
    rwt = sb("rwt", [128, NDC * 16], BF16)
    rbt = sb("rbt", [1, 16], BF16)
    gwb = sb("gwb", [128, NDC], F32)
    e1s = sb("e1s", [128, 1024], F32)
    e2s = sb("e2s", [128, 1024], F32)
    kds = sb("kds", [128, 1024], BF16)
    qis = sb("qis", [128, 1024], BF16)
    pts = sb("pts", [128, 1024], BF16)
    kdts = sb("kdts", [128, 2048], BF16)
    u512 = sb("u512", [128, 512], BF16)
    ident = sb("ident", [128, 128], BF16)
    ones = sb("ones", [128, 128], BF16)
    rolef = sb("rolef", [128, 1], F32)
    exi = sb("exi", [128, 1], U32)
    exg = sb("exg", [128, 1], U32)
    exf = sb("exf", [2, 1], U32)
    flagst = sb("flagst", [2, 2048], BF16)
    noncei = sb("noncei", [1, 1], I32)
    prb = sb("prb", [128, 16], F32)
    col0 = sb("col0", [128, 1], F32)
    col1 = sb("col1", [128, 1], F32)
    epsc = sb("epsc", [128, 1], F32)
    col2 = sb("col2", [128, 1], F32)
    lndkc = sb("lndkc", [128, 1], F32)
    Mst = sb("Mst", [128, NH * 128], F32)
    Mbf = sb("Mbf", [128, NH * 128], BF16)
    Scum = sb("Scum", [128, NH], F32)
    Gtot = sb("Gtot", [128, NH], F32)
    EGtot = sb("EGtot", [128, NH], F32)
    EScum = sb("EScum", [128, NH], F32)

    pA = ps_("pA", [128, 512], F32)
    pB = ps_("pB", [128, 512], F32)
    pC = ps_("pC", [128, 512], F32)
    pD_ = ps_("pD_", [128, 512], F32)
    pE = ps_("pE", [128, 512], F32)
    pF = ps_("pF", [128, 512], F32)
    pS0 = ps_("pS0", [128, 512], F32)
    pT0 = ps_("pT0", [128, 512], BF16)

    phase1 = ExitStack()
    fm1 = phase1.enter_context(nc.sbuf_tensor("fm1", [128, NDC * T], BF16))
    xln = xv
    vtok = xv

    for dst, src in [(u512[:], u512c), (ident[:], identc), (ones[:], onesc),
                     (rolef[:], roleflag), (exi[:], exidx), (exg[:], exgidx),
                     (exf[:], exfidx), (flagst[:], noncein), (gwb[:], gwt),
                     (noncei[:], noncei_in)]:
        p.dmao("sync", dst, src)
    p.dmao("gpsimd", rbt[:], rbias)
    p.dmao("gpsimd", rwt[:], rw_in)

    p.op("vector", lambda e, s: e.memset(epsc[:], EPS))
    p.op("vector", lambda e, s: e.memset(lndkc[:], LNDK), tag=True)

    for j in range(NDC):
        b = (xs0, xs1)[j % 2]
        if j >= 2:
            p.w("sync", "scalar")
        p.dmao("sync", b[:], chunk(x_in, j))
        p.w("scalar", "sync", dma=True)
        if j >= 1:
            p.w("scalar", "tensor")
        p.op("scalar", lambda e, s, b=b: e.activation(out=sqs[:], in_=b[:], func=AF.Square))
        p.op("scalar", lambda e, s, b=b: e.activation(out=bfs[:], in_=b[:], func=AF.Copy), tag=True)
        p.w("tensor", "scalar")
        for half in range(2):
            p.op("tensor", lambda e, s, j=j, half=half: e.matmul(
                (pA, pB)[half][:, 0:512], ones[:], bfs[:, half * 512:half * 512 + 512],
                start=(j == 0), stop=(j == NDC - 1)))
            p.op("tensor", lambda e, s, j=j, half=half: e.matmul(
                (pC, pD_)[half][:, 0:512], ones[:], sqs[:, half * 512:half * 512 + 512],
                start=(j == 0), stop=(j == NDC - 1)),
                tag=(j == NDC - 1 and half == 1))
    emit_ln_finish(p, (pA, pB), (pC, pD_), mb, rstdb, scr, epsc=epsc)

    for j in range(NDC):
        b = (xs0, xs1)[j % 2]
        if j >= 2:
            p.w("sync", "vector")
        p.dmao("sync", b[:], chunk(x_in, j))
        p.w("vector", "sync", dma=True)
        p.w("vector", "scalar")
        p.op("vector", lambda e, s, b=b: e.tensor_tensor(
            out=scr[:], in0=b[:], in1=mb[:], op=ALU.subtract))
        p.op("vector", lambda e, s, j=j: e.tensor_tensor(
            out=xln[:, j * T:(j + 1) * T], in0=scr[:], in1=rstdb[:], op=ALU.mult), tag=True)

    def proj_fm(wpan, evac):
        bankdone = {0: (0, 0), 1: (0, 0)}
        for do in range(NDC):
            wt = (wt0, wt1)[do % 2]
            if do >= 2:
                p.w("gpsimd", "tensor")
            p.dmao("gpsimd", wt[:], chunk(wpan, do))
            p.w("tensor", "gpsimd", dma=True)
            if do == 0:
                p.w("tensor", "vector")
            for half in range(2):
                psx = (pE, pF)[half]
                sm, vm = bankdone[half]
                p.w_at("tensor", "scalar", sm)
                p.w_at("tensor", "vector", vm)
                for dj in range(NDC):
                    p.op("tensor", lambda e, s, psx=psx, wt=wt, dj=dj, half=half: e.matmul(
                        psx[:, 0:512], wt[:, dj * 128:(dj + 1) * 128],
                        xln[:, dj * T + half * 512: dj * T + half * 512 + 512],
                        start=(dj == 0), stop=(dj == NDC - 1)), tag=(dj == NDC - 1))
                evac(p, psx, do, half)
                bankdone[half] = (p.mark("scalar"), p.mark("vector"))

    def evac_silu(p, psx, do, half):
        p.w("scalar", "tensor")
        p.op("scalar", lambda e, s, psx=psx, do=do, half=half: e.activation(
            out=qbuf[:, do * T + half * 512: do * T + half * 512 + 512],
            in_=psx[:, 0:512], func=AF.Silu), tag=True)

    def evac_g(p, psx, do, half):
        sl = slice(half * 512, (half + 1) * 512)
        p.w("scalar", "tensor")
        p.w("scalar", "vector")
        p.op("scalar", lambda e, s, psx=psx, sl=sl: e.activation(
            out=e1s[:, sl], in_=psx[:, 0:512], func=AF.Abs))
        p.op("scalar", lambda e, s, sl=sl: e.activation(
            out=e1s[:, sl], in_=e1s[:, sl], func=AF.Exp, scale=-1.0))
        p.op("scalar", lambda e, s, sl=sl: e.activation(
            out=e1s[:, sl], in_=e1s[:, sl], func=AF.Ln, bias=1.0), tag=True)
        p.w("vector", "scalar")
        p.op("vector", lambda e, s, psx=psx, sl=sl: e.tensor_scalar(
            out=e2s[:, sl], in0=psx[:, 0:512], scalar1=0.0, scalar2=None, op0=ALU.min))
        p.op("vector", lambda e, s, do=do, half=half, sl=sl: e.tensor_tensor(
            out=fm1[:, do * T + half * 512: do * T + half * 512 + 512],
            in0=e2s[:, sl], in1=e1s[:, sl], op=ALU.subtract), tag=True)

    def evac_copy(p, psx, do, half):
        p.w("scalar", "tensor")
        p.op("scalar", lambda e, s, psx=psx, do=do, half=half: e.activation(
            out=fm1[:, do * T + half * 512: do * T + half * 512 + 512],
            in_=psx[:, 0:512], func=AF.Copy), tag=True)

    proj_fm(wq_p, evac_silu)
    p.barrier()
    if dbg:
        p.w("gpsimd", "scalar")
        p.dmao("gpsimd", d_qe, qbuf[:, 15 * T: 15 * T + 256])
        p.w("scalar", "gpsimd", dma=True)
        p.w("vector", "gpsimd", dma=True)
        p.w("tensor", "gpsimd", dma=True)
    proj_fm(wf_p, evac_g)
    p.barrier()

    def transpose_all(dstbuf):
        for k in range(NDC * NTC):
            dj, tt = divmod(k, NTC)
            slot = k % 4
            if k >= 4 and slot == 0:
                p.w("tensor", "vector")
            p.op("tensor", lambda e, s, dj=dj, tt=tt, slot=slot: e.transpose(
                pT0[:, slot * 128:(slot + 1) * 128],
                fm1[:, dj * T + tt * 128: dj * T + (tt + 1) * 128], ident[:]),
                tag=(slot == 3))
            if slot == 3:
                p.w("vector", "tensor")
                base = k - 3
                for kk in range(base, k + 1):
                    djj, ttt = divmod(kk, NTC)
                    sl2 = kk % 4
                    p.op("vector", lambda e, s, djj=djj, ttt=ttt, sl2=sl2: e.tensor_copy(
                        out=dstbuf[:, ttt * H + djj * 128: ttt * H + (djj + 1) * 128],
                        in_=pT0[:, sl2 * 128:(sl2 + 1) * 128]), tag=(sl2 == 3))

    transpose_all(gtok)
    p.barrier()
    if dbg:
        p.w("gpsimd", "vector")
        p.dmao("gpsimd", d_gt, gtok[:])
    proj_fm(wi_p, evac_copy)
    p.barrier()
    transpose_all(vtok)
    if dbg:
        p.w("gpsimd", "vector")
        p.dmao("gpsimd", d_vt, vtok[:])

    phase1.close()
    obuf = ctx.enter_context(nc.sbuf_tensor("obuf", [128, NDC * T], BF16))

    p.op("vector", lambda e, s: e.memset(Mst[:], 0.0))
    p.op("vector", lambda e, s: e.memset(Mbf[:], 0.0))
    p.op("vector", lambda e, s: e.memset(Scum[:], 0.0), tag=True)

    for c in range(NCH):
        if c % 4 == 0:
            p.barrier()
        tt, par = divmod(c, 2)
        par *= 64
        p.w("tensor", "vector")
        p.w("tensor", "scalar")
        for h in range(NH):
            psx = pA if h < 8 else pB
            colb = (h % 8) * 64
            p.op("tensor", lambda e, s, psx=psx, colb=colb, h=h, tt=tt, par=par: e.matmul(
                psx[:, colb:colb + 64],
                gtok[par:par + 64, tt * H + h * 128: tt * H + (h + 1) * 128],
                u512[par:par + 64, 0:64], start=(h % 8 == 0), stop=(h % 8 == 7)),
                tag=(h in (7, 15)))
        p.w("scalar", "tensor")
        p.w("scalar", "vector")
        p.op("scalar", lambda e, s: e.activation(out=e1s[:, 0:512], in_=pA[:, 0:512], func=AF.Exp, bias=lndkc[:, 0:1]))
        p.op("scalar", lambda e, s: e.activation(out=e1s[:, 512:1024], in_=pB[:, 0:512], func=AF.Exp, bias=lndkc[:, 0:1]))
        p.op("scalar", lambda e, s: e.activation(out=e2s[:, 0:512], in_=pA[:, 0:512], func=AF.Exp, scale=-1.0))
        p.op("scalar", lambda e, s: e.activation(out=e2s[:, 512:1024], in_=pB[:, 0:512], func=AF.Exp, scale=-1.0))
        p.op("scalar", lambda e, s: e.activation(out=EScum[:], in_=Scum[:], func=AF.Exp), tag=True)
        p.w("vector", "tensor")
        p.op("vector", lambda e, s: e.tensor_copy(out=Gtot[:, 0:8], in_=pA[:, 63:512:64]))
        p.op("vector", lambda e, s: e.tensor_copy(out=Gtot[:, 8:16], in_=pB[:, 63:512:64]), tag=True)
        p.w("scalar", "vector")
        p.op("scalar", lambda e, s: e.activation(out=EGtot[:], in_=Gtot[:], func=AF.Exp), tag=True)
        p.w("vector", "scalar")
        p.op("vector", lambda e, s: e.tensor_tensor(out=Scum[:], in0=Scum[:], in1=Gtot[:], op=ALU.add))
        p.op("vector", lambda e, s: e.tensor_tensor(
            out=kds[:, 1:1024], in0=e2s[:, 1:1024], in1=e2s[:, 0:1023], op=ALU.subtract))
        p.op("vector", lambda e, s: e.tensor_scalar(
            out=kds[:, 0:1024:64], in0=e2s[:, 0:1024:64], scalar1=-1.0, scalar2=None,
            op0=ALU.add))
        for h in range(NH):
            p.op("vector", lambda e, s, c=c, h=h: e.tensor_tensor(
                out=qis[:, h * 64:(h + 1) * 64],
                in0=qbuf[:, h * T + c * 64: h * T + (c + 1) * 64],
                in1=e1s[:, h * 64:(h + 1) * 64], op=ALU.mult), tag=(h == NH - 1))
        if dbg and c == 0:
            p.w("gpsimd", "vector")
            p.w("gpsimd", "scalar")
            p.dmao("gpsimd", d_e1, e1s[:])
            p.dmao("gpsimd", d_qi, qis[:])
            p.dmao("gpsimd", d_es, EScum[:])
            p.dmao("gpsimd", d_qb, qbuf[:, 15 * T: 15 * T + 128])
            p.w("vector", "gpsimd", dma=True)
        for h in range(NH):
            p.op("vector", lambda e, s, h=h, c=c: e.tensor_scalar(
                out=qbuf[:, h * T + c * 64: h * T + (c + 1) * 64],
                in0=qis[:, h * 64:(h + 1) * 64],
                scalar1=EScum[:, h:h + 1], scalar2=None, op0=ALU.mult),
                tag=(h == NH - 1))
        p.w("tensor", "vector")
        for h in range(NH):
            psx = pS0 if h < 8 else pE
            colb = (h % 8) * 64
            p.op("tensor", lambda e, s, psx=psx, colb=colb, h=h, par=par: e.matmul(
                psx[par:par + 64, colb:colb + 64], kds[:, h * 64:(h + 1) * 64],
                qis[:, h * 64:(h + 1) * 64], start=(h % 8 == 0), stop=(h % 8 == 7),
                tile_position=(0, par)),
                tag=(h in (7, 15)))
        p.w("vector", "tensor")
        p.op("vector", lambda e, s, par=par: e.tensor_tensor(
            out=pts[par:par + 64, 0:512], in0=pS0[par:par + 64, 0:512],
            in1=u512[par:par + 64, :], op=ALU.mult))
        p.op("vector", lambda e, s, par=par: e.tensor_tensor(
            out=pts[par:par + 64, 512:1024], in0=pE[par:par + 64, 0:512],
            in1=u512[par:par + 64, :], op=ALU.mult))
        p.op("vector", lambda e, s: e.tensor_copy(out=Mbf[:], in_=Mst[:]), tag=True)
        p.w("tensor", "vector")
        p.w("tensor", "scalar")
        for h in range(NH):
            psx = pC if h < 8 else pD_
            colb = (h % 8) * 64
            p.op("tensor", lambda e, s, psx=psx, colb=colb, h=h, tt=tt, par=par: e.matmul(
                psx[:, colb:colb + 64],
                vtok[par:par + 64, tt * H + h * 128: tt * H + (h + 1) * 128],
                pts[par:par + 64, h * 64:(h + 1) * 64], start=(h % 8 == 0), stop=False))
            p.op("tensor", lambda e, s, psx=psx, colb=colb, h=h: e.matmul(
                psx[:, colb:colb + 64], Mbf[:, h * 128:(h + 1) * 128],
                qis[:, h * 64:(h + 1) * 64], start=False, stop=(h % 8 == 7)),
                tag=(h in (7, 15)))
        p.w("scalar", "tensor")
        obv = obuf[:].rearrange("q (h a) -> q h a", h=NH)
        p.op("scalar", lambda e, s, c=c, obv=obv: e.activation(
            out=obv[:, 0:8, c * 64:(c + 1) * 64],
            in_=pC[:, 0:512].rearrange("q (h a) -> q h a", h=8), func=AF.Copy))
        p.op("scalar", lambda e, s, c=c, obv=obv: e.activation(
            out=obv[:, 8:16, c * 64:(c + 1) * 64],
            in_=pD_[:, 0:512].rearrange("q (h a) -> q h a", h=8), func=AF.Copy), tag=True)
        for hg in range(2):
            p.w("tensor", "vector")
            for hh in range(4):
                h = hg * 8 + hh
                p.op("tensor", lambda e, s, hh=hh, h=h, par=par: e.transpose(
                    pT0[par:par + 64, hh * 128:(hh + 1) * 128], kds[:, h * 64:(h + 1) * 64],
                    ident[:], tile_position=(0, par)), tag=(hh == 3))
            p.w("vector", "tensor")
            p.op("vector", lambda e, s, hg=hg, par=par: e.tensor_copy(
                out=kdts[par:par + 64, hg * 1024:hg * 1024 + 512],
                in_=pT0[par:par + 64, 0:512]), tag=True)
            p.w("tensor", "vector")
            for hh in range(4):
                h = hg * 8 + 4 + hh
                p.op("tensor", lambda e, s, hh=hh, h=h, par=par: e.transpose(
                    pT0[par:par + 64, hh * 128:(hh + 1) * 128], kds[:, h * 64:(h + 1) * 64],
                    ident[:], tile_position=(0, par)), tag=(hh == 3))
            p.w("vector", "tensor")
            p.op("vector", lambda e, s, hg=hg, par=par: e.tensor_copy(
                out=kdts[par:par + 64, hg * 1024 + 512:hg * 1024 + 1024],
                in_=pT0[par:par + 64, 0:512]), tag=True)
        for h in range(NH):
            q4 = h % 4
            if q4 == 0:
                p.w("tensor", "vector")
            p.op("tensor", lambda e, s, h=h, q4=q4, tt=tt, par=par: e.matmul(
                pF[:, q4 * 128:(q4 + 1) * 128], kdts[par:par + 64, h * 128:(h + 1) * 128],
                vtok[par:par + 64, tt * H + h * 128: tt * H + (h + 1) * 128],
                start=(q4 == 0), stop=(q4 == 3)), tag=(q4 == 3))
            if q4 == 3:
                p.w("vector", "tensor")
                h0_ = h - 3
                mv = Mst[:, h0_ * 128:(h0_ + 4) * 128].rearrange("q (g a) -> q g a", g=4)
                pv = pF[:, 0:512].rearrange("q (g a) -> q g a", g=4)
                egb = EGtot[:, h0_:h0_ + 4].rearrange("q (g o) -> q g o", o=1).broadcast_to([128, 4, 128])
                p.op("vector", lambda e, s, mv=mv, pv=pv: e.tensor_tensor(
                    out=mv, in0=mv, in1=pv, op=ALU.add), tag=True)
                p.w("vector", "vector")
                p.op("vector", lambda e, s, mv=mv, egb=egb: e.tensor_tensor(
                    out=mv, in0=mv, in1=egb, op=ALU.mult), tag=True)

    p.barrier()
    p.op("vector", lambda e, s: e.tensor_copy(out=Mbf[:], in_=Mst[:]), tag=True)
    p.w("gpsimd", "vector")
    p.scat("gpsimd", exch[:], exi[:, 0:1], Mbf[:])
    p.wv("gpsimd", "gpsimd_dma", p.dma["gpsimd"])
    p.scat("gpsimd", exch[:], exf[:, 0:1], flagst[:])
    p.wv("gpsimd", "gpsimd_dma", p.dma["gpsimd"])
    p.spin_flags("gpsimd", [exch[256:257, 0:2].bitcast(I32),
                            exch[257:258, 0:2].bitcast(I32)], noncei[0:1, 0:1])
    hinit = Mbf
    p.gath("gpsimd", hinit[:], exch[:], exg[:, 0:1])
    p.w("vector", "gpsimd", dma=True)
    p.op("vector", lambda e, s: e.tensor_scalar(
        out=hinit[:], in0=hinit[:], scalar1=rolef[:, 0:1], scalar2=None,
        op0=ALU.mult), tag=True)
    for h in range(NH):
        p.w("tensor", "vector")
        for half in range(2):
            psx = (pC, pD_)[half]
            p.op("tensor", lambda e, s, h=h, half=half, psx=psx: e.matmul(
                psx[:, 0:512], hinit[:, h * 128:(h + 1) * 128],
                qbuf[:, h * T + half * 512: h * T + half * 512 + 512],
                start=True, stop=True), tag=(half == 1))
        p.w("vector", "tensor")
        for half in range(2):
            psx = (pC, pD_)[half]
            p.op("vector", lambda e, s, h=h, half=half, psx=psx: e.tensor_tensor(
                out=obuf[:, h * T + half * 512: h * T + half * 512 + 512],
                in0=obuf[:, h * T + half * 512: h * T + half * 512 + 512],
                in1=psx[:, 0:512], op=ALU.add), tag=(half == 1))

    if dbg:
        p.w("gpsimd", "vector")
        p.dmao("gpsimd", d_ob, obuf[:])
        p.dmao("gpsimd", d_q, qbuf[:])

    p.barrier()
    for j in range(NDC):
        p.w("scalar", "vector")
        if j >= 1:
            p.w("scalar", "tensor")
        p.op("scalar", lambda e, s, j=j: e.activation(
            out=sqs[:], in_=obuf[:, j * T:(j + 1) * T], func=AF.Square), tag=True)
        p.w("tensor", "scalar")
        for half in range(2):
            p.op("tensor", lambda e, s, j=j, half=half: e.matmul(
                (pA, pB)[half][:, 0:512], ones[:], sqs[:, half * 512:half * 512 + 512],
                start=(j == 0), stop=(j == NDC - 1)),
                tag=(j == NDC - 1 and half == 1))
    emit_ln_finish(p, None, (pA, pB), mb, rstdb, scr, rms_only=True, epsc=epsc)
    p.w("vector", "scalar")
    for j in range(NDC):
        p.op("vector", lambda e, s, j=j: e.scalar_tensor_tensor(
            out=obuf[:, j * T:(j + 1) * T], in0=obuf[:, j * T:(j + 1) * T],
            scalar=gwb[:, j:j + 1], in1=rstdb[:], op0=ALU.mult, op1=ALU.mult),
            tag=(j == NDC - 1))

    p.barrier()
    for do in range(NDC):
        wt = (wt0, wt1)[do % 2]
        if do >= 2:
            p.w("gpsimd", "tensor")
        p.dmao("gpsimd", wt[:], chunk(wo_p, do))
        b = (xs0, xs1)[do % 2]
        if do >= 2:
            p.w("sync", "scalar")
        p.dmao("sync", b[:], chunk(x_in, do))
        p.w("tensor", "gpsimd", dma=True)
        for half in range(2):
            psx = (pE, pF)[half]
            p.w("tensor", "vector")
            for dj in range(NDC):
                p.op("tensor", lambda e, s, psx=psx, wt=wt, dj=dj, half=half: e.matmul(
                    psx[:, 0:512], wt[:, dj * 128:(dj + 1) * 128],
                    obuf[:, dj * T + half * 512: dj * T + half * 512 + 512],
                    start=(dj == 0), stop=(dj == NDC - 1)), tag=(dj == NDC - 1))
        p.w("vector", "tensor")
        p.w("vector", "sync", dma=True)
        for half in range(2):
            psx = (pE, pF)[half]
            sl = slice(half * 512, (half + 1) * 512)
            p.op("vector", lambda e, s, psx=psx, b=b, sl=sl: e.tensor_tensor(
                out=b[:, sl], in0=b[:, sl], in1=psx[:, 0:512], op=ALU.add),
                tag=(half == 1))
        p.w("sync", "vector")
        p.dmao("sync", chunk(x1_out, do), b[:])
        p.w("scalar", "vector")
        if do >= 1:
            p.w("scalar", "tensor")
        p.op("scalar", lambda e, s, b=b: e.activation(out=sqs[:], in_=b[:], func=AF.Square))
        p.op("scalar", lambda e, s, b=b: e.activation(out=bfs[:], in_=b[:], func=AF.Copy), tag=True)
        p.w("tensor", "scalar")
        for half in range(2):
            p.op("tensor", lambda e, s, do=do, half=half: e.matmul(
                (pA, pB)[half][:, 0:512], ones[:], bfs[:, half * 512:half * 512 + 512],
                start=(do == 0), stop=(do == NDC - 1)))
            p.op("tensor", lambda e, s, do=do, half=half: e.matmul(
                (pC, pD_)[half][:, 0:512], ones[:], sqs[:, half * 512:half * 512 + 512],
                start=(do == 0), stop=(do == NDC - 1)),
                tag=(do == NDC - 1 and half == 1))

    emit_ln_finish(p, (pA, pB), (pC, pD_), mb, rstdb, scr, epsc=epsc)

    p.barrier()
    xln2 = gtok
    for j in range(NDC):
        b = (xs0, xs1)[j % 2]
        if j >= 2:
            p.w("sync", "vector")
        p.dmao("sync", b[:], chunk(x1_out, j))
        p.w("vector", "sync", dma=True)
        p.w("vector", "scalar")
        p.op("vector", lambda e, s, b=b: e.tensor_tensor(
            out=scr[:], in0=b[:], in1=mb[:], op=ALU.subtract))
        p.op("vector", lambda e, s, j=j, xln2=xln2: e.tensor_tensor(
            out=xln2[:, j * T:(j + 1) * T], in0=scr[:], in1=rstdb[:], op=ALU.mult), tag=True)
        p.w("gpsimd", "vector")
        p.dmao("gpsimd", chunk(xln2_out, j), xln2[:, j * T:(j + 1) * T])

    p.w("tensor", "vector")
    for tt in range(NTC):
        psx = (pE, pF)[tt % 2]
        if tt >= 2:
            p.w("tensor", "scalar")
        for dj in range(NDC):
            p.op("tensor", lambda e, s, psx=psx, tt=tt, dj=dj, xln2=xln2: e.matmul(
                psx[:, 0:16], xln2[:, dj * T + tt * 128: dj * T + (tt + 1) * 128],
                rwt[:, dj * 16:(dj + 1) * 16], start=(dj == 0), stop=False))
        p.op("tensor", lambda e, s, psx=psx: e.matmul(
            psx[:, 0:16], ones[0:1, :], rbt[:], start=False, stop=True), tag=True)
        p.w("vector", "tensor")
        p.w("vector", "gpsimd", dma=True)
        p.op("vector", lambda e, s, psx=psx: e.tensor_reduce(
            out=col0[:], in_=psx[:, 0:16], axis=AXX, op=ALU.max, negate=True), tag=True)
        p.w("scalar", "vector")
        p.w("scalar", "gpsimd", dma=True)
        p.op("scalar", lambda e, s, psx=psx: e.activation(
            out=prb[:], in_=psx[:, 0:16], func=AF.Exp, bias=col0[:, 0:1]), tag=True)
        p.w("vector", "scalar")
        p.op("vector", lambda e, s: e.tensor_reduce(
            out=col1[:], in_=prb[:], axis=AXX, op=ALU.add), tag=True)
        p.w("vector", "vector")
        p.op("vector", lambda e, s: e.reciprocal(out=col2[:], in_=col1[:]), tag=True)
        p.w("vector", "vector")
        p.op("vector", lambda e, s: e.tensor_scalar(
            out=prb[:], in0=prb[:], scalar1=col2[:, 0:1], scalar2=None, op0=ALU.mult),
            tag=True)
        p.w("gpsimd", "vector")
        p.dmao("gpsimd", p1_out[tt * 128:(tt + 1) * 128, :], prb[:])

    p.wv("sync", "sync_dma", p.dma["sync"])
    p.wv("sync", "gpsimd_dma", p.dma["gpsimd"])
    p.wv("gpsimd", "gpsimd_dma", p.dma["gpsimd"])

    p.build(nc, ctx)
    ctx.close()
    return nc






def emit_bisect_iter(p, it, pG_, sel8f, Tcol, Tb, pbis, junk, cnt, cntb, sel8t):
    step = 0.5 / (2.0 ** (it + 1))
    p.w("tensor", "vector")
    p.op("tensor", lambda e, s: e.matmul(
        pG_[:, 0:1], sel8f[:], Tcol[:], start=True, stop=True), tag=True)
    p.w("vector", "tensor")
    p.op("vector", lambda e, s: e.tensor_copy(out=Tb[:], in_=pG_[:, 0:1]), tag=True)
    p.w("vector", "vector")
    p.op("vector", lambda e, s: e.tensor_scalar(
        out=junk[:], in0=pbis[:], scalar1=Tb[:, 0:1], scalar2=None, op0=ALU.is_gt),
        tag=True)
    p.w("vector", "vector")
    p.op("vector", lambda e, s: e.tensor_reduce(
        out=cnt[:], in_=junk[:], axis=AXX, op=ALU.add), tag=True)
    p.w("vector", "vector")
    p.op("vector", lambda e, s: e.tensor_copy(out=cntb[:], in_=cnt[:]), tag=True)
    p.w("tensor", "vector")
    p.op("tensor", lambda e, s: e.matmul(
        pG_[0:16, 4:5], sel8t[:], cntb[:], start=True, stop=True), tag=True)
    p.w("vector", "tensor")
    p.op("vector", lambda e, s: e.tensor_scalar(
        out=cnt[0:16, 0:1], in0=pG_[0:16, 4:5], scalar1=512.0, scalar2=None,
        op0=ALU.is_gt), tag=True)
    p.w("vector", "vector")
    p.op("vector", lambda e, s, step=step: e.scalar_tensor_tensor(
        out=Tcol[:], in0=cnt[0:16, 0:1], scalar=2.0 * step, in1=Tcol[:],
        op0=ALU.mult, op1=ALU.add), tag=True)
    p.w("vector", "vector")
    p.op("vector", lambda e, s, step=step: e.tensor_scalar(
        out=Tcol[:], in0=Tcol[:], scalar1=step, scalar2=None, op0=ALU.subtract), tag=True)
    p.w("vector", "vector")



def build_L2(dbg=False):
    nc = bass.Bass("TRN2", target_bir_lowering=False)
    x1_in = nc.dram_tensor("x1i", [H, T], F32, kind="ExternalInput").ap()
    xln2_in = nc.dram_tensor("xln2i", [H, T], BF16, kind="ExternalInput").ap()
    pbis_in = nc.dram_tensor("pbis", [128, 1024], F32, kind="ExternalInput").ap()
    pmye_in = nc.dram_tensor("pmye", [16, T], F32, kind="ExternalInput").ap()
    we_p = nc.dram_tensor("we_p", [NDC * 128, H], F32, kind="ExternalInput").ap()
    web_in = nc.dram_tensor("web", [128, NDC], F32, kind="ExternalInput").ap()
    wqt_p = nc.dram_tensor("wqt_p", [NDC * 128, H], F32, kind="ExternalInput").ap()
    wkt_p = nc.dram_tensor("wkt_p", [4 * 128, H], F32, kind="ExternalInput").ap()
    wvt_in = nc.dram_tensor("wvt", [128, NDC * 512], F32, kind="ExternalInput").ap()
    wot_p = nc.dram_tensor("wot_p", [NDC * 128, H], F32, kind="ExternalInput").ap()
    rw_in = nc.dram_tensor("rw2", [128, NDC * 16], F32, kind="ExternalInput").ap()
    rbias = nc.dram_tensor("rb2", [1, 16], F32, kind="ExternalInput").ap()
    cosin = nc.dram_tensor("cosin", [128, T], F32, kind="ExternalInput").ap()
    sinin = nc.dram_tensor("sinin", [128, T], F32, kind="ExternalInput").ap()
    swapin = nc.dram_tensor("swapin", [128, 128], BF16, kind="ExternalInput").ap()
    qgidx = nc.dram_tensor("qgidx", [1, T], F32, kind="ExternalInput").ap()
    kgidx = nc.dram_tensor("kgidx", [128, 16], F32, kind="ExternalInput").ap()
    identc = nc.dram_tensor("ident2", [128, 128], BF16, kind="ExternalInput").ap()
    onesc = nc.dram_tensor("ones2", [128, 128], BF16, kind="ExternalInput").ap()
    onesfc = nc.dram_tensor("onesf2", [128, 128], F32, kind="ExternalInput").ap()
    sel8fc = nc.dram_tensor("sel8f", [16, 128], F32, kind="ExternalInput").ap()
    sel8tc = nc.dram_tensor("sel8t", [128, 16], BF16, kind="ExternalInput").ap()
    exidx = nc.dram_tensor("exidx2", [128, 8], U32, kind="ExternalInput").ap()
    exgidx = nc.dram_tensor("exgidx2", [128, 16], U32, kind="ExternalInput").ap()
    exfidx = nc.dram_tensor("exfidx2", [2, 1], U32, kind="ExternalInput").ap()
    noncein = nc.dram_tensor("noncef2", [2, 16], BF16, kind="ExternalInput").ap()
    noncei_in = nc.dram_tensor("noncei2", [1, 1], I32, kind="ExternalInput").ap()

    x3_out = nc.dram_tensor("x3", [H, T], F32, kind="ExternalOutput").ap()
    xln4_out = nc.dram_tensor("xln4", [H, T], BF16, kind="ExternalOutput").ap()
    p2_out = nc.dram_tensor("p2", [T, 16], F32, kind="ExternalOutput").ap()

    x2d = nc.dram_tensor("x2d", [H, T], F32, kind="Internal").ap()
    if dbg:
        d_sbc = nc.dram_tensor("d_sbc", [128, T], F32, kind="ExternalOutput").ap()
        d_x2 = nc.dram_tensor("d_x2", [H, T], F32, kind="ExternalOutput").ap()
        d_ob = nc.dram_tensor("d_ob", [128, NDC * T], BF16, kind="ExternalOutput").ap()
        d_kb = nc.dram_tensor("d_kb", [128, 4 * 2048], BF16, kind="ExternalOutput").ap()
        d_vb = nc.dram_tensor("d_vb", [128, 16 * 512], BF16, kind="ExternalOutput").ap()
    exch = nc.dram_tensor("exch_l2", [2050, 2048], BF16, kind="Internal",
                          addr_space="Shared").ap()

    p = Prog()
    ctx = ExitStack()

    def sb(name, shape, dt):
        return ctx.enter_context(nc.sbuf_tensor("sb_" + name, shape, dt))

    def ps_(name, shape, dt):
        return ctx.enter_context(nc.psum_tensor("ps_" + name, shape, dt))

    xbig = sb("xbig", [128, NDC * T], BF16)
    ebuf = sb("ebuf", [128, NDC * T], BF16)
    kbuf = sb("kbuf", [128, 4 * 2048], BF16)
    vbuf = sb("vbuf", [128, 16 * 512], BF16)
    qrope = sb("qrope", [128, 1024], BF16)
    xs0 = sb("xs0", [128, T], F32)
    xs1 = sb("xs1", [128, T], F32)
    sqs = sb("sqs", [128, T], BF16)
    bfs = sb("bfs", [128, T], BF16)
    mb = sb("mb", [128, T], F32)
    rstdb = sb("rstdb", [128, T], F32)
    scr = sb("scr", [128, T], F32)
    sbc = sb("sbc", [128, T], F32)
    wt0 = sb("wt0", [128, H], BF16)
    wt1 = sb("wt1", [128, H], BF16)
    wvtb = sb("wvtb", [128, NDC * 512], BF16)
    webb = sb("webb", [128, NDC], F32)
    pbis = sb("pbis", [128, 1024], F32)
    pmye = sb("pmye", [16, T], F32)
    pmyeb = sb("pmyeb", [16, T], BF16)
    Tcol = sb("Tcol", [16, 1], F32)
    Tb = sb("Tb", [128, 1], F32)
    stp = sb("stp", [16, 1], F32)
    cnt = sb("cnt", [128, 1], F32)
    cntb = sb("cntb", [128, 1], BF16)
    cnte = sb("cnte", [16, 1], F32)
    sgn = sb("sgn", [16, 1], F32)
    srow = sb("srow", [1, T], F32)
    cfull = sb("cfull", [128, T], F32)
    sfull = sb("sfull", [128, T], F32)
    swapc = sb("swapc", [128, 128], BF16)
    qgx = sb("qgx", [1, T], F32)
    kgx = sb("kgx", [128, 16], F32)
    qgb = sb("qgb", [128, 512], F32)
    masks = sb("masks", [128, 16 * 512], BF16)
    pts2 = sb("pts2", [128, 1024], BF16)
    recro = sb("recro", [1, 512], F32)
    recb = sb("recb", [128, 512], F32)
    ident = sb("ident", [128, 128], BF16)
    ones = sb("ones", [128, 128], BF16)
    onesf = sb("onesf", [128, 128], F32)
    sel8f = sb("sel8f", [16, 128], F32)
    sel8t = sb("sel8t", [128, 16], BF16)
    junk = sb("junk", [128, 1024], BF16)
    exi = sb("exi", [128, 8], U32)
    exg = sb("exg", [128, 16], U32)
    exf = sb("exf", [2, 1], U32)
    flagst = sb("flagst", [2, 16], BF16)
    noncei = sb("noncei", [1, 1], I32)
    epsc = sb("epsc", [128, 1], F32)
    col0 = sb("col0", [128, 1], F32)
    col1 = sb("col1", [128, 1], F32)
    col2 = sb("col2", [128, 1], F32)
    prb = sb("prb", [128, 16], F32)
    rwt = sb("rwt", [128, NDC * 16], BF16)
    rbt = sb("rbt", [1, 16], BF16)

    pA = ps_("pA", [128, 512], F32)
    pB = ps_("pB", [128, 512], F32)
    pC = ps_("pC", [128, 512], F32)
    pD_ = ps_("pD", [128, 512], F32)
    pE = ps_("pE", [128, 512], F32)
    pF = ps_("pF", [128, 512], F32)
    pG_ = ps_("pG", [128, 512], F32)
    pH_ = ps_("pH", [128, 512], F32)

    for dst, srcd in [(pbis[:], pbis_in), (pmye[:], pmye_in), (cfull[:], cosin),
                      (sfull[:], sinin), (swapc[:], swapin), (qgx[:], qgidx), (kgx[:], kgidx),
                      (exf[:], exfidx), (flagst[:], noncein),
                      (noncei[:], noncei_in), (webb[:], web_in),
                      (ident[:], identc), (ones[:], onesc), (onesf[:], onesfc),
                      (sel8f[:], sel8fc), (sel8t[:], sel8tc)]:
        p.dmao("sync", dst, srcd)
    p.dmao("sync", exi[:], exidx)
    p.dmao("sync", exg[:], exgidx)
    p.dmao("gpsimd", rwt[:], rw_in)
    p.dmao("gpsimd", rbt[:], rbias)
    p.dmao("gpsimd", wvtb[:], wvt_in)
    for j in range(NDC):
        p.dmao("sync", xbig[:, j * T:(j + 1) * T], chunk(xln2_in, j))
    p.op("vector", lambda e, s: e.memset(epsc[:], EPS))
    p.op("vector", lambda e, s: e.memset(Tcol[:], 0.5))
    p.op("vector", lambda e, s: e.memset(stp[:], 0.25))
    p.w("vector", "sync", dma=True)
    p.op("vector", lambda e, s: e.tensor_copy(out=pmyeb[:], in_=pmye[:]), tag=True)

    p.w("tensor", "sync", dma=True)
    p.w("vector", "sync", dma=True)
    for do in range(NDC):
        wt = (wt0, wt1)[do % 2]
        if do >= 2:
            p.w("gpsimd", "tensor")
        p.dmao("gpsimd", wt[:], chunk(we_p, do))
        p.w("tensor", "gpsimd", dma=True)
        for half in range(2):
            psx = (pA, pB)[half]
            p.w("tensor", "scalar")
            for dj in range(NDC):
                p.op("tensor", lambda e, s, psx=psx, wt=wt, dj=dj, half=half: e.matmul(
                    psx[:, 0:512], wt[:, dj * 128:(dj + 1) * 128],
                    xbig[:, dj * T + half * 512: dj * T + half * 512 + 512],
                    start=(dj == 0), stop=(dj == NDC - 1)), tag=(dj == NDC - 1))
            p.w("scalar", "tensor")
            p.op("scalar", lambda e, s, psx=psx, do=do, half=half: e.activation(
                out=ebuf[:, do * T + half * 512: do * T + half * 512 + 512],
                in_=psx[:, 0:512], func=AF.Copy), tag=True)
            it = do * 2 + half
            if it < NBIS:
                emit_bisect_iter(p, it, pG_, sel8f, Tcol, Tb, pbis, junk, cnt, cntb, sel8t)


    p.op("vector", lambda e, s: e.tensor_scalar(
        out=junk[0:16, 0:1024], in0=pmye[:], scalar1=Tcol[0:16, 0:1], scalar2=None,
        op0=ALU.is_gt), tag=True)
    p.w("vector", "vector")
    p.op("vector", lambda e, s: e.tensor_tensor(
        out=junk[0:16, 0:1024], in0=junk[0:16, 0:1024], in1=pmyeb[:], op=ALU.mult), tag=True)
    p.w("tensor", "vector")
    for half in range(2):
        p.op("tensor", lambda e, s, half=half: e.matmul(
            pG_[0:1, 0:512], ones[0:16, 0:1], junk[0:16, half * 512:half * 512 + 512],
            start=True, stop=True), tag=True)
        p.w("vector", "tensor")
        p.op("vector", lambda e, s, half=half: e.tensor_copy(
            out=srow[0:1, half * 512:half * 512 + 512], in_=pG_[0:1, 0:512]), tag=True)
        p.w("tensor", "vector")
    for half in range(2):
        p.op("tensor", lambda e, s, half=half: e.matmul(
            pH_[:, 0:512], onesf[0:1, :], srow[0:1, half * 512:half * 512 + 512],
            start=True, stop=True), tag=True)
        p.w("vector", "tensor")
        p.op("vector", lambda e, s, half=half: e.tensor_copy(
            out=sbc[:, half * 512:half * 512 + 512], in_=pH_[:, 0:512]), tag=True)
        p.w("tensor", "vector")
    p.barrier()

    for do in range(NDC):
        b = (xs0, xs1)[do % 2]
        if do >= 2:
            p.w("sync", "scalar")
        p.dmao("sync", b[:], chunk(x1_in, do))
        p.w("vector", "sync", dma=True)
        p.op("vector", lambda e, s, do=do: e.scalar_tensor_tensor(
            out=scr[:], in0=ebuf[:, do * T:(do + 1) * T], scalar=webb[:, do:do + 1],
            in1=sbc[:], op0=ALU.add, op1=ALU.mult), tag=True)
        p.w("vector", "vector")
        p.op("vector", lambda e, s, b=b: e.tensor_tensor(
            out=b[:], in0=b[:], in1=scr[:], op=ALU.add), tag=True)
        p.w("sync", "vector")
        p.dmao("sync", chunk(x2d, do), b[:])
        p.w("scalar", "vector")
        if do >= 1:
            p.w("scalar", "tensor")
        p.op("scalar", lambda e, s, b=b: e.activation(out=sqs[:], in_=b[:], func=AF.Square))
        p.op("scalar", lambda e, s, b=b: e.activation(out=bfs[:], in_=b[:], func=AF.Copy), tag=True)
        p.w("tensor", "scalar")
        for half in range(2):
            p.op("tensor", lambda e, s, do=do, half=half: e.matmul(
                (pC, pD_)[half][:, 0:512], ones[:], bfs[:, half * 512:half * 512 + 512],
                start=(do == 0), stop=(do == NDC - 1)))
            p.op("tensor", lambda e, s, do=do, half=half: e.matmul(
                (pE, pF)[half][:, 0:512], ones[:], sqs[:, half * 512:half * 512 + 512],
                start=(do == 0), stop=(do == NDC - 1)),
                tag=(do == NDC - 1 and half == 1))
    emit_ln_finish(p, (pC, pD_), (pE, pF), mb, rstdb, scr, epsc=epsc)
    if dbg:
        p.barrier()
        p.dmao("gpsimd", d_sbc, sbc[:])
        for j in range(NDC):
            p.dmao("gpsimd", chunk(d_x2, j), chunk(x2d, j))
        p.barrier()

    p.barrier()
    for j in range(NDC):
        b = (xs0, xs1)[j % 2]
        if j >= 2:
            p.w("sync", "vector")
        p.dmao("sync", b[:], chunk(x2d, j))
        p.w("vector", "sync", dma=True)
        p.w("vector", "scalar")
        p.op("vector", lambda e, s, b=b: e.tensor_tensor(
            out=scr[:], in0=b[:], in1=mb[:], op=ALU.subtract))
        p.op("vector", lambda e, s, j=j: e.tensor_tensor(
            out=xbig[:, j * T:(j + 1) * T], in0=scr[:], in1=rstdb[:], op=ALU.mult), tag=True)
    p.barrier()

    myoff_col = None
    for g in range(NKV):
        wt = (wt0, wt1)[g % 2]
        if g >= 2:
            p.w("gpsimd", "tensor")
        p.dmao("gpsimd", wt[:], chunk(wkt_p, g))
        p.w("tensor", "gpsimd", dma=True)
        for half in range(2):
            psx = (pA, pB)[half]
            p.w("tensor", "vector")
            for dj in range(NDC):
                p.op("tensor", lambda e, s, psx=psx, wt=wt, dj=dj, half=half: e.matmul(
                    psx[:, 0:512], wt[:, dj * 128:(dj + 1) * 128],
                    xbig[:, dj * T + half * 512: dj * T + half * 512 + 512],
                    start=(dj == 0), stop=(dj == NDC - 1)), tag=(dj == NDC - 1))
        for half in range(2):
            psx = (pA, pB)[half]
            sl = slice(half * 512, (half + 1) * 512)
            p.w("scalar", "tensor")
            p.w("scalar", "vector")
            p.op("scalar", lambda e, s, psx=psx: e.activation(
                out=junk[:, 0:512], in_=psx[:, 0:512], func=AF.Copy), tag=True)
            p.w("tensor", "scalar")
            p.w("tensor", "vector")
            p.op("tensor", lambda e, s: e.matmul(
                pC[:, 0:512], swapc[:], junk[:, 0:512], start=True, stop=True), tag=True)
            p.w("vector", "tensor")
            p.w("vector", "scalar")
            p.op("vector", lambda e, s, sl=sl: e.tensor_tensor(
                out=scr[:, 0:512], in0=junk[:, 0:512], in1=cfull[:, sl], op=ALU.mult), tag=True)
            p.w("vector", "vector")
            p.op("vector", lambda e, s, sl=sl: e.tensor_tensor(
                out=scr[:, 512:1024], in0=pC[:, 0:512], in1=sfull[:, sl], op=ALU.mult), tag=True)
            p.w("vector", "vector")
            p.op("vector", lambda e, s, g=g, sl=sl: e.tensor_tensor(
                out=kbuf[:, g * 2048 + sl.start: g * 2048 + sl.stop],
                in0=scr[:, 0:512], in1=scr[:, 512:1024], op=ALU.add), tag=True)
    p.w("tensor", "gpsimd", dma=True)
    for tt in range(NTC):
        psx = (pG_, pH_)[tt % 2]
        if tt >= 2:
            p.w("tensor", "scalar")
        for dj in range(NDC):
            p.op("tensor", lambda e, s, psx=psx, tt=tt, dj=dj: e.matmul(
                psx[:, 0:512], xbig[:, dj * T + tt * 128: dj * T + (tt + 1) * 128],
                wvtb[:, dj * 512:(dj + 1) * 512], start=(dj == 0), stop=(dj == NDC - 1)),
                tag=(dj == NDC - 1))
        p.w("scalar", "tensor")
        p.op("scalar", lambda e, s, psx=psx, tt=tt: e.activation(
            out=vbuf[:, tt * 512:(tt + 1) * 512], in_=psx[:, 0:512], func=AF.Copy), tag=True)
    p.barrier()

    for g in range(NKV):
        p.scat("gpsimd", exch[:], exi[:, g:g + 1],
               kbuf[:, g * 2048: g * 2048 + 1024])
    for vb in range(4):
        p.scat("gpsimd", exch[:], exi[:, 4 + vb:5 + vb],
               vbuf[:, vb * 1024: (vb + 1) * 1024])
    p.wv("gpsimd", "gpsimd_dma", p.dma["gpsimd"])
    p.scat("gpsimd", exch[:], exf[:, 0:1], flagst[:])
    p.wv("gpsimd", "gpsimd_dma", p.dma["gpsimd"])
    p.spin_flags("gpsimd", [exch[2048:2049, 0:2].bitcast(I32),
                            exch[2049:2050, 0:2].bitcast(I32)], noncei[0:1, 0:1])
    for g in range(NKV):
        for hf in range(2):
            p.gath("gpsimd", kbuf[:, g * 2048 + hf * 1024: g * 2048 + (hf + 1) * 1024],
                   exch[:], exg[:, g * 2 + hf: g * 2 + hf + 1])
    for vb in range(8):
        p.gath("gpsimd", vbuf[:, vb * 1024:(vb + 1) * 1024], exch[:],
               exg[:, 8 + vb: 9 + vb])
    p.barrier()

    NEXT = [12, 16]
    obank_done = {0: (0, 0), 1: (0, 0)}
    obank_rd = 0
    den_done = {0: 0, 1: 0}
    sco_done = {0: 0, 1: 0}
    panel_mark = [0, 0]
    for grp in range(2):
        p.w("tensor", "vector")
        p.op("tensor", lambda e, s, grp=grp: e.matmul(
            pH_[:, 0:512], onesf[0:1, :], qgx[0:1, grp * 512:(grp + 1) * 512],
            start=True, stop=True), tag=True)
        p.w("vector", "tensor")
        p.op("vector", lambda e, s: e.tensor_copy(out=qgb[:], in_=pH_[:, 0:512]), tag=True)
        p.w("vector", "vector")
        for j in range(NEXT[grp]):
            p.op("vector", lambda e, s, j=j: e.tensor_scalar(
                out=masks[:, j * 512:(j + 1) * 512], in0=qgb[:], scalar1=kgx[:, j:j + 1],
                scalar2=None, op0=ALU.is_ge), tag=(j == NEXT[grp] - 1))
        HORD = list(range(NH))
        p.dmao("gpsimd", wt0[:], chunk(wqt_p, HORD[0]))
        for hi, h in enumerate(HORD):
            g = h // GRP
            wt = (wt0, wt1)[hi % 2]
            psq = (pA, pB)[hi % 2]
            pden = (pF, pG_)[hi % 2]
            if hi + 1 < NH:
                p.w_at("gpsimd", "tensor", panel_mark[(hi + 1) % 2])
                p.dmao("gpsimd", wt1[:] if (hi + 1) % 2 else wt0[:], chunk(wqt_p, HORD[hi + 1]))
            p.w("tensor", "gpsimd", dma=True)
            sm, vm = obank_done[hi % 2]
            p.w_at("tensor", "scalar", sm)
            p.w_at("tensor", "vector", vm)
            for dj in range(NDC):
                p.op("tensor", lambda e, s, psq=psq, wt=wt, dj=dj, grp=grp: e.matmul(
                    psq[:, 0:512], wt[:, dj * 128:(dj + 1) * 128],
                    xbig[:, dj * T + grp * 512: dj * T + grp * 512 + 512],
                    start=(dj == 0), stop=(dj == NDC - 1)), tag=(dj == NDC - 1))
            panel_mark[hi % 2] = p.mark("tensor")
            qs_ = qrope[:, (hi % 2) * 512:(hi % 2) * 512 + 512]
            sl = slice(grp * 512, (grp + 1) * 512)
            p.w("scalar", "tensor")
            p.w("scalar", "vector")
            p.op("scalar", lambda e, s, psq=psq: e.activation(
                out=junk[:, 0:512], in_=psq[:, 0:512], func=AF.Copy), tag=True)
            p.w("tensor", "scalar")
            p.w("tensor", "vector")
            p.op("tensor", lambda e, s: e.matmul(
                pH_[:, 0:512], swapc[:], junk[:, 0:512], start=True, stop=True), tag=True)
            p.w("vector", "tensor")
            p.w("vector", "scalar")
            p.op("vector", lambda e, s, sl=sl: e.tensor_tensor(
                out=scr[:, 0:512], in0=junk[:, 0:512], in1=cfull[:, sl], op=ALU.mult), tag=True)
            p.w("vector", "vector")
            p.op("vector", lambda e, s, sl=sl: e.tensor_tensor(
                out=scr[:, 512:1024], in0=pH_[:, 0:512], in1=sfull[:, sl], op=ALU.mult), tag=True)
            p.w("vector", "vector")
            p.op("vector", lambda e, s, qs_=qs_: e.tensor_tensor(
                out=qs_[:], in0=scr[:, 0:512], in1=scr[:, 512:1024], op=ALU.add), tag=True)
            rope_vm = p.mark("vector")
            nj = NEXT[grp]
            mask_vm = {}
            for j in range(nj + 1):
                if j < nj:
                    psS = (pC, pD_)[j % 2]
                    slot = (j % 2) * 512
                    p.w_at("tensor", "scalar", sco_done[j % 2])
                    if j == 0:
                        p.w_at("tensor", "vector", rope_vm)
                    p.op("tensor", lambda e, s, psS=psS, g=g, j=j, qs_=qs_: e.matmul(
                        psS[:, 0:512], kbuf[:, g * 2048 + j * 128: g * 2048 + (j + 1) * 128],
                        qs_[:, 0:512], start=True, stop=True, skip_group_check=True), tag=True)
                    p.w("scalar", "tensor")
                    p.op("scalar", lambda e, s, psS=psS, slot=slot: e.activation(
                        out=pts2[:, slot:slot + 512], in_=psS[:, 0:512], func=AF.Exp,
                        scale=1.0 / 11.313708498984761), tag=True)
                    sco_done[j % 2] = p.mark("scalar")
                    p.w("vector", "scalar")
                    p.op("vector", lambda e, s, j=j, slot=slot: e.tensor_tensor(
                        out=pts2[:, slot:slot + 512], in0=pts2[:, slot:slot + 512],
                        in1=masks[:, j * 512:(j + 1) * 512], op=ALU.mult), tag=True)
                    mask_vm[j] = p.mark("vector")
                if j >= 1:
                    jj = j - 1
                    slot2 = (jj % 2) * 512
                    p.w_at("tensor", "vector", mask_vm[jj])
                    if jj == 0:
                        p.w_at("tensor", "vector", den_done[hi % 2])
                    if jj == 0:
                        p.w_at("tensor", "vector", obank_rd)
                    p.op("tensor", lambda e, s, jj=jj, g=g, grp=grp, slot2=slot2: e.matmul(
                        pE[:, 0:512], vbuf[:, jj * 512 + g * 128: jj * 512 + (g + 1) * 128],
                        pts2[:, slot2:slot2 + 512], start=(jj == 0), stop=(jj == nj - 1),
                        skip_group_check=True))
                    p.op("tensor", lambda e, s, jj=jj, grp=grp, slot2=slot2, pden=pden: e.matmul(
                        pden[0:1, 0:512], ones[:, 0:1], pts2[:, slot2:slot2 + 512],
                        start=(jj == 0), stop=(jj == nj - 1), skip_group_check=True),
                        tag=True)
            p.w("vector", "tensor")
            p.op("vector", lambda e, s, pden=pden: e.reciprocal(out=recro[:], in_=pden[0:1, 0:512]), tag=True)
            den_done[hi % 2] = p.mark("vector")
            p.w("vector", "vector")
            p.w("tensor", "vector")
            p.op("tensor", lambda e, s: e.matmul(
                pH_[:, 0:512], onesf[0:1, :], recro[:], start=True, stop=True), tag=True)
            p.w("vector", "tensor")
            p.op("vector", lambda e, s: e.tensor_copy(out=recb[:], in_=pH_[:, 0:512]), tag=True)
            p.w("vector", "vector")
            p.op("vector", lambda e, s, h=h, grp=grp: e.tensor_tensor(
                out=ebuf[:, h * T + grp * 512: h * T + grp * 512 + 512],
                in0=pE[:, 0:512], in1=recb[:], op=ALU.mult), tag=True)
            obank_rd = p.mark("vector")
            obank_done[hi % 2] = (p.mark("scalar"), p.mark("vector"))
    p.barrier()

    p.barrier()
    if dbg:
        p.dmao("gpsimd", d_ob, ebuf[:])
        p.dmao("gpsimd", d_kb, kbuf[:])
        p.dmao("gpsimd", d_vb, vbuf[:])
        p.barrier()

    for do in range(NDC):
        wt = (wt0, wt1)[do % 2]
        if do >= 2:
            p.w("gpsimd", "tensor")
        p.dmao("gpsimd", wt[:], chunk(wot_p, do))
        b = (xs0, xs1)[do % 2]
        if do >= 2:
            p.w("sync", "scalar")
        p.dmao("sync", b[:], chunk(x2d, do))
        p.w("tensor", "gpsimd", dma=True)
        for half in range(2):
            psx = (pA, pB)[half]
            p.w("tensor", "vector")
            for dj in range(NDC):
                p.op("tensor", lambda e, s, psx=psx, wt=wt, dj=dj, half=half: e.matmul(
                    psx[:, 0:512], wt[:, dj * 128:(dj + 1) * 128],
                    ebuf[:, dj * T + half * 512: dj * T + half * 512 + 512],
                    start=(dj == 0), stop=(dj == NDC - 1)), tag=(dj == NDC - 1))
        p.w("vector", "tensor")
        p.w("vector", "sync", dma=True)
        for half in range(2):
            psx = (pA, pB)[half]
            sl = slice(half * 512, (half + 1) * 512)
            p.op("vector", lambda e, s, psx=psx, b=b, sl=sl: e.tensor_tensor(
                out=b[:, sl], in0=b[:, sl], in1=psx[:, 0:512], op=ALU.add), tag=(half == 1))
        p.w("sync", "vector")
        p.dmao("sync", chunk(x3_out, do), b[:])
        p.w("scalar", "vector")
        if do >= 1:
            p.w("scalar", "tensor")
        p.op("scalar", lambda e, s, b=b: e.activation(out=sqs[:], in_=b[:], func=AF.Square))
        p.op("scalar", lambda e, s, b=b: e.activation(out=bfs[:], in_=b[:], func=AF.Copy), tag=True)
        p.w("tensor", "scalar")
        for half in range(2):
            p.op("tensor", lambda e, s, do=do, half=half: e.matmul(
                (pC, pD_)[half][:, 0:512], ones[:], bfs[:, half * 512:half * 512 + 512],
                start=(do == 0), stop=(do == NDC - 1)))
            p.op("tensor", lambda e, s, do=do, half=half: e.matmul(
                (pE, pF)[half][:, 0:512], ones[:], sqs[:, half * 512:half * 512 + 512],
                start=(do == 0), stop=(do == NDC - 1)),
                tag=(do == NDC - 1 and half == 1))
    emit_ln_finish(p, (pC, pD_), (pE, pF), mb, rstdb, scr, epsc=epsc)
    p.barrier()

    xln4 = ebuf
    for j in range(NDC):
        b = (xs0, xs1)[j % 2]
        if j >= 2:
            p.w("sync", "vector")
        p.dmao("sync", b[:], chunk(x3_out, j))
        p.w("vector", "sync", dma=True)
        p.w("vector", "scalar")
        p.op("vector", lambda e, s, b=b: e.tensor_tensor(
            out=scr[:], in0=b[:], in1=mb[:], op=ALU.subtract))
        p.op("vector", lambda e, s, j=j, xln4=xln4: e.tensor_tensor(
            out=xln4[:, j * T:(j + 1) * T], in0=scr[:], in1=rstdb[:], op=ALU.mult), tag=True)
        p.w("gpsimd", "vector")
        p.dmao("gpsimd", chunk(xln4_out, j), xln4[:, j * T:(j + 1) * T])

    p.w("tensor", "vector")
    for tt in range(NTC):
        psx = (pG_, pH_)[tt % 2]
        if tt >= 2:
            p.w("tensor", "scalar")
        for dj in range(NDC):
            p.op("tensor", lambda e, s, psx=psx, tt=tt, dj=dj, xln4=xln4: e.matmul(
                psx[:, 0:16], xln4[:, dj * T + tt * 128: dj * T + (tt + 1) * 128],
                rwt[:, dj * 16:(dj + 1) * 16], start=(dj == 0), stop=False))
        p.op("tensor", lambda e, s, psx=psx: e.matmul(
            psx[:, 0:16], ones[0:1, :], rbt[:], start=False, stop=True), tag=True)
        p.w("vector", "tensor")
        p.w("vector", "gpsimd", dma=True)
        p.op("vector", lambda e, s, psx=psx: e.tensor_reduce(
            out=col0[:], in_=psx[:, 0:16], axis=AXX, op=ALU.max, negate=True), tag=True)
        p.w("scalar", "vector")
        p.w("scalar", "gpsimd", dma=True)
        p.op("scalar", lambda e, s, psx=psx: e.activation(
            out=prb[:], in_=psx[:, 0:16], func=AF.Exp, bias=col0[:, 0:1]), tag=True)
        p.w("vector", "scalar")
        p.op("vector", lambda e, s: e.tensor_reduce(
            out=col1[:], in_=prb[:], axis=AXX, op=ALU.add), tag=True)
        p.w("vector", "vector")
        p.op("vector", lambda e, s: e.reciprocal(out=col2[:], in_=col1[:]), tag=True)
        p.w("vector", "vector")
        p.op("vector", lambda e, s: e.tensor_scalar(
            out=prb[:], in0=prb[:], scalar1=col2[:, 0:1], scalar2=None, op0=ALU.mult), tag=True)
        p.w("gpsimd", "vector")
        p.dmao("gpsimd", p2_out[tt * 128:(tt + 1) * 128, :], prb[:])

    p.wv("sync", "sync_dma", p.dma["sync"])
    p.wv("sync", "gpsimd_dma", p.dma["gpsimd"])
    p.wv("gpsimd", "gpsimd_dma", p.dma["gpsimd"])
    p.build(nc, ctx)
    ctx.close()
    return nc




def build_L3():
    nc = bass.Bass("TRN2", target_bir_lowering=False)
    x3_in = nc.dram_tensor("x3i", [H, T], F32, kind="ExternalInput").ap()
    xln4_in = nc.dram_tensor("xln4i", [H, T], BF16, kind="ExternalInput").ap()
    pbis_in = nc.dram_tensor("pbis3", [128, 1024], F32, kind="ExternalInput").ap()
    pmye_in = nc.dram_tensor("pmye3", [16, T], F32, kind="ExternalInput").ap()
    we_p = nc.dram_tensor("we_p3", [NDC * 128, H], F32, kind="ExternalInput").ap()
    web_in = nc.dram_tensor("web3", [128, NDC], F32, kind="ExternalInput").ap()
    onesc = nc.dram_tensor("ones3", [128, 128], BF16, kind="ExternalInput").ap()
    onesfc = nc.dram_tensor("onesf3", [128, 128], F32, kind="ExternalInput").ap()
    sel8fc = nc.dram_tensor("sel8f3", [16, 128], F32, kind="ExternalInput").ap()
    sel8tc = nc.dram_tensor("sel8t3", [128, 16], BF16, kind="ExternalInput").ap()
    out_o = nc.dram_tensor("outx", [H, T], F32, kind="ExternalOutput").ap()

    p = Prog()
    ctx = ExitStack()

    def sb(name, shape, dt):
        return ctx.enter_context(nc.sbuf_tensor("sb_" + name, shape, dt))

    def ps_(name, shape, dt):
        return ctx.enter_context(nc.psum_tensor("ps_" + name, shape, dt))

    xln4 = sb("xln4", [128, NDC * T], BF16)
    ebuf = sb("ebuf", [128, NDC * T], BF16)
    xs0 = sb("xs0", [128, T], F32)
    xs1 = sb("xs1", [128, T], F32)
    scr = sb("scr", [128, T], F32)
    sbc = sb("sbc", [128, T], F32)
    wt0 = sb("wt0", [128, H], BF16)
    wt1 = sb("wt1", [128, H], BF16)
    webb = sb("webb", [128, NDC], F32)
    pbis = sb("pbis", [128, 1024], F32)
    pmye = sb("pmye", [16, T], F32)
    pmyeb = sb("pmyeb", [16, T], BF16)
    Tcol = sb("Tcol", [16, 1], F32)
    Tb = sb("Tb", [128, 1], F32)
    stp = sb("stp", [16, 1], F32)
    cnt = sb("cnt", [128, 1], F32)
    cntb = sb("cntb", [128, 1], BF16)
    cnte = sb("cnte", [16, 1], F32)
    sgn = sb("sgn", [16, 1], F32)
    srow = sb("srow", [1, T], F32)
    ones = sb("ones", [128, 128], BF16)
    onesf = sb("onesf", [128, 128], F32)
    sel8f = sb("sel8f", [16, 128], F32)
    sel8t = sb("sel8t", [128, 16], BF16)
    junk = sb("junk", [128, 1024], BF16)

    pA = ps_("pA", [128, 512], F32)
    pB = ps_("pB", [128, 512], F32)
    pG_ = ps_("pG", [128, 512], F32)
    pH_ = ps_("pH", [128, 512], F32)

    for dst, srcd in [(pbis[:], pbis_in), (pmye[:], pmye_in), (webb[:], web_in),
                      (ones[:], onesc), (onesf[:], onesfc), (sel8f[:], sel8fc),
                      (sel8t[:], sel8tc)]:
        p.dmao("sync", dst, srcd)
    for j in range(NDC):
        p.dmao("sync", xln4[:, j * T:(j + 1) * T], chunk(xln4_in, j))
    p.w("vector", "sync", dma=True)
    p.op("vector", lambda e, s: e.memset(Tcol[:], 0.5))
    p.op("vector", lambda e, s: e.memset(stp[:], 0.25))
    p.op("vector", lambda e, s: e.tensor_copy(out=pmyeb[:], in_=pmye[:]), tag=True)

    p.w("tensor", "sync", dma=True)
    for do in range(NDC):
        wt = (wt0, wt1)[do % 2]
        if do >= 2:
            p.w("gpsimd", "tensor")
        p.dmao("gpsimd", wt[:], chunk(we_p, do))
        p.w("tensor", "gpsimd", dma=True)
        for half in range(2):
            psx = (pA, pB)[half]
            p.w("tensor", "scalar")
            for dj in range(NDC):
                p.op("tensor", lambda e, s, psx=psx, wt=wt, dj=dj, half=half: e.matmul(
                    psx[:, 0:512], wt[:, dj * 128:(dj + 1) * 128],
                    xln4[:, dj * T + half * 512: dj * T + half * 512 + 512],
                    start=(dj == 0), stop=(dj == NDC - 1)), tag=(dj == NDC - 1))
            p.w("scalar", "tensor")
            p.op("scalar", lambda e, s, psx=psx, do=do, half=half: e.activation(
                out=ebuf[:, do * T + half * 512: do * T + half * 512 + 512],
                in_=psx[:, 0:512], func=AF.Copy), tag=True)
            it = do * 2 + half
            if it < NBIS:
                emit_bisect_iter(p, it, pG_, sel8f, Tcol, Tb, pbis, junk, cnt, cntb, sel8t)


    p.op("vector", lambda e, s: e.tensor_scalar(
        out=junk[0:16, 0:1024], in0=pmye[:], scalar1=Tcol[0:16, 0:1], scalar2=None,
        op0=ALU.is_gt), tag=True)
    p.w("vector", "vector")
    p.op("vector", lambda e, s: e.tensor_tensor(
        out=junk[0:16, 0:1024], in0=junk[0:16, 0:1024], in1=pmyeb[:], op=ALU.mult), tag=True)
    p.w("tensor", "vector")
    for half in range(2):
        p.op("tensor", lambda e, s, half=half: e.matmul(
            pG_[0:1, 0:512], ones[0:16, 0:1], junk[0:16, half * 512:half * 512 + 512],
            start=True, stop=True), tag=True)
        p.w("vector", "tensor")
        p.op("vector", lambda e, s, half=half: e.tensor_copy(
            out=srow[0:1, half * 512:half * 512 + 512], in_=pG_[0:1, 0:512]), tag=True)
        p.w("tensor", "vector")
    for half in range(2):
        p.op("tensor", lambda e, s, half=half: e.matmul(
            pH_[:, 0:512], onesf[0:1, :], srow[0:1, half * 512:half * 512 + 512],
            start=True, stop=True), tag=True)
        p.w("vector", "tensor")
        p.op("vector", lambda e, s, half=half: e.tensor_copy(
            out=sbc[:, half * 512:half * 512 + 512], in_=pH_[:, 0:512]), tag=True)
        p.w("tensor", "vector")
    p.barrier()

    for do in range(NDC):
        b = (xs0, xs1)[do % 2]
        if do >= 2:
            p.w("sync", "sync", dma=True)
        p.dmao("sync", b[:], chunk(x3_in, do))
        p.w("vector", "sync", dma=True)
        p.op("vector", lambda e, s, do=do: e.scalar_tensor_tensor(
            out=scr[:], in0=ebuf[:, do * T:(do + 1) * T], scalar=webb[:, do:do + 1],
            in1=sbc[:], op0=ALU.add, op1=ALU.mult), tag=True)
        p.w("vector", "vector")
        p.op("vector", lambda e, s, b=b: e.tensor_tensor(
            out=b[:], in0=b[:], in1=scr[:], op=ALU.add), tag=True)
        p.w("sync", "vector")
        p.dmao("sync", chunk(out_o, do), b[:])
    p.wv("sync", "sync_dma", p.dma["sync"])
    p.wv("gpsimd", "gpsimd_dma", p.dma["gpsimd"])
    p.build(nc, ctx)
    ctx.close()
    return nc



def _panels(W):
    Wr = W.reshape(NDC, 128, NDC, 128)
    pan = Wr.transpose(2, 1, 0, 3)
    return np.ascontiguousarray(pan.reshape(NDC * 128, H))


_PERM = np.concatenate([np.arange(0, 128, 2), np.arange(1, 128, 2)])


def _l1_inmaps(inputs, nonce_bf, nonce_i):
    x = inputs["L0_attn_a_x"]
    wq_p = _panels(np.ascontiguousarray(inputs["L0_attn_a_Wq"]))
    wf_p = _panels(np.ascontiguousarray(inputs["L0_attn_a_Wf"]))
    wi_p = _panels(np.ascontiguousarray(inputs["L0_attn_a_Wi"]))
    wo_p = _panels(np.ascontiguousarray(inputs["L0_attn_a_Wo"]))
    gw = np.ascontiguousarray(inputs["L0_attn_a_g_weight"].reshape(NDC, 128).T)
    rw = np.ascontiguousarray(inputs["L0_ffn_router_w"].reshape(NDC, 128, 16)
                              .transpose(1, 0, 2).reshape(128, NDC * 16))
    rb = inputs["L0_ffn_router_b"].reshape(1, 16)
    u512 = np.ascontiguousarray(
        np.tile(np.triu(np.ones((64, 64), dtype=np.float32)), (2, 8))).astype(BF)
    ident = np.eye(128, dtype=np.float32).astype(BF)
    ones = np.ones((128, 128), dtype=np.float32).astype(BF)
    maps = []
    for c in range(8):
        b, h = divmod(c, 2)
        xs = np.ascontiguousarray(x[b, h * T:(h + 1) * T, :].T)
        role = float(h)
        maps.append({
            "x_fm": xs.astype(np.float32),
            "wq_p": wq_p, "wf_p": wf_p, "wi_p": wi_p, "wo_p": wo_p,
            "gw": gw.astype(np.float32), "rw": rw.astype(np.float32),
            "rb": rb.astype(np.float32),
            "u512": u512, "ident": ident, "ones128": ones,
            "roleflag": np.full((128, 1), role, dtype=np.float32),
            "exidx": (np.arange(128, dtype=np.uint32) + h * 128).reshape(128, 1),
            "exgidx": (np.arange(128, dtype=np.uint32) + (1 - h) * 128).reshape(128, 1),
            "exfidx": np.array([[256 + h], [256 + h]], dtype=np.uint32),
            "noncef": np.tile(nonce_bf, (2, 1024)),
            "noncei": nonce_i,
        })
    return maps




def _panels_g(W, nout):
    ndo = nout
    Wr = W.reshape(NDC, 128, ndo, 128)
    return np.ascontiguousarray(Wr.transpose(2, 1, 0, 3).reshape(ndo * 128, H))


def _perm_cols(W, nheads):
    Wp = W.copy()
    for h in range(nheads):
        Wp[:, h * 128:(h + 1) * 128] = Wp[:, h * 128:(h + 1) * 128][:, _PERM]
    return Wp


def _rw_layout(rw):
    return np.ascontiguousarray(rw.reshape(NDC, 128, 16).transpose(1, 0, 2)
                                .reshape(128, NDC * 16)).astype(np.float32)


def _l2_inmaps(inputs, res1, P_full, nonce_bf, nonce_i):
    WqT = _perm_cols(np.ascontiguousarray(inputs["L0_attn_b_Wq"].T), NH)
    WkT = _perm_cols(np.ascontiguousarray(inputs["L0_attn_b_Wk"].T), NKV)
    WvT = np.ascontiguousarray(inputs["L0_attn_b_Wv"].T)
    WoT = np.ascontiguousarray(inputs["L0_attn_b_Wo"].T)
    wqt_p = _panels_g(WqT, NDC)
    wkt_p = _panels_g(WkT, 4)
    wot_p = _panels_g(WoT, NDC)
    we_p = _panels_g(np.ascontiguousarray(inputs["L0_ffn_expert_w"]), NDC)
    web = np.ascontiguousarray(inputs["L0_ffn_expert_b"].reshape(NDC, 128).T).astype(np.float32)
    rw = _rw_layout(inputs["L0_ffn_router_w"])
    rb = inputs["L0_ffn_router_b"].reshape(1, 16).astype(np.float32)
    pbis = np.ascontiguousarray(P_full.T.reshape(16, 8, 1024).reshape(128, 1024)).astype(np.float32)
    ident = np.eye(128, dtype=np.float32).astype(BF)
    ones = np.ones((128, 128), dtype=np.float32).astype(BF)
    onesf = np.ones((128, 128), dtype=np.float32)
    sel8f = np.zeros((16, 128), dtype=np.float32)
    for e in range(16):
        sel8f[e, e * 8:(e + 1) * 8] = 1.0
    sel8t = sel8f.T.copy().astype(BF)
    theta = 10000.0
    freqs = 1.0 / theta ** (np.arange(0, 128, 2) / 128.0)
    SWAPM = np.zeros((128, 128), dtype=np.float32)
    for i in range(64):
        SWAPM[64 + i, i] = 1.0
        SWAPM[i, 64 + i] = 1.0
    SWAPM = SWAPM.astype(BF)
    kg = (np.arange(16)[None, :] * 128 + np.arange(128)[:, None]).astype(np.float32)
    maps = []
    for c in range(8):
        b, h = divmod(c, 2)
        pos = np.arange(T) + h * T
        ang = pos[None, :] * freqs[:, None]
        exi = np.zeros((128, 8), dtype=np.uint32)
        exg = np.zeros((128, 16), dtype=np.uint32)
        r = np.arange(128, dtype=np.uint32)
        for g in range(4):
            exi[:, g] = h * 1024 + g * 128 + r
        for vb in range(4):
            exi[:, 4 + vb] = h * 1024 + 512 + vb * 128 + r
        for g in range(4):
            for hf in range(2):
                exg[:, g * 2 + hf] = hf * 1024 + g * 128 + r
        for vbp in range(8):
            hf, vbb = divmod(vbp, 4)
            exg[:, 8 + vbp] = hf * 1024 + 512 + vbb * 128 + r
        maps.append({
            "x1i": res1[c]["x1"], "xln2i": res1[c]["xln2"],
            "pbis": pbis, "pmye": np.ascontiguousarray(P_full.T[:, c * T:(c + 1) * T]).astype(np.float32),
            "we_p": we_p, "web": web, "wqt_p": wqt_p, "wkt_p": wkt_p,
            "wvt": np.ascontiguousarray(WvT.reshape(16, 128, 512).transpose(1, 0, 2)
                                         .reshape(128, 16 * 512)).astype(np.float32),
            "wot_p": wot_p,
            "rw2": rw, "rb2": rb,
            "cosin": np.vstack([np.cos(ang), np.cos(ang)]).astype(np.float32),
            "sinin": np.vstack([-np.sin(ang), np.sin(ang)]).astype(np.float32),
            "swapin": SWAPM,
            "qgidx": pos.reshape(1, T).astype(np.float32), "kgidx": kg,
            "ident2": ident, "ones2": ones, "onesf2": onesf,
            "sel8f": sel8f, "sel8t": sel8t,
            "exidx2": exi, "exgidx2": exg,
            "exfidx2": np.array([[2048 + h], [2048 + h]], dtype=np.uint32),
            "noncef2": np.tile(nonce_bf, (2, 8)),
            "noncei2": nonce_i,
        })
    return maps


def _l3_inmaps(inputs, res2, P2_full):
    we_p = _panels_g(np.ascontiguousarray(inputs["L0_ffn_expert_w"]), NDC)
    web = np.ascontiguousarray(inputs["L0_ffn_expert_b"].reshape(NDC, 128).T).astype(np.float32)
    pbis = np.ascontiguousarray(P2_full.T.reshape(16, 8, 1024).reshape(128, 1024)).astype(np.float32)
    ones = np.ones((128, 128), dtype=np.float32).astype(BF)
    onesf = np.ones((128, 128), dtype=np.float32)
    sel8f = np.zeros((16, 128), dtype=np.float32)
    for e in range(16):
        sel8f[e, e * 8:(e + 1) * 8] = 1.0
    sel8t = sel8f.T.copy().astype(BF)
    maps = []
    for c in range(8):
        maps.append({
            "x3i": res2[c]["x3"], "xln4i": res2[c]["xln4"],
            "pbis3": pbis,
            "pmye3": np.ascontiguousarray(P2_full.T[:, c * T:(c + 1) * T]).astype(np.float32),
            "we_p3": we_p, "web3": web, "ones3": ones, "onesf3": onesf,
            "sel8f3": sel8f, "sel8t3": sel8t,
        })
    return maps


_NC_CACHE = {}


def kernel(**inputs):
    inputs = {k: np.asarray(v) for k, v in inputs.items()}
    nv = np.random.randint(1, 2 ** 31 - 1, dtype=np.int32)
    nonce_i = np.array([[nv]], dtype=np.int32)
    nonce_bf = np.frombuffer(nonce_i.tobytes(), dtype=BF).reshape(1, 2).copy()

    if "L1" not in _NC_CACHE:
        _NC_CACHE["L1"] = build_L1()
        _NC_CACHE["L2"] = build_L2()
        _NC_CACHE["L3"] = build_L3()
    res1 = run_bass_kernel_spmd(_NC_CACHE["L1"], _l1_inmaps(inputs, nonce_bf, nonce_i),
                                core_ids=list(range(8))).results
    P_full = np.concatenate([res1[c]["p1"] for c in range(8)], axis=0)
    nv2 = np.array([[np.int32(nv ^ 0x1234567)]], dtype=np.int32)
    nonce_bf2 = np.frombuffer(nv2.tobytes(), dtype=BF).reshape(1, 2).copy()
    res2 = run_bass_kernel_spmd(_NC_CACHE["L2"],
                                _l2_inmaps(inputs, res1, P_full, nonce_bf2, nv2),
                                core_ids=list(range(8))).results
    P2_full = np.concatenate([res2[c]["p2"] for c in range(8)], axis=0)
    res3 = run_bass_kernel_spmd(_NC_CACHE["L3"], _l3_inmaps(inputs, res2, P2_full),
                                core_ids=list(range(8))).results
    out = np.zeros((4, 2048, 2048), dtype=np.float32)
    for c in range(8):
        b, h = divmod(c, 2)
        out[b, h * T:(h + 1) * T, :] = res3[c]["outx"].T
    return out


# version 07
# speedup vs baseline: 1.6223x; 1.6223x over previous
import numpy as np
import ml_dtypes
from contextlib import ExitStack
import concourse.bass as bass
import concourse.mybir as mybir
from concourse.bass import IndirectOffsetOnAxis
from concourse.bass_utils import run_bass_kernel_spmd

F32 = mybir.dt.float32
BF16 = mybir.dt.bfloat16
U32 = mybir.dt.uint32
I32 = mybir.dt.int32
AF = mybir.ActivationFunctionType
ALU = mybir.AluOpType
AXX = mybir.AxisListType.X
BF = ml_dtypes.bfloat16

H = 2048
T = 1024
NH = 16
C = 64
NCH = 16
NDC = 16
NTC = 8
EPS = 1e-5
LNDK = float(np.log(128.0 ** -0.5))
NKV = 4
GRP = 4
NE = 16
NBIS = 30
MAGIC = 0x41FA5A5A
FLAGVAL = np.frombuffer(np.int32(MAGIC).tobytes(), dtype=BF).copy()

ENGINES = ("sync", "gpsimd", "tensor", "scalar", "vector")


class Prog:

    def __init__(self):
        self.ops = {e: [] for e in ENGINES}
        self.tag = {e: 0 for e in ENGINES}
        self.dma = {e: 0 for e in ENGINES}
        self.lastwait = {e: {} for e in ENGINES}

    def op(self, e, fn, tag=False):
        if tag:
            self.tag[e] += 1

            def wrapped(eng, sems, fn=fn, e=e):
                r = fn(eng, sems)
                assert r is not None, "tagged op must return an instruction"
                r.then_inc(sems[e], 1)
            self.ops[e].append(wrapped)
        else:
            self.ops[e].append(fn)

    def dmao(self, e, out, in_):
        self.dma[e] += 16

        def fn(eng, sems, out=out, in_=in_, e=e):
            eng.dma_start(out=out, in_=in_).then_inc(sems[e + "_dma"], 16)
        self.ops[e].append(fn)

    def scat(self, e, out, oidx, in_):
        self.dma[e] += 16

        def fn(eng, sems, out=out, oidx=oidx, in_=in_, e=e):
            eng.indirect_dma_start(
                out=out, out_offset=IndirectOffsetOnAxis(ap=oidx, axis=0),
                in_=in_, in_offset=None).then_inc(sems[e + "_dma"], 16)
        self.ops[e].append(fn)

    def gath(self, e, out, in_, iidx):
        self.dma[e] += 16

        def fn(eng, sems, out=out, in_=in_, iidx=iidx, e=e):
            eng.indirect_dma_start(
                out=out, out_offset=None,
                in_=in_, in_offset=IndirectOffsetOnAxis(ap=iidx, axis=0)).then_inc(sems[e + "_dma"], 16)
        self.ops[e].append(fn)

    def mark(self, prod, dma=False):
        return self.dma[prod] if dma else self.tag[prod]

    def w_at(self, waiter, prod, val, dma=False):
        key = prod + ("_dma" if dma else "")
        if val <= 0 or self.lastwait[waiter].get(key, -1) >= val:
            return
        self.lastwait[waiter][key] = val
        self.ops[waiter].append(lambda eng, sems, key=key, val=val: eng.wait_ge(sems[key], val))

    def w(self, waiter, prod, dma=False):
        key = prod + ("_dma" if dma else "")
        val = self.dma[prod] if dma else self.tag[prod]
        if val <= 0 or self.lastwait[waiter].get(key, -1) >= val:
            return
        self.lastwait[waiter][key] = val
        self.ops[waiter].append(lambda eng, sems, key=key, val=val: eng.wait_ge(sems[key], val))

    def barrier(self):
        for wtr in ENGINES:
            for prod in ENGINES:
                if prod != wtr:
                    self.w(wtr, prod)
                self.w(wtr, prod, dma=True)

    def wv(self, waiter, key, val):
        self.ops[waiter].append(lambda eng, sems, key=key, val=val: eng.wait_ge(sems[key], val))

    def spin_flags(self, e, flag_aps, expect_ap):
        def fn(eng, sems):
            rc = eng.alloc_register(name="r_spin_c")
            ra = eng.alloc_register(name="r_spin_a")
            rexp = eng.alloc_register(name="r_spin_e")
            eng.load(rexp, expect_ap)
            regs = [eng.alloc_register(name=f"r_spin_f{i}") for i in range(len(flag_aps))]
            eng.reg_mov(rc, 1)
            with eng.While(rc):
                for _ in range(200):
                    eng.reg_alu(ra, ra, 1, ALU.add)
                for rf, ap in zip(regs, flag_aps):
                    eng.load(rf, ap)
                eng.reg_mov(rc, 0)
                for rf in regs:
                    eng.reg_alu(ra, rf, rexp, ALU.not_equal)
                    eng.reg_alu(rc, rc, ra, ALU.bitwise_or)
            for r in regs + [ra, rc, rexp]:
                eng.free_register(r)
        self.ops[e].append(fn)

    def build(self, nc, ctx):
        semnames = list(ENGINES) + [e + "_dma" for e in ENGINES]
        sems = {s: ctx.enter_context(nc.semaphore(name="sem_" + s)) for s in semnames}
        blk = ctx.enter_context(nc.Block())

        def body(name):
            def f(eng):
                for fn in self.ops[name]:
                    fn(eng, sems)
            return f
        blk.sync(body("sync"))
        blk.gpsimd(body("gpsimd"))
        blk.tensor(body("tensor"))
        blk.scalar(body("scalar"))
        blk.vector(body("vector"))


def chunk(t, j, w=128):
    return t[j * w:(j + 1) * w, :]


def emit_ln_finish(p, psx, psx2, mb, rstdb, scr, rms_only=False, epsc=None):
    p.w("vector", "tensor")
    for half in range(2):
        sl = slice(half * 512, (half + 1) * 512)
        if rms_only:
            p.op("vector", lambda e, s, half=half, sl=sl: e.tensor_scalar(
                out=rstdb[:, sl], in0=psx2[half][:, 0:512], scalar1=1.0 / H,
                scalar2=None, op0=ALU.mult), tag=(half == 1))
        else:
            p.op("vector", lambda e, s, half=half, sl=sl: e.tensor_scalar(
                out=mb[:, sl], in0=psx[half][:, 0:512], scalar1=1.0 / H,
                scalar2=None, op0=ALU.mult))
            p.op("vector", lambda e, s, half=half, sl=sl: e.tensor_scalar(
                out=rstdb[:, sl], in0=psx2[half][:, 0:512], scalar1=1.0 / H,
                scalar2=None, op0=ALU.mult))
            p.op("vector", lambda e, s, sl=sl: e.scalar_tensor_tensor(
                out=scr[:, sl], in0=mb[:, sl], scalar=-1.0, in1=mb[:, sl],
                op0=ALU.mult, op1=ALU.mult))
            p.op("vector", lambda e, s, sl=sl: e.tensor_tensor(
                out=rstdb[:, sl], in0=rstdb[:, sl], in1=scr[:, sl], op=ALU.add),
                tag=(half == 1))
    p.w("scalar", "vector")
    p.op("scalar", lambda e, s: e.activation(out=rstdb[:], in_=rstdb[:], func=AF.Ln, bias=epsc[:, 0:1]))
    p.op("scalar", lambda e, s: e.activation(out=rstdb[:], in_=rstdb[:], func=AF.Exp, scale=-0.5), tag=True)



def build_L1(dbg=False):
    nc = bass.Bass("TRN2", target_bir_lowering=False)
    x_in = nc.dram_tensor("x_fm", [H, T], F32, kind="ExternalInput").ap()
    wq_p = nc.dram_tensor("wq_p", [NDC * 128, H], F32, kind="ExternalInput").ap()
    wf_p = nc.dram_tensor("wf_p", [NDC * 128, H], F32, kind="ExternalInput").ap()
    wi_p = nc.dram_tensor("wi_p", [NDC * 128, H], F32, kind="ExternalInput").ap()
    wo_p = nc.dram_tensor("wo_p", [NDC * 128, H], F32, kind="ExternalInput").ap()
    gwt = nc.dram_tensor("gw", [128, NDC], F32, kind="ExternalInput").ap()
    rw_in = nc.dram_tensor("rw", [128, NDC * 16], F32, kind="ExternalInput").ap()
    rbias = nc.dram_tensor("rb", [1, 16], F32, kind="ExternalInput").ap()
    u512c = nc.dram_tensor("u512", [128, 512], BF16, kind="ExternalInput").ap()
    identc = nc.dram_tensor("ident", [128, 128], BF16, kind="ExternalInput").ap()
    onesc = nc.dram_tensor("ones128", [128, 128], BF16, kind="ExternalInput").ap()
    roleflag = nc.dram_tensor("roleflag", [128, 1], F32, kind="ExternalInput").ap()
    exidx = nc.dram_tensor("exidx", [128, 1], U32, kind="ExternalInput").ap()
    exgidx = nc.dram_tensor("exgidx", [128, 1], U32, kind="ExternalInput").ap()
    exfidx = nc.dram_tensor("exfidx", [2, 1], U32, kind="ExternalInput").ap()
    noncein = nc.dram_tensor("noncef", [2, 2048], BF16, kind="ExternalInput").ap()
    noncei_in = nc.dram_tensor("noncei", [1, 1], I32, kind="ExternalInput").ap()

    x1_out = nc.dram_tensor("x1", [H, T], F32, kind="ExternalOutput").ap()
    xln2_out = nc.dram_tensor("xln2", [H, T], BF16, kind="ExternalOutput").ap()
    p1_out = nc.dram_tensor("p1", [T, 16], F32, kind="ExternalOutput").ap()

    exch = nc.dram_tensor("exch_l1", [258, 2048], BF16, kind="Internal",
                          addr_space="Shared").ap()
    if dbg:
        d_e1 = nc.dram_tensor("d_e1", [128, 1024], F32, kind="ExternalOutput").ap()
        d_qi = nc.dram_tensor("d_qi", [128, 1024], BF16, kind="ExternalOutput").ap()
        d_es = nc.dram_tensor("d_es", [128, 16], F32, kind="ExternalOutput").ap()
        d_qb = nc.dram_tensor("d_qb", [128, 128], BF16, kind="ExternalOutput").ap()
        d_qe = nc.dram_tensor("d_qe", [128, 256], BF16, kind="ExternalOutput").ap()
        d_gt = nc.dram_tensor("d_gt", [128, NTC * H], BF16, kind="ExternalOutput").ap()
        d_vt = nc.dram_tensor("d_vt", [128, NTC * H], BF16, kind="ExternalOutput").ap()
        d_ob = nc.dram_tensor("d_ob", [128, NDC * T], BF16, kind="ExternalOutput").ap()
        d_q = nc.dram_tensor("d_q", [128, NDC * T], BF16, kind="ExternalOutput").ap()

    p = Prog()
    ctx = ExitStack()
    def sb(name, shape, dt):
        return ctx.enter_context(nc.sbuf_tensor("sb_" + name, shape, dt))

    def ps_(name, shape, dt):
        return ctx.enter_context(nc.psum_tensor("ps_" + name, shape, dt))

    qbuf = sb("qbuf", [128, NDC * T], BF16)
    gtok = sb("gtok", [128, NTC * H], BF16)
    xv = sb("xv", [128, NDC * T], BF16)
    xs0 = sb("xs0", [128, T], F32)
    xs1 = sb("xs1", [128, T], F32)
    sqs = sb("sqs", [128, T], BF16)
    bfs = sb("bfs", [128, T], BF16)
    mb = sb("mb", [128, T], F32)
    rstdb = sb("rstdb", [128, T], F32)
    scr = sb("scr", [128, T], F32)
    wt0 = sb("wt0", [128, H], BF16)
    wt1 = sb("wt1", [128, H], BF16)
    rwt = sb("rwt", [128, NDC * 16], BF16)
    rbt = sb("rbt", [1, 16], BF16)
    gwb = sb("gwb", [128, NDC], F32)
    e1s = sb("e1s", [128, 1024], F32)
    e2s = sb("e2s", [128, 1024], F32)
    kds = sb("kds", [128, 1024], BF16)
    qis = sb("qis", [128, 1024], BF16)
    pts = sb("pts", [128, 1024], BF16)
    kdts = sb("kdts", [128, 2048], BF16)
    u512 = sb("u512", [128, 512], BF16)
    ident = sb("ident", [128, 128], BF16)
    ones = sb("ones", [128, 128], BF16)
    rolef = sb("rolef", [128, 1], F32)
    exi = sb("exi", [128, 1], U32)
    exg = sb("exg", [128, 1], U32)
    exf = sb("exf", [2, 1], U32)
    flagst = sb("flagst", [2, 2048], BF16)
    noncei = sb("noncei", [1, 1], I32)
    prb = sb("prb", [128, 16], F32)
    col0 = sb("col0", [128, 1], F32)
    col1 = sb("col1", [128, 1], F32)
    epsc = sb("epsc", [128, 1], F32)
    col2 = sb("col2", [128, 1], F32)
    lndkc = sb("lndkc", [128, 1], F32)
    Mst = sb("Mst", [128, NH * 128], F32)
    Mbf = sb("Mbf", [128, NH * 128], BF16)
    Scum = sb("Scum", [128, NH], F32)
    Gtot = sb("Gtot", [128, NH], F32)
    EGtot = sb("EGtot", [128, NH], F32)
    EScum = sb("EScum", [128, NH], F32)

    pA = ps_("pA", [128, 512], F32)
    pB = ps_("pB", [128, 512], F32)
    pC = ps_("pC", [128, 512], F32)
    pD_ = ps_("pD_", [128, 512], F32)
    pE = ps_("pE", [128, 512], F32)
    pF = ps_("pF", [128, 512], F32)
    pS0 = ps_("pS0", [128, 512], F32)
    pT0 = ps_("pT0", [128, 512], BF16)

    phase1 = ExitStack()
    fm1 = phase1.enter_context(nc.sbuf_tensor("fm1", [128, NDC * T], BF16))
    xln = xv
    vtok = xv

    for dst, src in [(u512[:], u512c), (ident[:], identc), (ones[:], onesc),
                     (rolef[:], roleflag), (exi[:], exidx), (exg[:], exgidx),
                     (exf[:], exfidx), (flagst[:], noncein), (gwb[:], gwt),
                     (noncei[:], noncei_in)]:
        p.dmao("sync", dst, src)
    p.dmao("gpsimd", rbt[:], rbias)
    p.dmao("gpsimd", rwt[:], rw_in)

    p.op("vector", lambda e, s: e.memset(epsc[:], EPS))
    p.op("vector", lambda e, s: e.memset(lndkc[:], LNDK), tag=True)

    for j in range(NDC):
        b = (xs0, xs1)[j % 2]
        if j >= 2:
            p.w("sync", "scalar")
        p.dmao("sync", b[:], chunk(x_in, j))
        p.w("scalar", "sync", dma=True)
        if j >= 1:
            p.w("scalar", "tensor")
        p.op("scalar", lambda e, s, b=b: e.activation(out=sqs[:], in_=b[:], func=AF.Square))
        p.op("scalar", lambda e, s, b=b: e.activation(out=bfs[:], in_=b[:], func=AF.Copy), tag=True)
        p.w("tensor", "scalar")
        for half in range(2):
            p.op("tensor", lambda e, s, j=j, half=half: e.matmul(
                (pA, pB)[half][:, 0:512], ones[:], bfs[:, half * 512:half * 512 + 512],
                start=(j == 0), stop=(j == NDC - 1)))
            p.op("tensor", lambda e, s, j=j, half=half: e.matmul(
                (pC, pD_)[half][:, 0:512], ones[:], sqs[:, half * 512:half * 512 + 512],
                start=(j == 0), stop=(j == NDC - 1)),
                tag=(j == NDC - 1 and half == 1))
    emit_ln_finish(p, (pA, pB), (pC, pD_), mb, rstdb, scr, epsc=epsc)

    for j in range(NDC):
        b = (xs0, xs1)[j % 2]
        if j >= 2:
            p.w("sync", "vector")
        p.dmao("sync", b[:], chunk(x_in, j))
        p.w("vector", "sync", dma=True)
        p.w("vector", "scalar")
        p.op("vector", lambda e, s, b=b: e.tensor_tensor(
            out=scr[:], in0=b[:], in1=mb[:], op=ALU.subtract))
        p.op("vector", lambda e, s, j=j: e.tensor_tensor(
            out=xln[:, j * T:(j + 1) * T], in0=scr[:], in1=rstdb[:], op=ALU.mult), tag=True)

    def proj_fm(wpan, evac):
        bankdone = {0: (0, 0), 1: (0, 0)}
        for do in range(NDC):
            wt = (wt0, wt1)[do % 2]
            if do >= 2:
                p.w("gpsimd", "tensor")
            p.dmao("gpsimd", wt[:], chunk(wpan, do))
            p.w("tensor", "gpsimd", dma=True)
            if do == 0:
                p.w("tensor", "vector")
            for half in range(2):
                psx = (pE, pF)[half]
                sm, vm = bankdone[half]
                p.w_at("tensor", "scalar", sm)
                p.w_at("tensor", "vector", vm)
                for dj in range(NDC):
                    p.op("tensor", lambda e, s, psx=psx, wt=wt, dj=dj, half=half: e.matmul(
                        psx[:, 0:512], wt[:, dj * 128:(dj + 1) * 128],
                        xln[:, dj * T + half * 512: dj * T + half * 512 + 512],
                        start=(dj == 0), stop=(dj == NDC - 1)), tag=(dj == NDC - 1))
                evac(p, psx, do, half)
                bankdone[half] = (p.mark("scalar"), p.mark("vector"))

    def evac_silu(p, psx, do, half):
        p.w("scalar", "tensor")
        p.op("scalar", lambda e, s, psx=psx, do=do, half=half: e.activation(
            out=qbuf[:, do * T + half * 512: do * T + half * 512 + 512],
            in_=psx[:, 0:512], func=AF.Silu), tag=True)

    def evac_g(p, psx, do, half):
        sl = slice(half * 512, (half + 1) * 512)
        p.w("scalar", "tensor")
        p.w("scalar", "vector")
        p.op("scalar", lambda e, s, psx=psx, sl=sl: e.activation(
            out=e1s[:, sl], in_=psx[:, 0:512], func=AF.Abs))
        p.op("scalar", lambda e, s, sl=sl: e.activation(
            out=e1s[:, sl], in_=e1s[:, sl], func=AF.Exp, scale=-1.0))
        p.op("scalar", lambda e, s, sl=sl: e.activation(
            out=e1s[:, sl], in_=e1s[:, sl], func=AF.Ln, bias=1.0), tag=True)
        p.w("vector", "scalar")
        p.op("vector", lambda e, s, psx=psx, sl=sl: e.tensor_scalar(
            out=e2s[:, sl], in0=psx[:, 0:512], scalar1=0.0, scalar2=None, op0=ALU.min))
        p.op("vector", lambda e, s, do=do, half=half, sl=sl: e.tensor_tensor(
            out=fm1[:, do * T + half * 512: do * T + half * 512 + 512],
            in0=e2s[:, sl], in1=e1s[:, sl], op=ALU.subtract), tag=True)

    def evac_copy(p, psx, do, half):
        p.w("scalar", "tensor")
        p.op("scalar", lambda e, s, psx=psx, do=do, half=half: e.activation(
            out=fm1[:, do * T + half * 512: do * T + half * 512 + 512],
            in_=psx[:, 0:512], func=AF.Copy), tag=True)

    proj_fm(wq_p, evac_silu)
    p.barrier()
    if dbg:
        p.w("gpsimd", "scalar")
        p.dmao("gpsimd", d_qe, qbuf[:, 15 * T: 15 * T + 256])
        p.w("scalar", "gpsimd", dma=True)
        p.w("vector", "gpsimd", dma=True)
        p.w("tensor", "gpsimd", dma=True)
    proj_fm(wf_p, evac_g)
    p.barrier()

    def transpose_all(dstbuf):
        for k in range(NDC * NTC):
            dj, tt = divmod(k, NTC)
            slot = k % 4
            if k >= 4 and slot == 0:
                p.w("tensor", "vector")
            p.op("tensor", lambda e, s, dj=dj, tt=tt, slot=slot: e.transpose(
                pT0[:, slot * 128:(slot + 1) * 128],
                fm1[:, dj * T + tt * 128: dj * T + (tt + 1) * 128], ident[:]),
                tag=(slot == 3))
            if slot == 3:
                p.w("vector", "tensor")
                base = k - 3
                for kk in range(base, k + 1):
                    djj, ttt = divmod(kk, NTC)
                    sl2 = kk % 4
                    p.op("vector", lambda e, s, djj=djj, ttt=ttt, sl2=sl2: e.tensor_copy(
                        out=dstbuf[:, ttt * H + djj * 128: ttt * H + (djj + 1) * 128],
                        in_=pT0[:, sl2 * 128:(sl2 + 1) * 128]), tag=(sl2 == 3))

    transpose_all(gtok)
    p.barrier()
    if dbg:
        p.w("gpsimd", "vector")
        p.dmao("gpsimd", d_gt, gtok[:])
    proj_fm(wi_p, evac_copy)
    p.barrier()
    transpose_all(vtok)
    if dbg:
        p.w("gpsimd", "vector")
        p.dmao("gpsimd", d_vt, vtok[:])

    phase1.close()
    obuf = ctx.enter_context(nc.sbuf_tensor("obuf", [128, NDC * T], BF16))

    p.op("vector", lambda e, s: e.memset(Mst[:], 0.0))
    p.op("vector", lambda e, s: e.memset(Mbf[:], 0.0))
    p.op("vector", lambda e, s: e.memset(Scum[:], 0.0), tag=True)

    for c in range(NCH):
        if c % 4 == 0:
            p.barrier()
        tt, par = divmod(c, 2)
        par *= 64
        p.w("tensor", "vector")
        p.w("tensor", "scalar")
        for h in range(NH):
            psx = pA if h < 8 else pB
            colb = (h % 8) * 64
            p.op("tensor", lambda e, s, psx=psx, colb=colb, h=h, tt=tt, par=par: e.matmul(
                psx[:, colb:colb + 64],
                gtok[par:par + 64, tt * H + h * 128: tt * H + (h + 1) * 128],
                u512[par:par + 64, 0:64], start=(h % 8 == 0), stop=(h % 8 == 7)),
                tag=(h in (7, 15)))
        p.w("scalar", "tensor")
        p.w("scalar", "vector")
        p.op("scalar", lambda e, s: e.activation(out=e1s[:, 0:512], in_=pA[:, 0:512], func=AF.Exp, bias=lndkc[:, 0:1]))
        p.op("scalar", lambda e, s: e.activation(out=e1s[:, 512:1024], in_=pB[:, 0:512], func=AF.Exp, bias=lndkc[:, 0:1]))
        p.op("scalar", lambda e, s: e.activation(out=e2s[:, 0:512], in_=pA[:, 0:512], func=AF.Exp, scale=-1.0))
        p.op("scalar", lambda e, s: e.activation(out=e2s[:, 512:1024], in_=pB[:, 0:512], func=AF.Exp, scale=-1.0))
        p.op("scalar", lambda e, s: e.activation(out=EScum[:], in_=Scum[:], func=AF.Exp), tag=True)
        p.w("vector", "tensor")
        p.op("vector", lambda e, s: e.tensor_copy(out=Gtot[:, 0:8], in_=pA[:, 63:512:64]))
        p.op("vector", lambda e, s: e.tensor_copy(out=Gtot[:, 8:16], in_=pB[:, 63:512:64]), tag=True)
        p.w("scalar", "vector")
        p.op("scalar", lambda e, s: e.activation(out=EGtot[:], in_=Gtot[:], func=AF.Exp), tag=True)
        p.w("vector", "scalar")
        p.op("vector", lambda e, s: e.tensor_tensor(out=Scum[:], in0=Scum[:], in1=Gtot[:], op=ALU.add))
        p.op("vector", lambda e, s: e.tensor_tensor(
            out=kds[:, 1:1024], in0=e2s[:, 1:1024], in1=e2s[:, 0:1023], op=ALU.subtract))
        p.op("vector", lambda e, s: e.tensor_scalar(
            out=kds[:, 0:1024:64], in0=e2s[:, 0:1024:64], scalar1=-1.0, scalar2=None,
            op0=ALU.add))
        for h in range(NH):
            p.op("vector", lambda e, s, c=c, h=h: e.tensor_tensor(
                out=qis[:, h * 64:(h + 1) * 64],
                in0=qbuf[:, h * T + c * 64: h * T + (c + 1) * 64],
                in1=e1s[:, h * 64:(h + 1) * 64], op=ALU.mult), tag=(h == NH - 1))
        if dbg and c == 0:
            p.w("gpsimd", "vector")
            p.w("gpsimd", "scalar")
            p.dmao("gpsimd", d_e1, e1s[:])
            p.dmao("gpsimd", d_qi, qis[:])
            p.dmao("gpsimd", d_es, EScum[:])
            p.dmao("gpsimd", d_qb, qbuf[:, 15 * T: 15 * T + 128])
            p.w("vector", "gpsimd", dma=True)
        for h in range(NH):
            p.op("vector", lambda e, s, h=h, c=c: e.tensor_scalar(
                out=qbuf[:, h * T + c * 64: h * T + (c + 1) * 64],
                in0=qis[:, h * 64:(h + 1) * 64],
                scalar1=EScum[:, h:h + 1], scalar2=None, op0=ALU.mult),
                tag=(h == NH - 1))
        p.w("tensor", "vector")
        for h in range(NH):
            psx = pS0 if h < 8 else pE
            colb = (h % 8) * 64
            p.op("tensor", lambda e, s, psx=psx, colb=colb, h=h, par=par: e.matmul(
                psx[par:par + 64, colb:colb + 64], kds[:, h * 64:(h + 1) * 64],
                qis[:, h * 64:(h + 1) * 64], start=(h % 8 == 0), stop=(h % 8 == 7),
                tile_position=(0, par)),
                tag=(h in (7, 15)))
        p.w("vector", "tensor")
        p.op("vector", lambda e, s, par=par: e.tensor_tensor(
            out=pts[par:par + 64, 0:512], in0=pS0[par:par + 64, 0:512],
            in1=u512[par:par + 64, :], op=ALU.mult))
        p.op("vector", lambda e, s, par=par: e.tensor_tensor(
            out=pts[par:par + 64, 512:1024], in0=pE[par:par + 64, 0:512],
            in1=u512[par:par + 64, :], op=ALU.mult))
        p.op("vector", lambda e, s: e.tensor_copy(out=Mbf[:], in_=Mst[:]), tag=True)
        p.w("tensor", "vector")
        p.w("tensor", "scalar")
        for h in range(NH):
            psx = pC if h < 8 else pD_
            colb = (h % 8) * 64
            p.op("tensor", lambda e, s, psx=psx, colb=colb, h=h, tt=tt, par=par: e.matmul(
                psx[:, colb:colb + 64],
                vtok[par:par + 64, tt * H + h * 128: tt * H + (h + 1) * 128],
                pts[par:par + 64, h * 64:(h + 1) * 64], start=(h % 8 == 0), stop=False))
            p.op("tensor", lambda e, s, psx=psx, colb=colb, h=h: e.matmul(
                psx[:, colb:colb + 64], Mbf[:, h * 128:(h + 1) * 128],
                qis[:, h * 64:(h + 1) * 64], start=False, stop=(h % 8 == 7)),
                tag=(h in (7, 15)))
        p.w("scalar", "tensor")
        obv = obuf[:].rearrange("q (h a) -> q h a", h=NH)
        p.op("scalar", lambda e, s, c=c, obv=obv: e.activation(
            out=obv[:, 0:8, c * 64:(c + 1) * 64],
            in_=pC[:, 0:512].rearrange("q (h a) -> q h a", h=8), func=AF.Copy))
        p.op("scalar", lambda e, s, c=c, obv=obv: e.activation(
            out=obv[:, 8:16, c * 64:(c + 1) * 64],
            in_=pD_[:, 0:512].rearrange("q (h a) -> q h a", h=8), func=AF.Copy), tag=True)
        for hg in range(2):
            p.w("tensor", "vector")
            for hh in range(4):
                h = hg * 8 + hh
                p.op("tensor", lambda e, s, hh=hh, h=h, par=par: e.transpose(
                    pT0[par:par + 64, hh * 128:(hh + 1) * 128], kds[:, h * 64:(h + 1) * 64],
                    ident[:], tile_position=(0, par)), tag=(hh == 3))
            p.w("vector", "tensor")
            p.op("vector", lambda e, s, hg=hg, par=par: e.tensor_copy(
                out=kdts[par:par + 64, hg * 1024:hg * 1024 + 512],
                in_=pT0[par:par + 64, 0:512]), tag=True)
            p.w("tensor", "vector")
            for hh in range(4):
                h = hg * 8 + 4 + hh
                p.op("tensor", lambda e, s, hh=hh, h=h, par=par: e.transpose(
                    pT0[par:par + 64, hh * 128:(hh + 1) * 128], kds[:, h * 64:(h + 1) * 64],
                    ident[:], tile_position=(0, par)), tag=(hh == 3))
            p.w("vector", "tensor")
            p.op("vector", lambda e, s, hg=hg, par=par: e.tensor_copy(
                out=kdts[par:par + 64, hg * 1024 + 512:hg * 1024 + 1024],
                in_=pT0[par:par + 64, 0:512]), tag=True)
        for h in range(NH):
            q4 = h % 4
            if q4 == 0:
                p.w("tensor", "vector")
            p.op("tensor", lambda e, s, h=h, q4=q4, tt=tt, par=par: e.matmul(
                pF[:, q4 * 128:(q4 + 1) * 128], kdts[par:par + 64, h * 128:(h + 1) * 128],
                vtok[par:par + 64, tt * H + h * 128: tt * H + (h + 1) * 128],
                start=(q4 == 0), stop=(q4 == 3)), tag=(q4 == 3))
            if q4 == 3:
                p.w("vector", "tensor")
                h0_ = h - 3
                mv = Mst[:, h0_ * 128:(h0_ + 4) * 128].rearrange("q (g a) -> q g a", g=4)
                pv = pF[:, 0:512].rearrange("q (g a) -> q g a", g=4)
                egb = EGtot[:, h0_:h0_ + 4].rearrange("q (g o) -> q g o", o=1).broadcast_to([128, 4, 128])
                p.op("vector", lambda e, s, mv=mv, pv=pv: e.tensor_tensor(
                    out=mv, in0=mv, in1=pv, op=ALU.add), tag=True)
                p.w("vector", "vector")
                p.op("vector", lambda e, s, mv=mv, egb=egb: e.tensor_tensor(
                    out=mv, in0=mv, in1=egb, op=ALU.mult), tag=True)

    p.barrier()
    p.op("vector", lambda e, s: e.tensor_copy(out=Mbf[:], in_=Mst[:]), tag=True)
    p.w("gpsimd", "vector")
    p.scat("gpsimd", exch[:], exi[:, 0:1], Mbf[:])
    p.wv("gpsimd", "gpsimd_dma", p.dma["gpsimd"])
    p.scat("gpsimd", exch[:], exf[:, 0:1], flagst[:])
    p.wv("gpsimd", "gpsimd_dma", p.dma["gpsimd"])
    p.spin_flags("gpsimd", [exch[256:257, 0:2].bitcast(I32),
                            exch[257:258, 0:2].bitcast(I32)], noncei[0:1, 0:1])
    hinit = Mbf
    p.gath("gpsimd", hinit[:], exch[:], exg[:, 0:1])
    p.w("vector", "gpsimd", dma=True)
    p.op("vector", lambda e, s: e.tensor_scalar(
        out=hinit[:], in0=hinit[:], scalar1=rolef[:, 0:1], scalar2=None,
        op0=ALU.mult), tag=True)
    for h in range(NH):
        p.w("tensor", "vector")
        for half in range(2):
            psx = (pC, pD_)[half]
            p.op("tensor", lambda e, s, h=h, half=half, psx=psx: e.matmul(
                psx[:, 0:512], hinit[:, h * 128:(h + 1) * 128],
                qbuf[:, h * T + half * 512: h * T + half * 512 + 512],
                start=True, stop=True), tag=(half == 1))
        p.w("vector", "tensor")
        for half in range(2):
            psx = (pC, pD_)[half]
            p.op("vector", lambda e, s, h=h, half=half, psx=psx: e.tensor_tensor(
                out=obuf[:, h * T + half * 512: h * T + half * 512 + 512],
                in0=obuf[:, h * T + half * 512: h * T + half * 512 + 512],
                in1=psx[:, 0:512], op=ALU.add), tag=(half == 1))

    if dbg:
        p.w("gpsimd", "vector")
        p.dmao("gpsimd", d_ob, obuf[:])
        p.dmao("gpsimd", d_q, qbuf[:])

    p.barrier()
    for j in range(NDC):
        p.w("scalar", "vector")
        if j >= 1:
            p.w("scalar", "tensor")
        p.op("scalar", lambda e, s, j=j: e.activation(
            out=sqs[:], in_=obuf[:, j * T:(j + 1) * T], func=AF.Square), tag=True)
        p.w("tensor", "scalar")
        for half in range(2):
            p.op("tensor", lambda e, s, j=j, half=half: e.matmul(
                (pA, pB)[half][:, 0:512], ones[:], sqs[:, half * 512:half * 512 + 512],
                start=(j == 0), stop=(j == NDC - 1)),
                tag=(j == NDC - 1 and half == 1))
    emit_ln_finish(p, None, (pA, pB), mb, rstdb, scr, rms_only=True, epsc=epsc)
    p.w("vector", "scalar")
    for j in range(NDC):
        p.op("vector", lambda e, s, j=j: e.scalar_tensor_tensor(
            out=obuf[:, j * T:(j + 1) * T], in0=obuf[:, j * T:(j + 1) * T],
            scalar=gwb[:, j:j + 1], in1=rstdb[:], op0=ALU.mult, op1=ALU.mult),
            tag=(j == NDC - 1))

    p.barrier()
    wo_done = {0: 0, 1: 0}
    for do in range(NDC):
        wt = (wt0, wt1)[do % 2]
        if do >= 2:
            p.w("gpsimd", "tensor")
        p.dmao("gpsimd", wt[:], chunk(wo_p, do))
        b = (xs0, xs1)[do % 2]
        if do >= 2:
            p.w("sync", "scalar")
        p.dmao("sync", b[:], chunk(x_in, do))
        p.w("tensor", "gpsimd", dma=True)
        for half in range(2):
            psx = (pE, pF)[half]
            p.w_at("tensor", "vector", wo_done[half])
            for dj in range(NDC):
                p.op("tensor", lambda e, s, psx=psx, wt=wt, dj=dj, half=half: e.matmul(
                    psx[:, 0:512], wt[:, dj * 128:(dj + 1) * 128],
                    obuf[:, dj * T + half * 512: dj * T + half * 512 + 512],
                    start=(dj == 0), stop=(dj == NDC - 1)), tag=(dj == NDC - 1))
        p.w("vector", "tensor")
        p.w("vector", "sync", dma=True)
        for half in range(2):
            psx = (pE, pF)[half]
            sl = slice(half * 512, (half + 1) * 512)
            p.op("vector", lambda e, s, psx=psx, b=b, sl=sl: e.tensor_tensor(
                out=b[:, sl], in0=b[:, sl], in1=psx[:, 0:512], op=ALU.add),
                tag=True)
            wo_done[half] = p.mark("vector")
        p.w("sync", "vector")
        p.dmao("sync", chunk(x1_out, do), b[:])
        p.w("scalar", "vector")
        if do >= 1:
            p.w("scalar", "tensor")
        p.op("scalar", lambda e, s, b=b: e.activation(out=sqs[:], in_=b[:], func=AF.Square))
        p.op("scalar", lambda e, s, b=b: e.activation(out=bfs[:], in_=b[:], func=AF.Copy), tag=True)
        p.w("tensor", "scalar")
        for half in range(2):
            p.op("tensor", lambda e, s, do=do, half=half: e.matmul(
                (pA, pB)[half][:, 0:512], ones[:], bfs[:, half * 512:half * 512 + 512],
                start=(do == 0), stop=(do == NDC - 1)))
            p.op("tensor", lambda e, s, do=do, half=half: e.matmul(
                (pC, pD_)[half][:, 0:512], ones[:], sqs[:, half * 512:half * 512 + 512],
                start=(do == 0), stop=(do == NDC - 1)),
                tag=(do == NDC - 1 and half == 1))

    emit_ln_finish(p, (pA, pB), (pC, pD_), mb, rstdb, scr, epsc=epsc)

    p.barrier()
    xln2 = gtok
    for j in range(NDC):
        b = (xs0, xs1)[j % 2]
        if j >= 2:
            p.w("sync", "vector")
        p.dmao("sync", b[:], chunk(x1_out, j))
        p.w("vector", "sync", dma=True)
        p.w("vector", "scalar")
        p.op("vector", lambda e, s, b=b: e.tensor_tensor(
            out=scr[:], in0=b[:], in1=mb[:], op=ALU.subtract))
        p.op("vector", lambda e, s, j=j, xln2=xln2: e.tensor_tensor(
            out=xln2[:, j * T:(j + 1) * T], in0=scr[:], in1=rstdb[:], op=ALU.mult), tag=True)
        p.w("gpsimd", "vector")
        p.dmao("gpsimd", chunk(xln2_out, j), xln2[:, j * T:(j + 1) * T])

    p.w("tensor", "vector")
    for tt in range(NTC):
        psx = (pE, pF)[tt % 2]
        if tt >= 2:
            p.w("tensor", "scalar")
        for dj in range(NDC):
            p.op("tensor", lambda e, s, psx=psx, tt=tt, dj=dj, xln2=xln2: e.matmul(
                psx[:, 0:16], xln2[:, dj * T + tt * 128: dj * T + (tt + 1) * 128],
                rwt[:, dj * 16:(dj + 1) * 16], start=(dj == 0), stop=False))
        p.op("tensor", lambda e, s, psx=psx: e.matmul(
            psx[:, 0:16], ones[0:1, :], rbt[:], start=False, stop=True), tag=True)
        p.w("vector", "tensor")
        p.w("vector", "gpsimd", dma=True)
        p.op("vector", lambda e, s, psx=psx: e.tensor_reduce(
            out=col0[:], in_=psx[:, 0:16], axis=AXX, op=ALU.max, negate=True), tag=True)
        p.w("scalar", "vector")
        p.w("scalar", "gpsimd", dma=True)
        p.op("scalar", lambda e, s, psx=psx: e.activation(
            out=prb[:], in_=psx[:, 0:16], func=AF.Exp, bias=col0[:, 0:1]), tag=True)
        p.w("vector", "scalar")
        p.op("vector", lambda e, s: e.tensor_reduce(
            out=col1[:], in_=prb[:], axis=AXX, op=ALU.add), tag=True)
        p.w("vector", "vector")
        p.op("vector", lambda e, s: e.reciprocal(out=col2[:], in_=col1[:]), tag=True)
        p.w("vector", "vector")
        p.op("vector", lambda e, s: e.tensor_scalar(
            out=prb[:], in0=prb[:], scalar1=col2[:, 0:1], scalar2=None, op0=ALU.mult),
            tag=True)
        p.w("gpsimd", "vector")
        p.dmao("gpsimd", p1_out[tt * 128:(tt + 1) * 128, :], prb[:])

    p.wv("sync", "sync_dma", p.dma["sync"])
    p.wv("sync", "gpsimd_dma", p.dma["gpsimd"])
    p.wv("gpsimd", "gpsimd_dma", p.dma["gpsimd"])

    p.build(nc, ctx)
    ctx.close()
    return nc






def emit_bisect_iter(p, it, pG_, sel8f, Tcol, Tb, pbis, junk, cnt, cntb, sel8t):
    step = 0.5 / (2.0 ** (it + 1))
    p.w("tensor", "vector")
    p.op("tensor", lambda e, s: e.matmul(
        pG_[:, 0:1], sel8f[:], Tcol[:], start=True, stop=True), tag=True)
    p.w("vector", "tensor")
    p.op("vector", lambda e, s: e.tensor_copy(out=Tb[:], in_=pG_[:, 0:1]), tag=True)
    p.w("vector", "vector")
    p.op("vector", lambda e, s: e.tensor_scalar(
        out=junk[:], in0=pbis[:], scalar1=Tb[:, 0:1], scalar2=None, op0=ALU.is_gt),
        tag=True)
    p.w("vector", "vector")
    p.op("vector", lambda e, s: e.tensor_reduce(
        out=cnt[:], in_=junk[:], axis=AXX, op=ALU.add), tag=True)
    p.w("vector", "vector")
    p.op("vector", lambda e, s: e.tensor_copy(out=cntb[:], in_=cnt[:]), tag=True)
    p.w("tensor", "vector")
    p.op("tensor", lambda e, s: e.matmul(
        pG_[0:16, 4:5], sel8t[:], cntb[:], start=True, stop=True), tag=True)
    p.w("vector", "tensor")
    p.op("vector", lambda e, s: e.tensor_scalar(
        out=cnt[0:16, 0:1], in0=pG_[0:16, 4:5], scalar1=512.0, scalar2=None,
        op0=ALU.is_gt), tag=True)
    p.w("vector", "vector")
    p.op("vector", lambda e, s, step=step: e.scalar_tensor_tensor(
        out=Tcol[:], in0=cnt[0:16, 0:1], scalar=2.0 * step, in1=Tcol[:],
        op0=ALU.mult, op1=ALU.add), tag=True)
    p.w("vector", "vector")
    p.op("vector", lambda e, s, step=step: e.tensor_scalar(
        out=Tcol[:], in0=Tcol[:], scalar1=step, scalar2=None, op0=ALU.subtract), tag=True)
    p.w("vector", "vector")



def build_L2(dbg=False):
    nc = bass.Bass("TRN2", target_bir_lowering=False)
    x1_in = nc.dram_tensor("x1i", [H, T], F32, kind="ExternalInput").ap()
    xln2_in = nc.dram_tensor("xln2i", [H, T], BF16, kind="ExternalInput").ap()
    pbis_in = nc.dram_tensor("pbis", [128, 1024], F32, kind="ExternalInput").ap()
    pmye_in = nc.dram_tensor("pmye", [16, T], F32, kind="ExternalInput").ap()
    we_p = nc.dram_tensor("we_p", [NDC * 128, H], F32, kind="ExternalInput").ap()
    web_in = nc.dram_tensor("web", [128, NDC], F32, kind="ExternalInput").ap()
    wqt_p = nc.dram_tensor("wqt_p", [NDC * 128, H], F32, kind="ExternalInput").ap()
    wkt_p = nc.dram_tensor("wkt_p", [4 * 128, H], F32, kind="ExternalInput").ap()
    wvt_in = nc.dram_tensor("wvt", [128, NDC * 512], F32, kind="ExternalInput").ap()
    wot_p = nc.dram_tensor("wot_p", [NDC * 128, H], F32, kind="ExternalInput").ap()
    rw_in = nc.dram_tensor("rw2", [128, NDC * 16], F32, kind="ExternalInput").ap()
    rbias = nc.dram_tensor("rb2", [1, 16], F32, kind="ExternalInput").ap()
    cosin = nc.dram_tensor("cosin", [128, T], F32, kind="ExternalInput").ap()
    sinin = nc.dram_tensor("sinin", [128, T], F32, kind="ExternalInput").ap()
    swapin = nc.dram_tensor("swapin", [128, 128], BF16, kind="ExternalInput").ap()
    qgidx = nc.dram_tensor("qgidx", [1, T], F32, kind="ExternalInput").ap()
    kgidx = nc.dram_tensor("kgidx", [128, 16], F32, kind="ExternalInput").ap()
    identc = nc.dram_tensor("ident2", [128, 128], BF16, kind="ExternalInput").ap()
    onesc = nc.dram_tensor("ones2", [128, 128], BF16, kind="ExternalInput").ap()
    onesfc = nc.dram_tensor("onesf2", [128, 128], F32, kind="ExternalInput").ap()
    sel8fc = nc.dram_tensor("sel8f", [16, 128], F32, kind="ExternalInput").ap()
    sel8tc = nc.dram_tensor("sel8t", [128, 16], BF16, kind="ExternalInput").ap()
    exidx = nc.dram_tensor("exidx2", [128, 8], U32, kind="ExternalInput").ap()
    exgidx = nc.dram_tensor("exgidx2", [128, 16], U32, kind="ExternalInput").ap()
    exfidx = nc.dram_tensor("exfidx2", [2, 1], U32, kind="ExternalInput").ap()
    noncein = nc.dram_tensor("noncef2", [2, 16], BF16, kind="ExternalInput").ap()
    noncei_in = nc.dram_tensor("noncei2", [1, 1], I32, kind="ExternalInput").ap()

    x3_out = nc.dram_tensor("x3", [H, T], F32, kind="ExternalOutput").ap()
    xln4_out = nc.dram_tensor("xln4", [H, T], BF16, kind="ExternalOutput").ap()
    p2_out = nc.dram_tensor("p2", [T, 16], F32, kind="ExternalOutput").ap()

    x2d = nc.dram_tensor("x2d", [H, T], F32, kind="Internal").ap()
    if dbg:
        d_sbc = nc.dram_tensor("d_sbc", [128, T], F32, kind="ExternalOutput").ap()
        d_x2 = nc.dram_tensor("d_x2", [H, T], F32, kind="ExternalOutput").ap()
        d_ob = nc.dram_tensor("d_ob", [128, NDC * T], BF16, kind="ExternalOutput").ap()
        d_kb = nc.dram_tensor("d_kb", [128, 4 * 2048], BF16, kind="ExternalOutput").ap()
        d_vb = nc.dram_tensor("d_vb", [128, 16 * 512], BF16, kind="ExternalOutput").ap()
    exch = nc.dram_tensor("exch_l2", [2050, 2048], BF16, kind="Internal",
                          addr_space="Shared").ap()

    p = Prog()
    ctx = ExitStack()

    def sb(name, shape, dt):
        return ctx.enter_context(nc.sbuf_tensor("sb_" + name, shape, dt))

    def ps_(name, shape, dt):
        return ctx.enter_context(nc.psum_tensor("ps_" + name, shape, dt))

    xbig = sb("xbig", [128, NDC * T], BF16)
    ebuf = sb("ebuf", [128, NDC * T], BF16)
    kbuf = sb("kbuf", [128, 4 * 2048], BF16)
    vbuf = sb("vbuf", [128, 16 * 512], BF16)
    qrope = sb("qrope", [128, 1024], BF16)
    xs0 = sb("xs0", [128, T], F32)
    xs1 = sb("xs1", [128, T], F32)
    sqs = sb("sqs", [128, T], BF16)
    bfs = sb("bfs", [128, T], BF16)
    mb = sb("mb", [128, T], F32)
    rstdb = sb("rstdb", [128, T], F32)
    scr = sb("scr", [128, T], F32)
    sbc = sb("sbc", [128, T], F32)
    wt0 = sb("wt0", [128, H], BF16)
    wt1 = sb("wt1", [128, H], BF16)
    wvtb = sb("wvtb", [128, NDC * 512], BF16)
    webb = sb("webb", [128, NDC], F32)
    pbis = sb("pbis", [128, 1024], F32)
    pmye = sb("pmye", [16, T], F32)
    pmyeb = sb("pmyeb", [16, T], BF16)
    Tcol = sb("Tcol", [16, 1], F32)
    Tb = sb("Tb", [128, 1], F32)
    stp = sb("stp", [16, 1], F32)
    cnt = sb("cnt", [128, 1], F32)
    cntb = sb("cntb", [128, 1], BF16)
    cnte = sb("cnte", [16, 1], F32)
    sgn = sb("sgn", [16, 1], F32)
    srow = sb("srow", [1, T], F32)
    cfull = sb("cfull", [128, T], F32)
    sfull = sb("sfull", [128, T], F32)
    swapc = sb("swapc", [128, 128], BF16)
    qgx = sb("qgx", [1, T], F32)
    kgx = sb("kgx", [128, 16], F32)
    qgb = sb("qgb", [128, 512], F32)
    masks = sb("masks", [128, 16 * 512], BF16)
    pts2 = sb("pts2", [128, 1024], BF16)
    recro = sb("recro", [1, 512], F32)
    recb = sb("recb", [128, 512], F32)
    ident = sb("ident", [128, 128], BF16)
    ones = sb("ones", [128, 128], BF16)
    onesf = sb("onesf", [128, 128], F32)
    sel8f = sb("sel8f", [16, 128], F32)
    sel8t = sb("sel8t", [128, 16], BF16)
    junk = sb("junk", [128, 1024], BF16)
    exi = sb("exi", [128, 8], U32)
    exg = sb("exg", [128, 16], U32)
    exf = sb("exf", [2, 1], U32)
    flagst = sb("flagst", [2, 16], BF16)
    noncei = sb("noncei", [1, 1], I32)
    epsc = sb("epsc", [128, 1], F32)
    col0 = sb("col0", [128, 1], F32)
    col1 = sb("col1", [128, 1], F32)
    col2 = sb("col2", [128, 1], F32)
    prb = sb("prb", [128, 16], F32)
    rwt = sb("rwt", [128, NDC * 16], BF16)
    rbt = sb("rbt", [1, 16], BF16)

    pA = ps_("pA", [128, 512], F32)
    pB = ps_("pB", [128, 512], F32)
    pC = ps_("pC", [128, 512], F32)
    pD_ = ps_("pD", [128, 512], F32)
    pE = ps_("pE", [128, 512], F32)
    pF = ps_("pF", [128, 512], F32)
    pG_ = ps_("pG", [128, 512], F32)
    pH_ = ps_("pH", [128, 512], F32)

    for dst, srcd in [(pbis[:], pbis_in), (pmye[:], pmye_in), (cfull[:], cosin),
                      (sfull[:], sinin), (swapc[:], swapin), (qgx[:], qgidx), (kgx[:], kgidx),
                      (exf[:], exfidx), (flagst[:], noncein),
                      (noncei[:], noncei_in), (webb[:], web_in),
                      (ident[:], identc), (ones[:], onesc), (onesf[:], onesfc),
                      (sel8f[:], sel8fc), (sel8t[:], sel8tc)]:
        p.dmao("sync", dst, srcd)
    p.dmao("sync", exi[:], exidx)
    p.dmao("sync", exg[:], exgidx)
    p.dmao("gpsimd", rwt[:], rw_in)
    p.dmao("gpsimd", rbt[:], rbias)
    p.dmao("gpsimd", wvtb[:], wvt_in)
    for j in range(NDC):
        p.dmao("sync", xbig[:, j * T:(j + 1) * T], chunk(xln2_in, j))
    p.op("vector", lambda e, s: e.memset(epsc[:], EPS))
    p.op("vector", lambda e, s: e.memset(Tcol[:], 0.5))
    p.op("vector", lambda e, s: e.memset(stp[:], 0.25))
    p.w("vector", "sync", dma=True)
    p.op("vector", lambda e, s: e.tensor_copy(out=pmyeb[:], in_=pmye[:]), tag=True)

    p.w("tensor", "sync", dma=True)
    p.w("vector", "sync", dma=True)
    for do in range(NDC):
        wt = (wt0, wt1)[do % 2]
        if do >= 2:
            p.w("gpsimd", "tensor")
        p.dmao("gpsimd", wt[:], chunk(we_p, do))
        p.w("tensor", "gpsimd", dma=True)
        for half in range(2):
            psx = (pA, pB)[half]
            p.w("tensor", "scalar")
            for dj in range(NDC):
                p.op("tensor", lambda e, s, psx=psx, wt=wt, dj=dj, half=half: e.matmul(
                    psx[:, 0:512], wt[:, dj * 128:(dj + 1) * 128],
                    xbig[:, dj * T + half * 512: dj * T + half * 512 + 512],
                    start=(dj == 0), stop=(dj == NDC - 1)), tag=(dj == NDC - 1))
            p.w("scalar", "tensor")
            p.op("scalar", lambda e, s, psx=psx, do=do, half=half: e.activation(
                out=ebuf[:, do * T + half * 512: do * T + half * 512 + 512],
                in_=psx[:, 0:512], func=AF.Copy), tag=True)
            it = do * 2 + half
            if it < NBIS:
                emit_bisect_iter(p, it, pG_, sel8f, Tcol, Tb, pbis, junk, cnt, cntb, sel8t)


    p.op("vector", lambda e, s: e.tensor_scalar(
        out=junk[0:16, 0:1024], in0=pmye[:], scalar1=Tcol[0:16, 0:1], scalar2=None,
        op0=ALU.is_gt), tag=True)
    p.w("vector", "vector")
    p.op("vector", lambda e, s: e.tensor_tensor(
        out=junk[0:16, 0:1024], in0=junk[0:16, 0:1024], in1=pmyeb[:], op=ALU.mult), tag=True)
    p.w("tensor", "vector")
    for half in range(2):
        p.op("tensor", lambda e, s, half=half: e.matmul(
            pG_[0:1, 0:512], ones[0:16, 0:1], junk[0:16, half * 512:half * 512 + 512],
            start=True, stop=True), tag=True)
        p.w("vector", "tensor")
        p.op("vector", lambda e, s, half=half: e.tensor_copy(
            out=srow[0:1, half * 512:half * 512 + 512], in_=pG_[0:1, 0:512]), tag=True)
        p.w("tensor", "vector")
    for half in range(2):
        p.op("tensor", lambda e, s, half=half: e.matmul(
            pH_[:, 0:512], onesf[0:1, :], srow[0:1, half * 512:half * 512 + 512],
            start=True, stop=True), tag=True)
        p.w("vector", "tensor")
        p.op("vector", lambda e, s, half=half: e.tensor_copy(
            out=sbc[:, half * 512:half * 512 + 512], in_=pH_[:, 0:512]), tag=True)
        p.w("tensor", "vector")
    p.barrier()

    for do in range(NDC):
        b = (xs0, xs1)[do % 2]
        if do >= 2:
            p.w("sync", "scalar")
        p.dmao("sync", b[:], chunk(x1_in, do))
        p.w("vector", "sync", dma=True)
        p.op("vector", lambda e, s, do=do: e.scalar_tensor_tensor(
            out=scr[:], in0=ebuf[:, do * T:(do + 1) * T], scalar=webb[:, do:do + 1],
            in1=sbc[:], op0=ALU.add, op1=ALU.mult), tag=True)
        p.w("vector", "vector")
        p.op("vector", lambda e, s, b=b: e.tensor_tensor(
            out=b[:], in0=b[:], in1=scr[:], op=ALU.add), tag=True)
        p.w("sync", "vector")
        p.dmao("sync", chunk(x2d, do), b[:])
        p.w("scalar", "vector")
        if do >= 1:
            p.w("scalar", "tensor")
        p.op("scalar", lambda e, s, b=b: e.activation(out=sqs[:], in_=b[:], func=AF.Square))
        p.op("scalar", lambda e, s, b=b: e.activation(out=bfs[:], in_=b[:], func=AF.Copy), tag=True)
        p.w("tensor", "scalar")
        for half in range(2):
            p.op("tensor", lambda e, s, do=do, half=half: e.matmul(
                (pC, pD_)[half][:, 0:512], ones[:], bfs[:, half * 512:half * 512 + 512],
                start=(do == 0), stop=(do == NDC - 1)))
            p.op("tensor", lambda e, s, do=do, half=half: e.matmul(
                (pE, pF)[half][:, 0:512], ones[:], sqs[:, half * 512:half * 512 + 512],
                start=(do == 0), stop=(do == NDC - 1)),
                tag=(do == NDC - 1 and half == 1))
    emit_ln_finish(p, (pC, pD_), (pE, pF), mb, rstdb, scr, epsc=epsc)
    if dbg:
        p.barrier()
        p.dmao("gpsimd", d_sbc, sbc[:])
        for j in range(NDC):
            p.dmao("gpsimd", chunk(d_x2, j), chunk(x2d, j))
        p.barrier()

    p.barrier()
    for j in range(NDC):
        b = (xs0, xs1)[j % 2]
        if j >= 2:
            p.w("sync", "vector")
        p.dmao("sync", b[:], chunk(x2d, j))
        p.w("vector", "sync", dma=True)
        p.w("vector", "scalar")
        p.op("vector", lambda e, s, b=b: e.tensor_tensor(
            out=scr[:], in0=b[:], in1=mb[:], op=ALU.subtract))
        p.op("vector", lambda e, s, j=j: e.tensor_tensor(
            out=xbig[:, j * T:(j + 1) * T], in0=scr[:], in1=rstdb[:], op=ALU.mult), tag=True)
    p.barrier()

    myoff_col = None
    for g in range(NKV):
        wt = (wt0, wt1)[g % 2]
        if g >= 2:
            p.w("gpsimd", "tensor")
        p.dmao("gpsimd", wt[:], chunk(wkt_p, g))
        p.w("tensor", "gpsimd", dma=True)
        for half in range(2):
            psx = (pA, pB)[half]
            p.w("tensor", "vector")
            for dj in range(NDC):
                p.op("tensor", lambda e, s, psx=psx, wt=wt, dj=dj, half=half: e.matmul(
                    psx[:, 0:512], wt[:, dj * 128:(dj + 1) * 128],
                    xbig[:, dj * T + half * 512: dj * T + half * 512 + 512],
                    start=(dj == 0), stop=(dj == NDC - 1)), tag=(dj == NDC - 1))
        for half in range(2):
            psx = (pA, pB)[half]
            sl = slice(half * 512, (half + 1) * 512)
            p.w("scalar", "tensor")
            p.w("scalar", "vector")
            p.op("scalar", lambda e, s, psx=psx: e.activation(
                out=junk[:, 0:512], in_=psx[:, 0:512], func=AF.Copy), tag=True)
            p.w("tensor", "scalar")
            p.w("tensor", "vector")
            p.op("tensor", lambda e, s: e.matmul(
                pC[:, 0:512], swapc[:], junk[:, 0:512], start=True, stop=True), tag=True)
            p.w("vector", "tensor")
            p.w("vector", "scalar")
            p.op("vector", lambda e, s, sl=sl: e.tensor_tensor(
                out=scr[:, 0:512], in0=junk[:, 0:512], in1=cfull[:, sl], op=ALU.mult), tag=True)
            p.w("vector", "vector")
            p.op("vector", lambda e, s, sl=sl: e.tensor_tensor(
                out=scr[:, 512:1024], in0=pC[:, 0:512], in1=sfull[:, sl], op=ALU.mult), tag=True)
            p.w("vector", "vector")
            p.op("vector", lambda e, s, g=g, sl=sl: e.tensor_tensor(
                out=kbuf[:, g * 2048 + sl.start: g * 2048 + sl.stop],
                in0=scr[:, 0:512], in1=scr[:, 512:1024], op=ALU.add), tag=True)
    p.w("tensor", "gpsimd", dma=True)
    for tt in range(NTC):
        psx = (pG_, pH_)[tt % 2]
        if tt >= 2:
            p.w("tensor", "scalar")
        for dj in range(NDC):
            p.op("tensor", lambda e, s, psx=psx, tt=tt, dj=dj: e.matmul(
                psx[:, 0:512], xbig[:, dj * T + tt * 128: dj * T + (tt + 1) * 128],
                wvtb[:, dj * 512:(dj + 1) * 512], start=(dj == 0), stop=(dj == NDC - 1)),
                tag=(dj == NDC - 1))
        p.w("scalar", "tensor")
        p.op("scalar", lambda e, s, psx=psx, tt=tt: e.activation(
            out=vbuf[:, tt * 512:(tt + 1) * 512], in_=psx[:, 0:512], func=AF.Copy), tag=True)
    p.barrier()

    for g in range(NKV):
        p.scat("gpsimd", exch[:], exi[:, g:g + 1],
               kbuf[:, g * 2048: g * 2048 + 1024])
    for vb in range(4):
        p.scat("gpsimd", exch[:], exi[:, 4 + vb:5 + vb],
               vbuf[:, vb * 1024: (vb + 1) * 1024])
    p.wv("gpsimd", "gpsimd_dma", p.dma["gpsimd"])
    p.scat("gpsimd", exch[:], exf[:, 0:1], flagst[:])
    p.wv("gpsimd", "gpsimd_dma", p.dma["gpsimd"])
    p.spin_flags("gpsimd", [exch[2048:2049, 0:2].bitcast(I32),
                            exch[2049:2050, 0:2].bitcast(I32)], noncei[0:1, 0:1])
    for g in range(NKV):
        for hf in range(2):
            p.gath("gpsimd", kbuf[:, g * 2048 + hf * 1024: g * 2048 + (hf + 1) * 1024],
                   exch[:], exg[:, g * 2 + hf: g * 2 + hf + 1])
    for vb in range(8):
        p.gath("gpsimd", vbuf[:, vb * 1024:(vb + 1) * 1024], exch[:],
               exg[:, 8 + vb: 9 + vb])
    p.barrier()

    NEXT = [12, 16]
    obank_done = {0: (0, 0), 1: (0, 0)}
    obank_rd = 0
    den_done = {0: 0, 1: 0}
    sco_done = {0: 0, 1: 0}
    panel_mark = [0, 0]
    for grp in range(2):
        p.w("tensor", "vector")
        p.op("tensor", lambda e, s, grp=grp: e.matmul(
            pH_[:, 0:512], onesf[0:1, :], qgx[0:1, grp * 512:(grp + 1) * 512],
            start=True, stop=True), tag=True)
        p.w("vector", "tensor")
        p.op("vector", lambda e, s: e.tensor_copy(out=qgb[:], in_=pH_[:, 0:512]), tag=True)
        p.w("vector", "vector")
        for j in range(NEXT[grp]):
            p.op("vector", lambda e, s, j=j: e.tensor_scalar(
                out=masks[:, j * 512:(j + 1) * 512], in0=qgb[:], scalar1=kgx[:, j:j + 1],
                scalar2=None, op0=ALU.is_ge), tag=(j == NEXT[grp] - 1))
        HORD = list(range(NH))
        p.dmao("gpsimd", wt0[:], chunk(wqt_p, HORD[0]))
        for hi, h in enumerate(HORD):
            g = h // GRP
            wt = (wt0, wt1)[hi % 2]
            psq = (pA, pB)[hi % 2]
            pden = (pF, pG_)[hi % 2]
            if hi + 1 < NH:
                p.w_at("gpsimd", "tensor", panel_mark[(hi + 1) % 2])
                p.dmao("gpsimd", wt1[:] if (hi + 1) % 2 else wt0[:], chunk(wqt_p, HORD[hi + 1]))
            p.w("tensor", "gpsimd", dma=True)
            sm, vm = obank_done[hi % 2]
            p.w_at("tensor", "scalar", sm)
            p.w_at("tensor", "vector", vm)
            for dj in range(NDC):
                p.op("tensor", lambda e, s, psq=psq, wt=wt, dj=dj, grp=grp: e.matmul(
                    psq[:, 0:512], wt[:, dj * 128:(dj + 1) * 128],
                    xbig[:, dj * T + grp * 512: dj * T + grp * 512 + 512],
                    start=(dj == 0), stop=(dj == NDC - 1)), tag=(dj == NDC - 1))
            panel_mark[hi % 2] = p.mark("tensor")
            qs_ = qrope[:, (hi % 2) * 512:(hi % 2) * 512 + 512]
            sl = slice(grp * 512, (grp + 1) * 512)
            p.w("scalar", "tensor")
            p.w("scalar", "vector")
            p.op("scalar", lambda e, s, psq=psq: e.activation(
                out=junk[:, 0:512], in_=psq[:, 0:512], func=AF.Copy), tag=True)
            p.w("tensor", "scalar")
            p.w("tensor", "vector")
            p.op("tensor", lambda e, s: e.matmul(
                pH_[:, 0:512], swapc[:], junk[:, 0:512], start=True, stop=True), tag=True)
            p.w("vector", "tensor")
            p.w("vector", "scalar")
            p.op("vector", lambda e, s, sl=sl: e.tensor_tensor(
                out=scr[:, 0:512], in0=junk[:, 0:512], in1=cfull[:, sl], op=ALU.mult), tag=True)
            p.w("vector", "vector")
            p.op("vector", lambda e, s, sl=sl: e.tensor_tensor(
                out=scr[:, 512:1024], in0=pH_[:, 0:512], in1=sfull[:, sl], op=ALU.mult), tag=True)
            p.w("vector", "vector")
            p.op("vector", lambda e, s, qs_=qs_: e.tensor_tensor(
                out=qs_[:], in0=scr[:, 0:512], in1=scr[:, 512:1024], op=ALU.add), tag=True)
            rope_vm = p.mark("vector")
            nj = NEXT[grp]
            mask_vm = {}
            for j in range(nj + 1):
                if j < nj:
                    psS = (pC, pD_)[j % 2]
                    slot = (j % 2) * 512
                    p.w_at("tensor", "scalar", sco_done[j % 2])
                    if j == 0:
                        p.w_at("tensor", "vector", rope_vm)
                    p.op("tensor", lambda e, s, psS=psS, g=g, j=j, qs_=qs_: e.matmul(
                        psS[:, 0:512], kbuf[:, g * 2048 + j * 128: g * 2048 + (j + 1) * 128],
                        qs_[:, 0:512], start=True, stop=True, skip_group_check=True), tag=True)
                    p.w("scalar", "tensor")
                    p.op("scalar", lambda e, s, psS=psS, slot=slot: e.activation(
                        out=pts2[:, slot:slot + 512], in_=psS[:, 0:512], func=AF.Exp,
                        scale=1.0 / 11.313708498984761), tag=True)
                    sco_done[j % 2] = p.mark("scalar")
                    p.w("vector", "scalar")
                    p.op("vector", lambda e, s, j=j, slot=slot: e.tensor_tensor(
                        out=pts2[:, slot:slot + 512], in0=pts2[:, slot:slot + 512],
                        in1=masks[:, j * 512:(j + 1) * 512], op=ALU.mult), tag=True)
                    mask_vm[j] = p.mark("vector")
                if j >= 1:
                    jj = j - 1
                    slot2 = (jj % 2) * 512
                    p.w_at("tensor", "vector", mask_vm[jj])
                    if jj == 0:
                        p.w_at("tensor", "vector", den_done[hi % 2])
                    if jj == 0:
                        p.w_at("tensor", "vector", obank_rd)
                    p.op("tensor", lambda e, s, jj=jj, g=g, grp=grp, slot2=slot2: e.matmul(
                        pE[:, 0:512], vbuf[:, jj * 512 + g * 128: jj * 512 + (g + 1) * 128],
                        pts2[:, slot2:slot2 + 512], start=(jj == 0), stop=(jj == nj - 1),
                        skip_group_check=True))
                    p.op("tensor", lambda e, s, jj=jj, grp=grp, slot2=slot2, pden=pden: e.matmul(
                        pden[0:1, 0:512], ones[:, 0:1], pts2[:, slot2:slot2 + 512],
                        start=(jj == 0), stop=(jj == nj - 1), skip_group_check=True),
                        tag=True)
            p.w("vector", "tensor")
            p.op("vector", lambda e, s, pden=pden: e.reciprocal(out=recro[:], in_=pden[0:1, 0:512]), tag=True)
            den_done[hi % 2] = p.mark("vector")
            p.w("vector", "vector")
            p.w("tensor", "vector")
            p.op("tensor", lambda e, s: e.matmul(
                pH_[:, 0:512], onesf[0:1, :], recro[:], start=True, stop=True), tag=True)
            p.w("vector", "tensor")
            p.op("vector", lambda e, s: e.tensor_copy(out=recb[:], in_=pH_[:, 0:512]), tag=True)
            p.w("vector", "vector")
            p.op("vector", lambda e, s, h=h, grp=grp: e.tensor_tensor(
                out=ebuf[:, h * T + grp * 512: h * T + grp * 512 + 512],
                in0=pE[:, 0:512], in1=recb[:], op=ALU.mult), tag=True)
            obank_rd = p.mark("vector")
            obank_done[hi % 2] = (p.mark("scalar"), p.mark("vector"))
    p.barrier()

    p.barrier()
    if dbg:
        p.dmao("gpsimd", d_ob, ebuf[:])
        p.dmao("gpsimd", d_kb, kbuf[:])
        p.dmao("gpsimd", d_vb, vbuf[:])
        p.barrier()

    wo2_done = {0: 0, 1: 0}
    for do in range(NDC):
        wt = (wt0, wt1)[do % 2]
        if do >= 2:
            p.w("gpsimd", "tensor")
        p.dmao("gpsimd", wt[:], chunk(wot_p, do))
        b = (xs0, xs1)[do % 2]
        if do >= 2:
            p.w("sync", "scalar")
        p.dmao("sync", b[:], chunk(x2d, do))
        p.w("tensor", "gpsimd", dma=True)
        for half in range(2):
            psx = (pA, pB)[half]
            p.w_at("tensor", "vector", wo2_done[half])
            for dj in range(NDC):
                p.op("tensor", lambda e, s, psx=psx, wt=wt, dj=dj, half=half: e.matmul(
                    psx[:, 0:512], wt[:, dj * 128:(dj + 1) * 128],
                    ebuf[:, dj * T + half * 512: dj * T + half * 512 + 512],
                    start=(dj == 0), stop=(dj == NDC - 1)), tag=(dj == NDC - 1))
        p.w("vector", "tensor")
        p.w("vector", "sync", dma=True)
        for half in range(2):
            psx = (pA, pB)[half]
            sl = slice(half * 512, (half + 1) * 512)
            p.op("vector", lambda e, s, psx=psx, b=b, sl=sl: e.tensor_tensor(
                out=b[:, sl], in0=b[:, sl], in1=psx[:, 0:512], op=ALU.add), tag=True)
            wo2_done[half] = p.mark("vector")
        p.w("sync", "vector")
        p.dmao("sync", chunk(x3_out, do), b[:])
        p.w("scalar", "vector")
        if do >= 1:
            p.w("scalar", "tensor")
        p.op("scalar", lambda e, s, b=b: e.activation(out=sqs[:], in_=b[:], func=AF.Square))
        p.op("scalar", lambda e, s, b=b: e.activation(out=bfs[:], in_=b[:], func=AF.Copy), tag=True)
        p.w("tensor", "scalar")
        for half in range(2):
            p.op("tensor", lambda e, s, do=do, half=half: e.matmul(
                (pC, pD_)[half][:, 0:512], ones[:], bfs[:, half * 512:half * 512 + 512],
                start=(do == 0), stop=(do == NDC - 1)))
            p.op("tensor", lambda e, s, do=do, half=half: e.matmul(
                (pE, pF)[half][:, 0:512], ones[:], sqs[:, half * 512:half * 512 + 512],
                start=(do == 0), stop=(do == NDC - 1)),
                tag=(do == NDC - 1 and half == 1))
    emit_ln_finish(p, (pC, pD_), (pE, pF), mb, rstdb, scr, epsc=epsc)
    p.barrier()

    xln4 = ebuf
    for j in range(NDC):
        b = (xs0, xs1)[j % 2]
        if j >= 2:
            p.w("sync", "vector")
        p.dmao("sync", b[:], chunk(x3_out, j))
        p.w("vector", "sync", dma=True)
        p.w("vector", "scalar")
        p.op("vector", lambda e, s, b=b: e.tensor_tensor(
            out=scr[:], in0=b[:], in1=mb[:], op=ALU.subtract))
        p.op("vector", lambda e, s, j=j, xln4=xln4: e.tensor_tensor(
            out=xln4[:, j * T:(j + 1) * T], in0=scr[:], in1=rstdb[:], op=ALU.mult), tag=True)
        p.w("gpsimd", "vector")
        p.dmao("gpsimd", chunk(xln4_out, j), xln4[:, j * T:(j + 1) * T])

    p.w("tensor", "vector")
    for tt in range(NTC):
        psx = (pG_, pH_)[tt % 2]
        if tt >= 2:
            p.w("tensor", "scalar")
        for dj in range(NDC):
            p.op("tensor", lambda e, s, psx=psx, tt=tt, dj=dj, xln4=xln4: e.matmul(
                psx[:, 0:16], xln4[:, dj * T + tt * 128: dj * T + (tt + 1) * 128],
                rwt[:, dj * 16:(dj + 1) * 16], start=(dj == 0), stop=False))
        p.op("tensor", lambda e, s, psx=psx: e.matmul(
            psx[:, 0:16], ones[0:1, :], rbt[:], start=False, stop=True), tag=True)
        p.w("vector", "tensor")
        p.w("vector", "gpsimd", dma=True)
        p.op("vector", lambda e, s, psx=psx: e.tensor_reduce(
            out=col0[:], in_=psx[:, 0:16], axis=AXX, op=ALU.max, negate=True), tag=True)
        p.w("scalar", "vector")
        p.w("scalar", "gpsimd", dma=True)
        p.op("scalar", lambda e, s, psx=psx: e.activation(
            out=prb[:], in_=psx[:, 0:16], func=AF.Exp, bias=col0[:, 0:1]), tag=True)
        p.w("vector", "scalar")
        p.op("vector", lambda e, s: e.tensor_reduce(
            out=col1[:], in_=prb[:], axis=AXX, op=ALU.add), tag=True)
        p.w("vector", "vector")
        p.op("vector", lambda e, s: e.reciprocal(out=col2[:], in_=col1[:]), tag=True)
        p.w("vector", "vector")
        p.op("vector", lambda e, s: e.tensor_scalar(
            out=prb[:], in0=prb[:], scalar1=col2[:, 0:1], scalar2=None, op0=ALU.mult), tag=True)
        p.w("gpsimd", "vector")
        p.dmao("gpsimd", p2_out[tt * 128:(tt + 1) * 128, :], prb[:])

    p.wv("sync", "sync_dma", p.dma["sync"])
    p.wv("sync", "gpsimd_dma", p.dma["gpsimd"])
    p.wv("gpsimd", "gpsimd_dma", p.dma["gpsimd"])
    p.build(nc, ctx)
    ctx.close()
    return nc




def build_L3():
    nc = bass.Bass("TRN2", target_bir_lowering=False)
    x3_in = nc.dram_tensor("x3i", [H, T], F32, kind="ExternalInput").ap()
    xln4_in = nc.dram_tensor("xln4i", [H, T], BF16, kind="ExternalInput").ap()
    pbis_in = nc.dram_tensor("pbis3", [128, 1024], F32, kind="ExternalInput").ap()
    pmye_in = nc.dram_tensor("pmye3", [16, T], F32, kind="ExternalInput").ap()
    we_p = nc.dram_tensor("we_p3", [NDC * 128, H], F32, kind="ExternalInput").ap()
    web_in = nc.dram_tensor("web3", [128, NDC], F32, kind="ExternalInput").ap()
    onesc = nc.dram_tensor("ones3", [128, 128], BF16, kind="ExternalInput").ap()
    onesfc = nc.dram_tensor("onesf3", [128, 128], F32, kind="ExternalInput").ap()
    sel8fc = nc.dram_tensor("sel8f3", [16, 128], F32, kind="ExternalInput").ap()
    sel8tc = nc.dram_tensor("sel8t3", [128, 16], BF16, kind="ExternalInput").ap()
    out_o = nc.dram_tensor("outx", [H, T], F32, kind="ExternalOutput").ap()

    p = Prog()
    ctx = ExitStack()

    def sb(name, shape, dt):
        return ctx.enter_context(nc.sbuf_tensor("sb_" + name, shape, dt))

    def ps_(name, shape, dt):
        return ctx.enter_context(nc.psum_tensor("ps_" + name, shape, dt))

    xln4 = sb("xln4", [128, NDC * T], BF16)
    ebuf = sb("ebuf", [128, NDC * T], BF16)
    xs0 = sb("xs0", [128, T], F32)
    xs1 = sb("xs1", [128, T], F32)
    scr = sb("scr", [128, T], F32)
    sbc = sb("sbc", [128, T], F32)
    wt0 = sb("wt0", [128, H], BF16)
    wt1 = sb("wt1", [128, H], BF16)
    webb = sb("webb", [128, NDC], F32)
    pbis = sb("pbis", [128, 1024], F32)
    pmye = sb("pmye", [16, T], F32)
    pmyeb = sb("pmyeb", [16, T], BF16)
    Tcol = sb("Tcol", [16, 1], F32)
    Tb = sb("Tb", [128, 1], F32)
    stp = sb("stp", [16, 1], F32)
    cnt = sb("cnt", [128, 1], F32)
    cntb = sb("cntb", [128, 1], BF16)
    cnte = sb("cnte", [16, 1], F32)
    sgn = sb("sgn", [16, 1], F32)
    srow = sb("srow", [1, T], F32)
    ones = sb("ones", [128, 128], BF16)
    onesf = sb("onesf", [128, 128], F32)
    sel8f = sb("sel8f", [16, 128], F32)
    sel8t = sb("sel8t", [128, 16], BF16)
    junk = sb("junk", [128, 1024], BF16)

    pA = ps_("pA", [128, 512], F32)
    pB = ps_("pB", [128, 512], F32)
    pG_ = ps_("pG", [128, 512], F32)
    pH_ = ps_("pH", [128, 512], F32)

    for dst, srcd in [(pbis[:], pbis_in), (pmye[:], pmye_in), (webb[:], web_in),
                      (ones[:], onesc), (onesf[:], onesfc), (sel8f[:], sel8fc),
                      (sel8t[:], sel8tc)]:
        p.dmao("sync", dst, srcd)
    for j in range(NDC):
        p.dmao("sync", xln4[:, j * T:(j + 1) * T], chunk(xln4_in, j))
    p.w("vector", "sync", dma=True)
    p.op("vector", lambda e, s: e.memset(Tcol[:], 0.5))
    p.op("vector", lambda e, s: e.memset(stp[:], 0.25))
    p.op("vector", lambda e, s: e.tensor_copy(out=pmyeb[:], in_=pmye[:]), tag=True)

    p.w("tensor", "sync", dma=True)
    for do in range(NDC):
        wt = (wt0, wt1)[do % 2]
        if do >= 2:
            p.w("gpsimd", "tensor")
        p.dmao("gpsimd", wt[:], chunk(we_p, do))
        p.w("tensor", "gpsimd", dma=True)
        for half in range(2):
            psx = (pA, pB)[half]
            p.w("tensor", "scalar")
            for dj in range(NDC):
                p.op("tensor", lambda e, s, psx=psx, wt=wt, dj=dj, half=half: e.matmul(
                    psx[:, 0:512], wt[:, dj * 128:(dj + 1) * 128],
                    xln4[:, dj * T + half * 512: dj * T + half * 512 + 512],
                    start=(dj == 0), stop=(dj == NDC - 1)), tag=(dj == NDC - 1))
            p.w("scalar", "tensor")
            p.op("scalar", lambda e, s, psx=psx, do=do, half=half: e.activation(
                out=ebuf[:, do * T + half * 512: do * T + half * 512 + 512],
                in_=psx[:, 0:512], func=AF.Copy), tag=True)
            it = do * 2 + half
            if it < NBIS:
                emit_bisect_iter(p, it, pG_, sel8f, Tcol, Tb, pbis, junk, cnt, cntb, sel8t)


    p.op("vector", lambda e, s: e.tensor_scalar(
        out=junk[0:16, 0:1024], in0=pmye[:], scalar1=Tcol[0:16, 0:1], scalar2=None,
        op0=ALU.is_gt), tag=True)
    p.w("vector", "vector")
    p.op("vector", lambda e, s: e.tensor_tensor(
        out=junk[0:16, 0:1024], in0=junk[0:16, 0:1024], in1=pmyeb[:], op=ALU.mult), tag=True)
    p.w("tensor", "vector")
    for half in range(2):
        p.op("tensor", lambda e, s, half=half: e.matmul(
            pG_[0:1, 0:512], ones[0:16, 0:1], junk[0:16, half * 512:half * 512 + 512],
            start=True, stop=True), tag=True)
        p.w("vector", "tensor")
        p.op("vector", lambda e, s, half=half: e.tensor_copy(
            out=srow[0:1, half * 512:half * 512 + 512], in_=pG_[0:1, 0:512]), tag=True)
        p.w("tensor", "vector")
    for half in range(2):
        p.op("tensor", lambda e, s, half=half: e.matmul(
            pH_[:, 0:512], onesf[0:1, :], srow[0:1, half * 512:half * 512 + 512],
            start=True, stop=True), tag=True)
        p.w("vector", "tensor")
        p.op("vector", lambda e, s, half=half: e.tensor_copy(
            out=sbc[:, half * 512:half * 512 + 512], in_=pH_[:, 0:512]), tag=True)
        p.w("tensor", "vector")
    p.barrier()

    for do in range(NDC):
        b = (xs0, xs1)[do % 2]
        if do >= 2:
            p.w("sync", "sync", dma=True)
        p.dmao("sync", b[:], chunk(x3_in, do))
        p.w("vector", "sync", dma=True)
        p.op("vector", lambda e, s, do=do: e.scalar_tensor_tensor(
            out=scr[:], in0=ebuf[:, do * T:(do + 1) * T], scalar=webb[:, do:do + 1],
            in1=sbc[:], op0=ALU.add, op1=ALU.mult), tag=True)
        p.w("vector", "vector")
        p.op("vector", lambda e, s, b=b: e.tensor_tensor(
            out=b[:], in0=b[:], in1=scr[:], op=ALU.add), tag=True)
        p.w("sync", "vector")
        p.dmao("sync", chunk(out_o, do), b[:])
    p.wv("sync", "sync_dma", p.dma["sync"])
    p.wv("gpsimd", "gpsimd_dma", p.dma["gpsimd"])
    p.build(nc, ctx)
    ctx.close()
    return nc



def _panels(W):
    Wr = W.reshape(NDC, 128, NDC, 128)
    pan = Wr.transpose(2, 1, 0, 3)
    return np.ascontiguousarray(pan.reshape(NDC * 128, H))


_PERM = np.concatenate([np.arange(0, 128, 2), np.arange(1, 128, 2)])


def _l1_inmaps(inputs, nonce_bf, nonce_i):
    x = inputs["L0_attn_a_x"]
    wq_p = _panels(np.ascontiguousarray(inputs["L0_attn_a_Wq"]))
    wf_p = _panels(np.ascontiguousarray(inputs["L0_attn_a_Wf"]))
    wi_p = _panels(np.ascontiguousarray(inputs["L0_attn_a_Wi"]))
    wo_p = _panels(np.ascontiguousarray(inputs["L0_attn_a_Wo"]))
    gw = np.ascontiguousarray(inputs["L0_attn_a_g_weight"].reshape(NDC, 128).T)
    rw = np.ascontiguousarray(inputs["L0_ffn_router_w"].reshape(NDC, 128, 16)
                              .transpose(1, 0, 2).reshape(128, NDC * 16))
    rb = inputs["L0_ffn_router_b"].reshape(1, 16)
    u512 = np.ascontiguousarray(
        np.tile(np.triu(np.ones((64, 64), dtype=np.float32)), (2, 8))).astype(BF)
    ident = np.eye(128, dtype=np.float32).astype(BF)
    ones = np.ones((128, 128), dtype=np.float32).astype(BF)
    maps = []
    for c in range(8):
        b, h = divmod(c, 2)
        xs = np.ascontiguousarray(x[b, h * T:(h + 1) * T, :].T)
        role = float(h)
        maps.append({
            "x_fm": xs.astype(np.float32),
            "wq_p": wq_p, "wf_p": wf_p, "wi_p": wi_p, "wo_p": wo_p,
            "gw": gw.astype(np.float32), "rw": rw.astype(np.float32),
            "rb": rb.astype(np.float32),
            "u512": u512, "ident": ident, "ones128": ones,
            "roleflag": np.full((128, 1), role, dtype=np.float32),
            "exidx": (np.arange(128, dtype=np.uint32) + h * 128).reshape(128, 1),
            "exgidx": (np.arange(128, dtype=np.uint32) + (1 - h) * 128).reshape(128, 1),
            "exfidx": np.array([[256 + h], [256 + h]], dtype=np.uint32),
            "noncef": np.tile(nonce_bf, (2, 1024)),
            "noncei": nonce_i,
        })
    return maps




def _panels_g(W, nout):
    ndo = nout
    Wr = W.reshape(NDC, 128, ndo, 128)
    return np.ascontiguousarray(Wr.transpose(2, 1, 0, 3).reshape(ndo * 128, H))


def _perm_cols(W, nheads):
    Wp = W.copy()
    for h in range(nheads):
        Wp[:, h * 128:(h + 1) * 128] = Wp[:, h * 128:(h + 1) * 128][:, _PERM]
    return Wp


def _rw_layout(rw):
    return np.ascontiguousarray(rw.reshape(NDC, 128, 16).transpose(1, 0, 2)
                                .reshape(128, NDC * 16)).astype(np.float32)


def _l2_inmaps(inputs, res1, P_full, nonce_bf, nonce_i):
    WqT = _perm_cols(np.ascontiguousarray(inputs["L0_attn_b_Wq"].T), NH)
    WkT = _perm_cols(np.ascontiguousarray(inputs["L0_attn_b_Wk"].T), NKV)
    WvT = np.ascontiguousarray(inputs["L0_attn_b_Wv"].T)
    WoT = np.ascontiguousarray(inputs["L0_attn_b_Wo"].T)
    wqt_p = _panels_g(WqT, NDC)
    wkt_p = _panels_g(WkT, 4)
    wot_p = _panels_g(WoT, NDC)
    we_p = _panels_g(np.ascontiguousarray(inputs["L0_ffn_expert_w"]), NDC)
    web = np.ascontiguousarray(inputs["L0_ffn_expert_b"].reshape(NDC, 128).T).astype(np.float32)
    rw = _rw_layout(inputs["L0_ffn_router_w"])
    rb = inputs["L0_ffn_router_b"].reshape(1, 16).astype(np.float32)
    pbis = np.ascontiguousarray(P_full.T.reshape(16, 8, 1024).reshape(128, 1024)).astype(np.float32)
    ident = np.eye(128, dtype=np.float32).astype(BF)
    ones = np.ones((128, 128), dtype=np.float32).astype(BF)
    onesf = np.ones((128, 128), dtype=np.float32)
    sel8f = np.zeros((16, 128), dtype=np.float32)
    for e in range(16):
        sel8f[e, e * 8:(e + 1) * 8] = 1.0
    sel8t = sel8f.T.copy().astype(BF)
    theta = 10000.0
    freqs = 1.0 / theta ** (np.arange(0, 128, 2) / 128.0)
    SWAPM = np.zeros((128, 128), dtype=np.float32)
    for i in range(64):
        SWAPM[64 + i, i] = 1.0
        SWAPM[i, 64 + i] = 1.0
    SWAPM = SWAPM.astype(BF)
    kg = (np.arange(16)[None, :] * 128 + np.arange(128)[:, None]).astype(np.float32)
    maps = []
    for c in range(8):
        b, h = divmod(c, 2)
        pos = np.arange(T) + h * T
        ang = pos[None, :] * freqs[:, None]
        exi = np.zeros((128, 8), dtype=np.uint32)
        exg = np.zeros((128, 16), dtype=np.uint32)
        r = np.arange(128, dtype=np.uint32)
        for g in range(4):
            exi[:, g] = h * 1024 + g * 128 + r
        for vb in range(4):
            exi[:, 4 + vb] = h * 1024 + 512 + vb * 128 + r
        for g in range(4):
            for hf in range(2):
                exg[:, g * 2 + hf] = hf * 1024 + g * 128 + r
        for vbp in range(8):
            hf, vbb = divmod(vbp, 4)
            exg[:, 8 + vbp] = hf * 1024 + 512 + vbb * 128 + r
        maps.append({
            "x1i": res1[c]["x1"], "xln2i": res1[c]["xln2"],
            "pbis": pbis, "pmye": np.ascontiguousarray(P_full.T[:, c * T:(c + 1) * T]).astype(np.float32),
            "we_p": we_p, "web": web, "wqt_p": wqt_p, "wkt_p": wkt_p,
            "wvt": np.ascontiguousarray(WvT.reshape(16, 128, 512).transpose(1, 0, 2)
                                         .reshape(128, 16 * 512)).astype(np.float32),
            "wot_p": wot_p,
            "rw2": rw, "rb2": rb,
            "cosin": np.vstack([np.cos(ang), np.cos(ang)]).astype(np.float32),
            "sinin": np.vstack([-np.sin(ang), np.sin(ang)]).astype(np.float32),
            "swapin": SWAPM,
            "qgidx": pos.reshape(1, T).astype(np.float32), "kgidx": kg,
            "ident2": ident, "ones2": ones, "onesf2": onesf,
            "sel8f": sel8f, "sel8t": sel8t,
            "exidx2": exi, "exgidx2": exg,
            "exfidx2": np.array([[2048 + h], [2048 + h]], dtype=np.uint32),
            "noncef2": np.tile(nonce_bf, (2, 8)),
            "noncei2": nonce_i,
        })
    return maps


def _l3_inmaps(inputs, res2, P2_full):
    we_p = _panels_g(np.ascontiguousarray(inputs["L0_ffn_expert_w"]), NDC)
    web = np.ascontiguousarray(inputs["L0_ffn_expert_b"].reshape(NDC, 128).T).astype(np.float32)
    pbis = np.ascontiguousarray(P2_full.T.reshape(16, 8, 1024).reshape(128, 1024)).astype(np.float32)
    ones = np.ones((128, 128), dtype=np.float32).astype(BF)
    onesf = np.ones((128, 128), dtype=np.float32)
    sel8f = np.zeros((16, 128), dtype=np.float32)
    for e in range(16):
        sel8f[e, e * 8:(e + 1) * 8] = 1.0
    sel8t = sel8f.T.copy().astype(BF)
    maps = []
    for c in range(8):
        maps.append({
            "x3i": res2[c]["x3"], "xln4i": res2[c]["xln4"],
            "pbis3": pbis,
            "pmye3": np.ascontiguousarray(P2_full.T[:, c * T:(c + 1) * T]).astype(np.float32),
            "we_p3": we_p, "web3": web, "ones3": ones, "onesf3": onesf,
            "sel8f3": sel8f, "sel8t3": sel8t,
        })
    return maps


_NC_CACHE = {}


def kernel(**inputs):
    inputs = {k: np.asarray(v) for k, v in inputs.items()}
    nv = np.random.randint(1, 2 ** 31 - 1, dtype=np.int32)
    nonce_i = np.array([[nv]], dtype=np.int32)
    nonce_bf = np.frombuffer(nonce_i.tobytes(), dtype=BF).reshape(1, 2).copy()

    if "L1" not in _NC_CACHE:
        _NC_CACHE["L1"] = build_L1()
        _NC_CACHE["L2"] = build_L2()
        _NC_CACHE["L3"] = build_L3()
    res1 = run_bass_kernel_spmd(_NC_CACHE["L1"], _l1_inmaps(inputs, nonce_bf, nonce_i),
                                core_ids=list(range(8))).results
    P_full = np.concatenate([res1[c]["p1"] for c in range(8)], axis=0)
    nv2 = np.array([[np.int32(nv ^ 0x1234567)]], dtype=np.int32)
    nonce_bf2 = np.frombuffer(nv2.tobytes(), dtype=BF).reshape(1, 2).copy()
    res2 = run_bass_kernel_spmd(_NC_CACHE["L2"],
                                _l2_inmaps(inputs, res1, P_full, nonce_bf2, nv2),
                                core_ids=list(range(8))).results
    P2_full = np.concatenate([res2[c]["p2"] for c in range(8)], axis=0)
    res3 = run_bass_kernel_spmd(_NC_CACHE["L3"], _l3_inmaps(inputs, res2, P2_full),
                                core_ids=list(range(8))).results
    out = np.zeros((4, 2048, 2048), dtype=np.float32)
    for c in range(8):
        b, h = divmod(c, 2)
        out[b, h * T:(h + 1) * T, :] = res3[c]["outx"].T
    return out


# version 08
# speedup vs baseline: 1.7457x; 1.0761x over previous
import numpy as np
import ml_dtypes
from contextlib import ExitStack
import concourse.bass as bass
import concourse.mybir as mybir
from concourse.bass import IndirectOffsetOnAxis
from concourse.bass_utils import run_bass_kernel_spmd

F32 = mybir.dt.float32
BF16 = mybir.dt.bfloat16
U32 = mybir.dt.uint32
I32 = mybir.dt.int32
AF = mybir.ActivationFunctionType
ALU = mybir.AluOpType
AXX = mybir.AxisListType.X
BF = ml_dtypes.bfloat16

H = 2048
T = 1024
NH = 16
C = 64
NCH = 16
NDC = 16
NTC = 8
EPS = 1e-5
LNDK = float(np.log(128.0 ** -0.5))
NKV = 4
GRP = 4
NE = 16
NBIS = 30
MAGIC = 0x41FA5A5A
FLAGVAL = np.frombuffer(np.int32(MAGIC).tobytes(), dtype=BF).copy()

ENGINES = ("sync", "gpsimd", "tensor", "scalar", "vector")


class Prog:

    def __init__(self):
        self.ops = {e: [] for e in ENGINES}
        self.tag = {e: 0 for e in ENGINES}
        self.dma = {e: 0 for e in ENGINES}
        self.lastwait = {e: {} for e in ENGINES}

    def op(self, e, fn, tag=False):
        if tag:
            self.tag[e] += 1

            def wrapped(eng, sems, fn=fn, e=e):
                r = fn(eng, sems)
                assert r is not None, "tagged op must return an instruction"
                r.then_inc(sems[e], 1)
            self.ops[e].append(wrapped)
        else:
            self.ops[e].append(fn)

    def dmao(self, e, out, in_):
        self.dma[e] += 16

        def fn(eng, sems, out=out, in_=in_, e=e):
            eng.dma_start(out=out, in_=in_).then_inc(sems[e + "_dma"], 16)
        self.ops[e].append(fn)

    def scat(self, e, out, oidx, in_):
        self.dma[e] += 16

        def fn(eng, sems, out=out, oidx=oidx, in_=in_, e=e):
            eng.indirect_dma_start(
                out=out, out_offset=IndirectOffsetOnAxis(ap=oidx, axis=0),
                in_=in_, in_offset=None).then_inc(sems[e + "_dma"], 16)
        self.ops[e].append(fn)

    def gath(self, e, out, in_, iidx):
        self.dma[e] += 16

        def fn(eng, sems, out=out, in_=in_, iidx=iidx, e=e):
            eng.indirect_dma_start(
                out=out, out_offset=None,
                in_=in_, in_offset=IndirectOffsetOnAxis(ap=iidx, axis=0)).then_inc(sems[e + "_dma"], 16)
        self.ops[e].append(fn)

    def mark(self, prod, dma=False):
        return self.dma[prod] if dma else self.tag[prod]

    def w_at(self, waiter, prod, val, dma=False):
        key = prod + ("_dma" if dma else "")
        if val <= 0 or self.lastwait[waiter].get(key, -1) >= val:
            return
        self.lastwait[waiter][key] = val
        self.ops[waiter].append(lambda eng, sems, key=key, val=val: eng.wait_ge(sems[key], val))

    def w(self, waiter, prod, dma=False):
        key = prod + ("_dma" if dma else "")
        val = self.dma[prod] if dma else self.tag[prod]
        if val <= 0 or self.lastwait[waiter].get(key, -1) >= val:
            return
        self.lastwait[waiter][key] = val
        self.ops[waiter].append(lambda eng, sems, key=key, val=val: eng.wait_ge(sems[key], val))

    def barrier(self):
        for wtr in ENGINES:
            for prod in ENGINES:
                if prod != wtr:
                    self.w(wtr, prod)
                self.w(wtr, prod, dma=True)

    def wv(self, waiter, key, val):
        self.ops[waiter].append(lambda eng, sems, key=key, val=val: eng.wait_ge(sems[key], val))

    def spin_flags(self, e, flag_aps, expect_ap):
        def fn(eng, sems):
            rc = eng.alloc_register(name="r_spin_c")
            ra = eng.alloc_register(name="r_spin_a")
            rexp = eng.alloc_register(name="r_spin_e")
            eng.load(rexp, expect_ap)
            regs = [eng.alloc_register(name=f"r_spin_f{i}") for i in range(len(flag_aps))]
            eng.reg_mov(rc, 1)
            with eng.While(rc):
                for _ in range(200):
                    eng.reg_alu(ra, ra, 1, ALU.add)
                for rf, ap in zip(regs, flag_aps):
                    eng.load(rf, ap)
                eng.reg_mov(rc, 0)
                for rf in regs:
                    eng.reg_alu(ra, rf, rexp, ALU.not_equal)
                    eng.reg_alu(rc, rc, ra, ALU.bitwise_or)
            for r in regs + [ra, rc, rexp]:
                eng.free_register(r)
        self.ops[e].append(fn)

    def build(self, nc, ctx):
        semnames = list(ENGINES) + [e + "_dma" for e in ENGINES]
        sems = {s: ctx.enter_context(nc.semaphore(name="sem_" + s)) for s in semnames}
        blk = ctx.enter_context(nc.Block())

        def body(name):
            def f(eng):
                for fn in self.ops[name]:
                    fn(eng, sems)
            return f
        blk.sync(body("sync"))
        blk.gpsimd(body("gpsimd"))
        blk.tensor(body("tensor"))
        blk.scalar(body("scalar"))
        blk.vector(body("vector"))


def chunk(t, j, w=128):
    return t[j * w:(j + 1) * w, :]


def emit_ln_finish(p, psx, psx2, mb, rstdb, scr, rms_only=False, epsc=None):
    p.w("vector", "tensor")
    for half in range(2):
        sl = slice(half * 512, (half + 1) * 512)
        if rms_only:
            p.op("vector", lambda e, s, half=half, sl=sl: e.tensor_scalar(
                out=rstdb[:, sl], in0=psx2[half][:, 0:512], scalar1=1.0 / H,
                scalar2=None, op0=ALU.mult), tag=(half == 1))
        else:
            p.op("vector", lambda e, s, half=half, sl=sl: e.tensor_scalar(
                out=mb[:, sl], in0=psx[half][:, 0:512], scalar1=1.0 / H,
                scalar2=None, op0=ALU.mult))
            p.op("vector", lambda e, s, half=half, sl=sl: e.tensor_scalar(
                out=rstdb[:, sl], in0=psx2[half][:, 0:512], scalar1=1.0 / H,
                scalar2=None, op0=ALU.mult))
            p.op("vector", lambda e, s, sl=sl: e.scalar_tensor_tensor(
                out=scr[:, sl], in0=mb[:, sl], scalar=-1.0, in1=mb[:, sl],
                op0=ALU.mult, op1=ALU.mult))
            p.op("vector", lambda e, s, sl=sl: e.tensor_tensor(
                out=rstdb[:, sl], in0=rstdb[:, sl], in1=scr[:, sl], op=ALU.add),
                tag=(half == 1))
    p.w("scalar", "vector")
    p.op("scalar", lambda e, s: e.activation(out=rstdb[:], in_=rstdb[:], func=AF.Ln, bias=epsc[:, 0:1]))
    p.op("scalar", lambda e, s: e.activation(out=rstdb[:], in_=rstdb[:], func=AF.Exp, scale=-0.5), tag=True)




def emit_ln_apply(p, nchunks, load_chunk, lanes, mb, rstdb, scratches, out_fn, dma_out_fn=None):
    p.barrier()
    lane_done = [0] * len(lanes)
    for j in range(nchunks):
        ln = j % len(lanes)
        buf, eng = lanes[ln]
        p.w_at("sync", eng, lane_done[ln])
        p.dmao("sync", buf, load_chunk(j))
        p.w(eng, "sync", dma=True)
        p.w(eng, "scalar")
        scr = scratches[eng]
        p.op(eng, lambda e, s, buf=buf, scr=scr: e.tensor_tensor(
            out=scr, in0=buf, in1=mb[:], op=ALU.subtract))
        p.op(eng, lambda e, s, scr=scr, j=j: e.tensor_tensor(
            out=out_fn(j), in0=scr, in1=rstdb[:], op=ALU.mult), tag=True)
        lane_done[ln] = p.mark(eng)
        if dma_out_fn is not None:
            p.w("gpsimd", eng)
            p.dmao("gpsimd", dma_out_fn(j), out_fn(j))
    p.w("vector", "gpsimd")
    p.op("vector", lambda e, s: e.tensor_copy(out=scratches["vector"][:, 0:8],
                                              in_=scratches["vector"][:, 8:16]), tag=True)
    p.w("gpsimd", "vector")


def build_L1(dbg=False):
    nc = bass.Bass("TRN2", target_bir_lowering=False)
    x_in = nc.dram_tensor("x_fm", [H, T], F32, kind="ExternalInput").ap()
    wq_p = nc.dram_tensor("wq_p", [NDC * 128, H], F32, kind="ExternalInput").ap()
    wf_p = nc.dram_tensor("wf_p", [NDC * 128, H], F32, kind="ExternalInput").ap()
    wi_p = nc.dram_tensor("wi_p", [NDC * 128, H], F32, kind="ExternalInput").ap()
    wo_p = nc.dram_tensor("wo_p", [NDC * 128, H], F32, kind="ExternalInput").ap()
    gwt = nc.dram_tensor("gw", [128, NDC], F32, kind="ExternalInput").ap()
    rw_in = nc.dram_tensor("rw", [128, NDC * 16], F32, kind="ExternalInput").ap()
    rbias = nc.dram_tensor("rb", [1, 16], F32, kind="ExternalInput").ap()
    u512c = nc.dram_tensor("u512", [128, 512], BF16, kind="ExternalInput").ap()
    identc = nc.dram_tensor("ident", [128, 128], BF16, kind="ExternalInput").ap()
    onesc = nc.dram_tensor("ones128", [128, 128], BF16, kind="ExternalInput").ap()
    roleflag = nc.dram_tensor("roleflag", [128, 1], F32, kind="ExternalInput").ap()
    exidx = nc.dram_tensor("exidx", [128, 1], U32, kind="ExternalInput").ap()
    exgidx = nc.dram_tensor("exgidx", [128, 1], U32, kind="ExternalInput").ap()
    exfidx = nc.dram_tensor("exfidx", [2, 1], U32, kind="ExternalInput").ap()
    noncein = nc.dram_tensor("noncef", [2, 2048], BF16, kind="ExternalInput").ap()
    noncei_in = nc.dram_tensor("noncei", [1, 1], I32, kind="ExternalInput").ap()

    x1_out = nc.dram_tensor("x1", [H, T], F32, kind="ExternalOutput").ap()
    xln2_out = nc.dram_tensor("xln2", [H, T], BF16, kind="ExternalOutput").ap()
    p1_out = nc.dram_tensor("p1", [T, 16], F32, kind="ExternalOutput").ap()

    exch = nc.dram_tensor("exch_l1", [258, 2048], BF16, kind="Internal",
                          addr_space="Shared").ap()
    if dbg:
        d_e1 = nc.dram_tensor("d_e1", [128, 1024], F32, kind="ExternalOutput").ap()
        d_qi = nc.dram_tensor("d_qi", [128, 1024], BF16, kind="ExternalOutput").ap()
        d_es = nc.dram_tensor("d_es", [128, 16], F32, kind="ExternalOutput").ap()
        d_qb = nc.dram_tensor("d_qb", [128, 128], BF16, kind="ExternalOutput").ap()
        d_qe = nc.dram_tensor("d_qe", [128, 256], BF16, kind="ExternalOutput").ap()
        d_gt = nc.dram_tensor("d_gt", [128, NTC * H], BF16, kind="ExternalOutput").ap()
        d_vt = nc.dram_tensor("d_vt", [128, NTC * H], BF16, kind="ExternalOutput").ap()
        d_ob = nc.dram_tensor("d_ob", [128, NDC * T], BF16, kind="ExternalOutput").ap()
        d_q = nc.dram_tensor("d_q", [128, NDC * T], BF16, kind="ExternalOutput").ap()

    p = Prog()
    ctx = ExitStack()
    def sb(name, shape, dt):
        return ctx.enter_context(nc.sbuf_tensor("sb_" + name, shape, dt))

    def ps_(name, shape, dt):
        return ctx.enter_context(nc.psum_tensor("ps_" + name, shape, dt))

    qbuf = sb("qbuf", [128, NDC * T], BF16)
    gtok = sb("gtok", [128, NTC * H], BF16)
    xv = sb("xv", [128, NDC * T], BF16)
    xs0 = sb("xs0", [128, T], F32)
    xs1 = sb("xs1", [128, T], F32)
    sqs = sb("sqs", [128, T], BF16)
    bfs = sb("bfs", [128, T], BF16)
    mb = sb("mb", [128, T], F32)
    rstdb = sb("rstdb", [128, T], F32)
    scr = sb("scr", [128, T], F32)
    wt0 = sb("wt0", [128, H], BF16)
    wt1 = sb("wt1", [128, H], BF16)
    rwt = sb("rwt", [128, NDC * 16], BF16)
    rbt = sb("rbt", [1, 16], BF16)
    gwb = sb("gwb", [128, NDC], F32)
    e1s = sb("e1s", [128, 1024], F32)
    e2s = sb("e2s", [128, 1024], F32)
    kds = sb("kds", [128, 1024], BF16)
    qis = sb("qis", [128, 1024], BF16)
    pts = sb("pts", [128, 1024], BF16)
    kdts = sb("kdts", [128, 2048], BF16)
    u512 = sb("u512", [128, 512], BF16)
    ident = sb("ident", [128, 128], BF16)
    ones = sb("ones", [128, 128], BF16)
    rolef = sb("rolef", [128, 1], F32)
    exi = sb("exi", [128, 1], U32)
    exg = sb("exg", [128, 1], U32)
    exf = sb("exf", [2, 1], U32)
    flagst = sb("flagst", [2, 2048], BF16)
    noncei = sb("noncei", [1, 1], I32)
    prb = sb("prb", [128, 16], F32)
    col0 = sb("col0", [128, 1], F32)
    col1 = sb("col1", [128, 1], F32)
    epsc = sb("epsc", [128, 1], F32)
    col2 = sb("col2", [128, 1], F32)
    lndkc = sb("lndkc", [128, 1], F32)
    Mst = sb("Mst", [128, NH * 128], F32)
    Mbf = sb("Mbf", [128, NH * 128], BF16)
    Scum = sb("Scum", [128, NH], F32)
    Gtot = sb("Gtot", [128, NH], F32)
    EGtot = sb("EGtot", [128, NH], F32)
    EScum = sb("EScum", [128, NH], F32)

    pA = ps_("pA", [128, 512], F32)
    pB = ps_("pB", [128, 512], F32)
    pC = ps_("pC", [128, 512], F32)
    pD_ = ps_("pD_", [128, 512], F32)
    pE = ps_("pE", [128, 512], F32)
    pF = ps_("pF", [128, 512], F32)
    pS0 = ps_("pS0", [128, 512], F32)
    pT0 = ps_("pT0", [128, 512], BF16)

    phase1 = ExitStack()
    fm1 = phase1.enter_context(nc.sbuf_tensor("fm1", [128, NDC * T], BF16))
    xln = xv
    vtok = xv

    for dst, src in [(u512[:], u512c), (ident[:], identc), (ones[:], onesc),
                     (rolef[:], roleflag), (exi[:], exidx), (exg[:], exgidx),
                     (exf[:], exfidx), (flagst[:], noncein), (gwb[:], gwt),
                     (noncei[:], noncei_in)]:
        p.dmao("sync", dst, src)
    p.dmao("gpsimd", rbt[:], rbias)
    p.dmao("gpsimd", rwt[:], rw_in)

    p.op("vector", lambda e, s: e.memset(epsc[:], EPS))
    p.op("vector", lambda e, s: e.memset(lndkc[:], LNDK), tag=True)

    for j in range(NDC):
        b = (xs0, xs1)[j % 2]
        if j >= 2:
            p.w("sync", "scalar")
        p.dmao("sync", b[:], chunk(x_in, j))
        p.w("scalar", "sync", dma=True)
        if j >= 1:
            p.w("scalar", "tensor")
        p.op("scalar", lambda e, s, b=b: e.activation(out=sqs[:], in_=b[:], func=AF.Square))
        p.op("scalar", lambda e, s, b=b: e.activation(out=bfs[:], in_=b[:], func=AF.Copy), tag=True)
        p.w("tensor", "scalar")
        for half in range(2):
            p.op("tensor", lambda e, s, j=j, half=half: e.matmul(
                (pA, pB)[half][:, 0:512], ones[:], bfs[:, half * 512:half * 512 + 512],
                start=(j == 0), stop=(j == NDC - 1)))
            p.op("tensor", lambda e, s, j=j, half=half: e.matmul(
                (pC, pD_)[half][:, 0:512], ones[:], sqs[:, half * 512:half * 512 + 512],
                start=(j == 0), stop=(j == NDC - 1)),
                tag=(j == NDC - 1 and half == 1))
    emit_ln_finish(p, (pA, pB), (pC, pD_), mb, rstdb, scr, epsc=epsc)

    lanes1 = [(xs0[:], "vector"), (xs1[:], "vector"),
              (wt0[:].bitcast(F32), "gpsimd"), (wt1[:].bitcast(F32), "gpsimd")]
    emit_ln_apply(p, NDC, lambda j: chunk(x_in, j), lanes1, mb, rstdb,
                  {"vector": scr[:], "gpsimd": e1s[:]},
                  lambda j: xln[:, j * T:(j + 1) * T])

    def proj_fm(wpan, evac):
        bankdone = {0: (0, 0), 1: (0, 0)}
        for do in range(NDC):
            wt = (wt0, wt1)[do % 2]
            if do >= 2:
                p.w("gpsimd", "tensor")
            p.dmao("gpsimd", wt[:], chunk(wpan, do))
            p.w("tensor", "gpsimd", dma=True)
            if do == 0:
                p.w("tensor", "vector")
            for half in range(2):
                psx = (pE, pF)[half]
                sm, vm = bankdone[half]
                p.w_at("tensor", "scalar", sm)
                p.w_at("tensor", "vector", vm)
                for dj in range(NDC):
                    p.op("tensor", lambda e, s, psx=psx, wt=wt, dj=dj, half=half: e.matmul(
                        psx[:, 0:512], wt[:, dj * 128:(dj + 1) * 128],
                        xln[:, dj * T + half * 512: dj * T + half * 512 + 512],
                        start=(dj == 0), stop=(dj == NDC - 1)), tag=(dj == NDC - 1))
                evac(p, psx, do, half)
                bankdone[half] = (p.mark("scalar"), p.mark("vector"))

    def evac_silu(p, psx, do, half):
        p.w("scalar", "tensor")
        p.op("scalar", lambda e, s, psx=psx, do=do, half=half: e.activation(
            out=qbuf[:, do * T + half * 512: do * T + half * 512 + 512],
            in_=psx[:, 0:512], func=AF.Silu), tag=True)

    def evac_g(p, psx, do, half):
        sl = slice(half * 512, (half + 1) * 512)
        p.w("scalar", "tensor")
        p.w("scalar", "vector")
        p.op("scalar", lambda e, s, psx=psx, sl=sl: e.activation(
            out=e1s[:, sl], in_=psx[:, 0:512], func=AF.Abs))
        p.op("scalar", lambda e, s, sl=sl: e.activation(
            out=e1s[:, sl], in_=e1s[:, sl], func=AF.Exp, scale=-1.0))
        p.op("scalar", lambda e, s, sl=sl: e.activation(
            out=e1s[:, sl], in_=e1s[:, sl], func=AF.Ln, bias=1.0), tag=True)
        p.w("vector", "scalar")
        p.op("vector", lambda e, s, psx=psx, sl=sl: e.tensor_scalar(
            out=e2s[:, sl], in0=psx[:, 0:512], scalar1=0.0, scalar2=None, op0=ALU.min))
        p.op("vector", lambda e, s, do=do, half=half, sl=sl: e.tensor_tensor(
            out=fm1[:, do * T + half * 512: do * T + half * 512 + 512],
            in0=e2s[:, sl], in1=e1s[:, sl], op=ALU.subtract), tag=True)

    def evac_copy(p, psx, do, half):
        p.w("scalar", "tensor")
        p.op("scalar", lambda e, s, psx=psx, do=do, half=half: e.activation(
            out=fm1[:, do * T + half * 512: do * T + half * 512 + 512],
            in_=psx[:, 0:512], func=AF.Copy), tag=True)

    proj_fm(wq_p, evac_silu)
    p.barrier()
    if dbg:
        p.w("gpsimd", "scalar")
        p.dmao("gpsimd", d_qe, qbuf[:, 15 * T: 15 * T + 256])
        p.w("scalar", "gpsimd", dma=True)
        p.w("vector", "gpsimd", dma=True)
        p.w("tensor", "gpsimd", dma=True)
    proj_fm(wf_p, evac_g)
    p.barrier()

    def transpose_all(dstbuf):
        for k in range(NDC * NTC):
            dj, tt = divmod(k, NTC)
            slot = k % 4
            if k >= 4 and slot == 0:
                p.w("tensor", "vector")
            p.op("tensor", lambda e, s, dj=dj, tt=tt, slot=slot: e.transpose(
                pT0[:, slot * 128:(slot + 1) * 128],
                fm1[:, dj * T + tt * 128: dj * T + (tt + 1) * 128], ident[:]),
                tag=(slot == 3))
            if slot == 3:
                p.w("vector", "tensor")
                base = k - 3
                for kk in range(base, k + 1):
                    djj, ttt = divmod(kk, NTC)
                    sl2 = kk % 4
                    p.op("vector", lambda e, s, djj=djj, ttt=ttt, sl2=sl2: e.tensor_copy(
                        out=dstbuf[:, ttt * H + djj * 128: ttt * H + (djj + 1) * 128],
                        in_=pT0[:, sl2 * 128:(sl2 + 1) * 128]), tag=(sl2 == 3))

    transpose_all(gtok)
    p.barrier()
    if dbg:
        p.w("gpsimd", "vector")
        p.dmao("gpsimd", d_gt, gtok[:])
    proj_fm(wi_p, evac_copy)
    p.barrier()
    transpose_all(vtok)
    if dbg:
        p.w("gpsimd", "vector")
        p.dmao("gpsimd", d_vt, vtok[:])

    phase1.close()
    obuf = ctx.enter_context(nc.sbuf_tensor("obuf", [128, NDC * T], BF16))

    p.op("vector", lambda e, s: e.memset(Mst[:], 0.0))
    p.op("vector", lambda e, s: e.memset(Mbf[:], 0.0))
    p.op("vector", lambda e, s: e.memset(Scum[:], 0.0), tag=True)

    for c in range(NCH):
        if c % 4 == 0:
            p.barrier()
        tt, par = divmod(c, 2)
        par *= 64
        p.w("tensor", "vector")
        p.w("tensor", "scalar")
        for h in range(NH):
            psx = pA if h < 8 else pB
            colb = (h % 8) * 64
            p.op("tensor", lambda e, s, psx=psx, colb=colb, h=h, tt=tt, par=par: e.matmul(
                psx[:, colb:colb + 64],
                gtok[par:par + 64, tt * H + h * 128: tt * H + (h + 1) * 128],
                u512[par:par + 64, 0:64], start=(h % 8 == 0), stop=(h % 8 == 7)),
                tag=(h in (7, 15)))
        p.w("scalar", "tensor")
        p.w("scalar", "vector")
        p.op("scalar", lambda e, s: e.activation(out=e1s[:, 0:512], in_=pA[:, 0:512], func=AF.Exp, bias=lndkc[:, 0:1]))
        p.op("scalar", lambda e, s: e.activation(out=e1s[:, 512:1024], in_=pB[:, 0:512], func=AF.Exp, bias=lndkc[:, 0:1]))
        p.op("scalar", lambda e, s: e.activation(out=e2s[:, 0:512], in_=pA[:, 0:512], func=AF.Exp, scale=-1.0))
        p.op("scalar", lambda e, s: e.activation(out=e2s[:, 512:1024], in_=pB[:, 0:512], func=AF.Exp, scale=-1.0))
        p.op("scalar", lambda e, s: e.activation(out=EScum[:], in_=Scum[:], func=AF.Exp), tag=True)
        p.w("vector", "tensor")
        p.op("vector", lambda e, s: e.tensor_copy(out=Gtot[:, 0:8], in_=pA[:, 63:512:64]))
        p.op("vector", lambda e, s: e.tensor_copy(out=Gtot[:, 8:16], in_=pB[:, 63:512:64]), tag=True)
        p.w("scalar", "vector")
        p.op("scalar", lambda e, s: e.activation(out=EGtot[:], in_=Gtot[:], func=AF.Exp), tag=True)
        p.w("vector", "scalar")
        p.op("vector", lambda e, s: e.tensor_tensor(out=Scum[:], in0=Scum[:], in1=Gtot[:], op=ALU.add))
        p.op("vector", lambda e, s: e.tensor_tensor(
            out=kds[:, 1:1024], in0=e2s[:, 1:1024], in1=e2s[:, 0:1023], op=ALU.subtract))
        p.op("vector", lambda e, s: e.tensor_scalar(
            out=kds[:, 0:1024:64], in0=e2s[:, 0:1024:64], scalar1=-1.0, scalar2=None,
            op0=ALU.add))
        for h in range(NH):
            p.op("vector", lambda e, s, c=c, h=h: e.tensor_tensor(
                out=qis[:, h * 64:(h + 1) * 64],
                in0=qbuf[:, h * T + c * 64: h * T + (c + 1) * 64],
                in1=e1s[:, h * 64:(h + 1) * 64], op=ALU.mult), tag=(h == NH - 1))
        if dbg and c == 0:
            p.w("gpsimd", "vector")
            p.w("gpsimd", "scalar")
            p.dmao("gpsimd", d_e1, e1s[:])
            p.dmao("gpsimd", d_qi, qis[:])
            p.dmao("gpsimd", d_es, EScum[:])
            p.dmao("gpsimd", d_qb, qbuf[:, 15 * T: 15 * T + 128])
            p.w("vector", "gpsimd", dma=True)
        for h in range(NH):
            p.op("vector", lambda e, s, h=h, c=c: e.tensor_scalar(
                out=qbuf[:, h * T + c * 64: h * T + (c + 1) * 64],
                in0=qis[:, h * 64:(h + 1) * 64],
                scalar1=EScum[:, h:h + 1], scalar2=None, op0=ALU.mult),
                tag=(h == NH - 1))
        p.w("tensor", "vector")
        for h in range(NH):
            psx = pS0 if h < 8 else pE
            colb = (h % 8) * 64
            p.op("tensor", lambda e, s, psx=psx, colb=colb, h=h, par=par: e.matmul(
                psx[par:par + 64, colb:colb + 64], kds[:, h * 64:(h + 1) * 64],
                qis[:, h * 64:(h + 1) * 64], start=(h % 8 == 0), stop=(h % 8 == 7),
                tile_position=(0, par)),
                tag=(h in (7, 15)))
        p.w("vector", "tensor")
        p.op("vector", lambda e, s, par=par: e.tensor_tensor(
            out=pts[par:par + 64, 0:512], in0=pS0[par:par + 64, 0:512],
            in1=u512[par:par + 64, :], op=ALU.mult))
        p.op("vector", lambda e, s, par=par: e.tensor_tensor(
            out=pts[par:par + 64, 512:1024], in0=pE[par:par + 64, 0:512],
            in1=u512[par:par + 64, :], op=ALU.mult))
        p.op("vector", lambda e, s: e.tensor_copy(out=Mbf[:], in_=Mst[:]), tag=True)
        p.w("tensor", "vector")
        p.w("tensor", "scalar")
        for h in range(NH):
            psx = pC if h < 8 else pD_
            colb = (h % 8) * 64
            p.op("tensor", lambda e, s, psx=psx, colb=colb, h=h, tt=tt, par=par: e.matmul(
                psx[:, colb:colb + 64],
                vtok[par:par + 64, tt * H + h * 128: tt * H + (h + 1) * 128],
                pts[par:par + 64, h * 64:(h + 1) * 64], start=(h % 8 == 0), stop=False))
            p.op("tensor", lambda e, s, psx=psx, colb=colb, h=h: e.matmul(
                psx[:, colb:colb + 64], Mbf[:, h * 128:(h + 1) * 128],
                qis[:, h * 64:(h + 1) * 64], start=False, stop=(h % 8 == 7)),
                tag=(h in (7, 15)))
        p.w("scalar", "tensor")
        obv = obuf[:].rearrange("q (h a) -> q h a", h=NH)
        p.op("scalar", lambda e, s, c=c, obv=obv: e.activation(
            out=obv[:, 0:8, c * 64:(c + 1) * 64],
            in_=pC[:, 0:512].rearrange("q (h a) -> q h a", h=8), func=AF.Copy))
        p.op("scalar", lambda e, s, c=c, obv=obv: e.activation(
            out=obv[:, 8:16, c * 64:(c + 1) * 64],
            in_=pD_[:, 0:512].rearrange("q (h a) -> q h a", h=8), func=AF.Copy), tag=True)
        for hg in range(2):
            p.w("tensor", "vector")
            for hh in range(4):
                h = hg * 8 + hh
                p.op("tensor", lambda e, s, hh=hh, h=h, par=par: e.transpose(
                    pT0[par:par + 64, hh * 128:(hh + 1) * 128], kds[:, h * 64:(h + 1) * 64],
                    ident[:], tile_position=(0, par)), tag=(hh == 3))
            p.w("vector", "tensor")
            p.op("vector", lambda e, s, hg=hg, par=par: e.tensor_copy(
                out=kdts[par:par + 64, hg * 1024:hg * 1024 + 512],
                in_=pT0[par:par + 64, 0:512]), tag=True)
            p.w("tensor", "vector")
            for hh in range(4):
                h = hg * 8 + 4 + hh
                p.op("tensor", lambda e, s, hh=hh, h=h, par=par: e.transpose(
                    pT0[par:par + 64, hh * 128:(hh + 1) * 128], kds[:, h * 64:(h + 1) * 64],
                    ident[:], tile_position=(0, par)), tag=(hh == 3))
            p.w("vector", "tensor")
            p.op("vector", lambda e, s, hg=hg, par=par: e.tensor_copy(
                out=kdts[par:par + 64, hg * 1024 + 512:hg * 1024 + 1024],
                in_=pT0[par:par + 64, 0:512]), tag=True)
        for h in range(NH):
            q4 = h % 4
            if q4 == 0:
                p.w("tensor", "vector")
            p.op("tensor", lambda e, s, h=h, q4=q4, tt=tt, par=par: e.matmul(
                pF[:, q4 * 128:(q4 + 1) * 128], kdts[par:par + 64, h * 128:(h + 1) * 128],
                vtok[par:par + 64, tt * H + h * 128: tt * H + (h + 1) * 128],
                start=(q4 == 0), stop=(q4 == 3)), tag=(q4 == 3))
            if q4 == 3:
                p.w("vector", "tensor")
                h0_ = h - 3
                mv = Mst[:, h0_ * 128:(h0_ + 4) * 128].rearrange("q (g a) -> q g a", g=4)
                pv = pF[:, 0:512].rearrange("q (g a) -> q g a", g=4)
                egb = EGtot[:, h0_:h0_ + 4].rearrange("q (g o) -> q g o", o=1).broadcast_to([128, 4, 128])
                p.op("vector", lambda e, s, mv=mv, pv=pv: e.tensor_tensor(
                    out=mv, in0=mv, in1=pv, op=ALU.add), tag=True)
                p.w("vector", "vector")
                p.op("vector", lambda e, s, mv=mv, egb=egb: e.tensor_tensor(
                    out=mv, in0=mv, in1=egb, op=ALU.mult), tag=True)

    p.barrier()
    p.op("vector", lambda e, s: e.tensor_copy(out=Mbf[:], in_=Mst[:]), tag=True)
    p.w("gpsimd", "vector")
    p.scat("gpsimd", exch[:], exi[:, 0:1], Mbf[:])
    p.wv("gpsimd", "gpsimd_dma", p.dma["gpsimd"])
    p.scat("gpsimd", exch[:], exf[:, 0:1], flagst[:])
    p.wv("gpsimd", "gpsimd_dma", p.dma["gpsimd"])
    p.spin_flags("gpsimd", [exch[256:257, 0:2].bitcast(I32),
                            exch[257:258, 0:2].bitcast(I32)], noncei[0:1, 0:1])
    hinit = Mbf
    p.gath("gpsimd", hinit[:], exch[:], exg[:, 0:1])
    p.w("vector", "gpsimd", dma=True)
    p.op("vector", lambda e, s: e.tensor_scalar(
        out=hinit[:], in0=hinit[:], scalar1=rolef[:, 0:1], scalar2=None,
        op0=ALU.mult), tag=True)
    for h in range(NH):
        p.w("tensor", "vector")
        for half in range(2):
            psx = (pC, pD_)[half]
            p.op("tensor", lambda e, s, h=h, half=half, psx=psx: e.matmul(
                psx[:, 0:512], hinit[:, h * 128:(h + 1) * 128],
                qbuf[:, h * T + half * 512: h * T + half * 512 + 512],
                start=True, stop=True), tag=(half == 1))
        p.w("vector", "tensor")
        for half in range(2):
            psx = (pC, pD_)[half]
            p.op("vector", lambda e, s, h=h, half=half, psx=psx: e.tensor_tensor(
                out=obuf[:, h * T + half * 512: h * T + half * 512 + 512],
                in0=obuf[:, h * T + half * 512: h * T + half * 512 + 512],
                in1=psx[:, 0:512], op=ALU.add), tag=(half == 1))

    if dbg:
        p.w("gpsimd", "vector")
        p.dmao("gpsimd", d_ob, obuf[:])
        p.dmao("gpsimd", d_q, qbuf[:])

    p.barrier()
    for j in range(NDC):
        p.w("scalar", "vector")
        if j >= 1:
            p.w("scalar", "tensor")
        p.op("scalar", lambda e, s, j=j: e.activation(
            out=sqs[:], in_=obuf[:, j * T:(j + 1) * T], func=AF.Square), tag=True)
        p.w("tensor", "scalar")
        for half in range(2):
            p.op("tensor", lambda e, s, j=j, half=half: e.matmul(
                (pA, pB)[half][:, 0:512], ones[:], sqs[:, half * 512:half * 512 + 512],
                start=(j == 0), stop=(j == NDC - 1)),
                tag=(j == NDC - 1 and half == 1))
    emit_ln_finish(p, None, (pA, pB), mb, rstdb, scr, rms_only=True, epsc=epsc)
    p.w("vector", "scalar")
    for j in range(NDC):
        p.op("vector", lambda e, s, j=j: e.scalar_tensor_tensor(
            out=obuf[:, j * T:(j + 1) * T], in0=obuf[:, j * T:(j + 1) * T],
            scalar=gwb[:, j:j + 1], in1=rstdb[:], op0=ALU.mult, op1=ALU.mult),
            tag=(j == NDC - 1))

    p.barrier()
    wo_done = {0: 0, 1: 0}
    for do in range(NDC):
        wt = (wt0, wt1)[do % 2]
        if do >= 2:
            p.w("gpsimd", "tensor")
        p.dmao("gpsimd", wt[:], chunk(wo_p, do))
        b = (xs0, xs1)[do % 2]
        if do >= 2:
            p.w("sync", "scalar")
        p.dmao("sync", b[:], chunk(x_in, do))
        p.w("tensor", "gpsimd", dma=True)
        for half in range(2):
            psx = (pE, pF)[half]
            p.w_at("tensor", "vector", wo_done[half])
            for dj in range(NDC):
                p.op("tensor", lambda e, s, psx=psx, wt=wt, dj=dj, half=half: e.matmul(
                    psx[:, 0:512], wt[:, dj * 128:(dj + 1) * 128],
                    obuf[:, dj * T + half * 512: dj * T + half * 512 + 512],
                    start=(dj == 0), stop=(dj == NDC - 1)), tag=(dj == NDC - 1))
        p.w("vector", "tensor")
        p.w("vector", "sync", dma=True)
        for half in range(2):
            psx = (pE, pF)[half]
            sl = slice(half * 512, (half + 1) * 512)
            p.op("vector", lambda e, s, psx=psx, b=b, sl=sl: e.tensor_tensor(
                out=b[:, sl], in0=b[:, sl], in1=psx[:, 0:512], op=ALU.add),
                tag=True)
            wo_done[half] = p.mark("vector")
        p.w("sync", "vector")
        p.dmao("sync", chunk(x1_out, do), b[:])
        p.w("scalar", "vector")
        if do >= 1:
            p.w("scalar", "tensor")
        p.op("scalar", lambda e, s, b=b: e.activation(out=sqs[:], in_=b[:], func=AF.Square))
        p.op("scalar", lambda e, s, b=b: e.activation(out=bfs[:], in_=b[:], func=AF.Copy), tag=True)
        p.w("tensor", "scalar")
        for half in range(2):
            p.op("tensor", lambda e, s, do=do, half=half: e.matmul(
                (pA, pB)[half][:, 0:512], ones[:], bfs[:, half * 512:half * 512 + 512],
                start=(do == 0), stop=(do == NDC - 1)))
            p.op("tensor", lambda e, s, do=do, half=half: e.matmul(
                (pC, pD_)[half][:, 0:512], ones[:], sqs[:, half * 512:half * 512 + 512],
                start=(do == 0), stop=(do == NDC - 1)),
                tag=(do == NDC - 1 and half == 1))

    emit_ln_finish(p, (pA, pB), (pC, pD_), mb, rstdb, scr, epsc=epsc)

    p.barrier()
    xln2 = gtok
    lanes2 = [(xs0[:], "vector"), (xs1[:], "vector"),
              (wt0[:].bitcast(F32), "gpsimd"), (wt1[:].bitcast(F32), "gpsimd")]
    emit_ln_apply(p, NDC, lambda j: chunk(x1_out, j), lanes2, mb, rstdb,
                  {"vector": scr[:], "gpsimd": e1s[:]},
                  lambda j: xln2[:, j * T:(j + 1) * T],
                  dma_out_fn=lambda j: chunk(xln2_out, j))

    p.w("tensor", "vector")
    for tt in range(NTC):
        psx = (pE, pF)[tt % 2]
        if tt >= 2:
            p.w("tensor", "scalar")
        for dj in range(NDC):
            p.op("tensor", lambda e, s, psx=psx, tt=tt, dj=dj, xln2=xln2: e.matmul(
                psx[:, 0:16], xln2[:, dj * T + tt * 128: dj * T + (tt + 1) * 128],
                rwt[:, dj * 16:(dj + 1) * 16], start=(dj == 0), stop=False))
        p.op("tensor", lambda e, s, psx=psx: e.matmul(
            psx[:, 0:16], ones[0:1, :], rbt[:], start=False, stop=True), tag=True)
        p.w("vector", "tensor")
        p.w("vector", "gpsimd", dma=True)
        p.op("vector", lambda e, s, psx=psx: e.tensor_reduce(
            out=col0[:], in_=psx[:, 0:16], axis=AXX, op=ALU.max, negate=True), tag=True)
        p.w("scalar", "vector")
        p.w("scalar", "gpsimd", dma=True)
        p.op("scalar", lambda e, s, psx=psx: e.activation(
            out=prb[:], in_=psx[:, 0:16], func=AF.Exp, bias=col0[:, 0:1]), tag=True)
        p.w("vector", "scalar")
        p.op("vector", lambda e, s: e.tensor_reduce(
            out=col1[:], in_=prb[:], axis=AXX, op=ALU.add), tag=True)
        p.w("vector", "vector")
        p.op("vector", lambda e, s: e.reciprocal(out=col2[:], in_=col1[:]), tag=True)
        p.w("vector", "vector")
        p.op("vector", lambda e, s: e.tensor_scalar(
            out=prb[:], in0=prb[:], scalar1=col2[:, 0:1], scalar2=None, op0=ALU.mult),
            tag=True)
        p.w("gpsimd", "vector")
        p.dmao("gpsimd", p1_out[tt * 128:(tt + 1) * 128, :], prb[:])

    p.wv("sync", "sync_dma", p.dma["sync"])
    p.wv("sync", "gpsimd_dma", p.dma["gpsimd"])
    p.wv("gpsimd", "gpsimd_dma", p.dma["gpsimd"])

    p.build(nc, ctx)
    ctx.close()
    return nc






def emit_bisect_iter(p, it, pG_, sel8f, Tcol, Tb, pbis, junk, cnt, cntb, sel8t):
    step = 0.5 / (2.0 ** (it + 1))
    p.w("tensor", "vector")
    p.op("tensor", lambda e, s: e.matmul(
        pG_[:, 0:1], sel8f[:], Tcol[:], start=True, stop=True), tag=True)
    p.w("vector", "tensor")
    p.op("vector", lambda e, s: e.tensor_copy(out=Tb[:], in_=pG_[:, 0:1]), tag=True)
    p.w("vector", "vector")
    p.op("vector", lambda e, s: e.tensor_scalar(
        out=junk[:], in0=pbis[:], scalar1=Tb[:, 0:1], scalar2=None, op0=ALU.is_gt),
        tag=True)
    p.w("vector", "vector")
    p.op("vector", lambda e, s: e.tensor_reduce(
        out=cnt[:], in_=junk[:], axis=AXX, op=ALU.add), tag=True)
    p.w("vector", "vector")
    p.op("vector", lambda e, s: e.tensor_copy(out=cntb[:], in_=cnt[:]), tag=True)
    p.w("tensor", "vector")
    p.op("tensor", lambda e, s: e.matmul(
        pG_[0:16, 4:5], sel8t[:], cntb[:], start=True, stop=True), tag=True)
    p.w("vector", "tensor")
    p.op("vector", lambda e, s: e.tensor_scalar(
        out=cnt[0:16, 0:1], in0=pG_[0:16, 4:5], scalar1=512.0, scalar2=None,
        op0=ALU.is_gt), tag=True)
    p.w("vector", "vector")
    p.op("vector", lambda e, s, step=step: e.scalar_tensor_tensor(
        out=Tcol[:], in0=cnt[0:16, 0:1], scalar=2.0 * step, in1=Tcol[:],
        op0=ALU.mult, op1=ALU.add), tag=True)
    p.w("vector", "vector")
    p.op("vector", lambda e, s, step=step: e.tensor_scalar(
        out=Tcol[:], in0=Tcol[:], scalar1=step, scalar2=None, op0=ALU.subtract), tag=True)
    p.w("vector", "vector")



def build_L2(dbg=False):
    nc = bass.Bass("TRN2", target_bir_lowering=False)
    x1_in = nc.dram_tensor("x1i", [H, T], F32, kind="ExternalInput").ap()
    xln2_in = nc.dram_tensor("xln2i", [H, T], BF16, kind="ExternalInput").ap()
    pbis_in = nc.dram_tensor("pbis", [128, 1024], F32, kind="ExternalInput").ap()
    pmye_in = nc.dram_tensor("pmye", [16, T], F32, kind="ExternalInput").ap()
    we_p = nc.dram_tensor("we_p", [NDC * 128, H], F32, kind="ExternalInput").ap()
    web_in = nc.dram_tensor("web", [128, NDC], F32, kind="ExternalInput").ap()
    wqt_p = nc.dram_tensor("wqt_p", [NDC * 128, H], F32, kind="ExternalInput").ap()
    wkt_p = nc.dram_tensor("wkt_p", [4 * 128, H], F32, kind="ExternalInput").ap()
    wvt_in = nc.dram_tensor("wvt", [128, NDC * 512], F32, kind="ExternalInput").ap()
    wot_p = nc.dram_tensor("wot_p", [NDC * 128, H], F32, kind="ExternalInput").ap()
    rw_in = nc.dram_tensor("rw2", [128, NDC * 16], F32, kind="ExternalInput").ap()
    rbias = nc.dram_tensor("rb2", [1, 16], F32, kind="ExternalInput").ap()
    cosin = nc.dram_tensor("cosin", [128, T], F32, kind="ExternalInput").ap()
    sinin = nc.dram_tensor("sinin", [128, T], F32, kind="ExternalInput").ap()
    swapin = nc.dram_tensor("swapin", [128, 128], BF16, kind="ExternalInput").ap()
    qgidx = nc.dram_tensor("qgidx", [1, T], F32, kind="ExternalInput").ap()
    kgidx = nc.dram_tensor("kgidx", [128, 16], F32, kind="ExternalInput").ap()
    identc = nc.dram_tensor("ident2", [128, 128], BF16, kind="ExternalInput").ap()
    onesc = nc.dram_tensor("ones2", [128, 128], BF16, kind="ExternalInput").ap()
    onesfc = nc.dram_tensor("onesf2", [128, 128], F32, kind="ExternalInput").ap()
    sel8fc = nc.dram_tensor("sel8f", [16, 128], F32, kind="ExternalInput").ap()
    sel8tc = nc.dram_tensor("sel8t", [128, 16], BF16, kind="ExternalInput").ap()
    exidx = nc.dram_tensor("exidx2", [128, 8], U32, kind="ExternalInput").ap()
    exgidx = nc.dram_tensor("exgidx2", [128, 16], U32, kind="ExternalInput").ap()
    exfidx = nc.dram_tensor("exfidx2", [2, 1], U32, kind="ExternalInput").ap()
    noncein = nc.dram_tensor("noncef2", [2, 16], BF16, kind="ExternalInput").ap()
    noncei_in = nc.dram_tensor("noncei2", [1, 1], I32, kind="ExternalInput").ap()

    x3_out = nc.dram_tensor("x3", [H, T], F32, kind="ExternalOutput").ap()
    xln4_out = nc.dram_tensor("xln4", [H, T], BF16, kind="ExternalOutput").ap()
    p2_out = nc.dram_tensor("p2", [T, 16], F32, kind="ExternalOutput").ap()

    x2d = nc.dram_tensor("x2d", [H, T], F32, kind="Internal").ap()
    if dbg:
        d_sbc = nc.dram_tensor("d_sbc", [128, T], F32, kind="ExternalOutput").ap()
        d_x2 = nc.dram_tensor("d_x2", [H, T], F32, kind="ExternalOutput").ap()
        d_ob = nc.dram_tensor("d_ob", [128, NDC * T], BF16, kind="ExternalOutput").ap()
        d_kb = nc.dram_tensor("d_kb", [128, 4 * 2048], BF16, kind="ExternalOutput").ap()
        d_vb = nc.dram_tensor("d_vb", [128, 16 * 512], BF16, kind="ExternalOutput").ap()
    exch = nc.dram_tensor("exch_l2", [2050, 2048], BF16, kind="Internal",
                          addr_space="Shared").ap()

    p = Prog()
    ctx = ExitStack()

    def sb(name, shape, dt):
        return ctx.enter_context(nc.sbuf_tensor("sb_" + name, shape, dt))

    def ps_(name, shape, dt):
        return ctx.enter_context(nc.psum_tensor("ps_" + name, shape, dt))

    xbig = sb("xbig", [128, NDC * T], BF16)
    ebuf = sb("ebuf", [128, NDC * T], BF16)
    kbuf = sb("kbuf", [128, 4 * 2048], BF16)
    vbuf = sb("vbuf", [128, 16 * 512], BF16)
    qrope = sb("qrope", [128, 1024], BF16)
    xs0 = sb("xs0", [128, T], F32)
    xs1 = sb("xs1", [128, T], F32)
    sqs = sb("sqs", [128, T], BF16)
    bfs = sb("bfs", [128, T], BF16)
    mb = sb("mb", [128, T], F32)
    rstdb = sb("rstdb", [128, T], F32)
    scr = sb("scr", [128, T], F32)
    sbc = sb("sbc", [128, T], F32)
    wt0 = sb("wt0", [128, H], BF16)
    wt1 = sb("wt1", [128, H], BF16)
    wvtb = sb("wvtb", [128, NDC * 512], BF16)
    webb = sb("webb", [128, NDC], F32)
    pbis = sb("pbis", [128, 1024], F32)
    pmye = sb("pmye", [16, T], F32)
    pmyeb = sb("pmyeb", [16, T], BF16)
    Tcol = sb("Tcol", [16, 1], F32)
    Tb = sb("Tb", [128, 1], F32)
    stp = sb("stp", [16, 1], F32)
    cnt = sb("cnt", [128, 1], F32)
    cntb = sb("cntb", [128, 1], BF16)
    cnte = sb("cnte", [16, 1], F32)
    sgn = sb("sgn", [16, 1], F32)
    srow = sb("srow", [1, T], F32)
    cfull = sb("cfull", [128, T], F32)
    sfull = sb("sfull", [128, T], F32)
    swapc = sb("swapc", [128, 128], BF16)
    qgx = sb("qgx", [1, T], F32)
    kgx = sb("kgx", [128, 16], F32)
    qgb = sb("qgb", [128, 512], F32)
    masks = sb("masks", [128, 16 * 512], BF16)
    pts2 = sb("pts2", [128, 1024], BF16)
    recro = sb("recro", [1, 512], F32)
    recb = sb("recb", [128, 512], F32)
    ident = sb("ident", [128, 128], BF16)
    ones = sb("ones", [128, 128], BF16)
    onesf = sb("onesf", [128, 128], F32)
    sel8f = sb("sel8f", [16, 128], F32)
    sel8t = sb("sel8t", [128, 16], BF16)
    junk = sb("junk", [128, 1024], BF16)
    exi = sb("exi", [128, 8], U32)
    exg = sb("exg", [128, 16], U32)
    exf = sb("exf", [2, 1], U32)
    flagst = sb("flagst", [2, 16], BF16)
    noncei = sb("noncei", [1, 1], I32)
    epsc = sb("epsc", [128, 1], F32)
    col0 = sb("col0", [128, 1], F32)
    col1 = sb("col1", [128, 1], F32)
    col2 = sb("col2", [128, 1], F32)
    prb = sb("prb", [128, 16], F32)
    rwt = sb("rwt", [128, NDC * 16], BF16)
    rbt = sb("rbt", [1, 16], BF16)

    pA = ps_("pA", [128, 512], F32)
    pB = ps_("pB", [128, 512], F32)
    pC = ps_("pC", [128, 512], F32)
    pD_ = ps_("pD", [128, 512], F32)
    pE = ps_("pE", [128, 512], F32)
    pF = ps_("pF", [128, 512], F32)
    pG_ = ps_("pG", [128, 512], F32)
    pH_ = ps_("pH", [128, 512], F32)

    for dst, srcd in [(pbis[:], pbis_in), (pmye[:], pmye_in), (cfull[:], cosin),
                      (sfull[:], sinin), (swapc[:], swapin), (qgx[:], qgidx), (kgx[:], kgidx),
                      (exf[:], exfidx), (flagst[:], noncein),
                      (noncei[:], noncei_in), (webb[:], web_in),
                      (ident[:], identc), (ones[:], onesc), (onesf[:], onesfc),
                      (sel8f[:], sel8fc), (sel8t[:], sel8tc)]:
        p.dmao("sync", dst, srcd)
    p.dmao("sync", exi[:], exidx)
    p.dmao("sync", exg[:], exgidx)
    p.dmao("gpsimd", rwt[:], rw_in)
    p.dmao("gpsimd", rbt[:], rbias)
    p.dmao("gpsimd", wvtb[:], wvt_in)
    for j in range(NDC):
        p.dmao("sync", xbig[:, j * T:(j + 1) * T], chunk(xln2_in, j))
    p.op("vector", lambda e, s: e.memset(epsc[:], EPS))
    p.op("vector", lambda e, s: e.memset(Tcol[:], 0.5))
    p.op("vector", lambda e, s: e.memset(stp[:], 0.25))
    p.w("vector", "sync", dma=True)
    p.op("vector", lambda e, s: e.tensor_copy(out=pmyeb[:], in_=pmye[:]), tag=True)

    p.w("tensor", "sync", dma=True)
    p.w("vector", "sync", dma=True)
    for do in range(NDC):
        wt = (wt0, wt1)[do % 2]
        if do >= 2:
            p.w("gpsimd", "tensor")
        p.dmao("gpsimd", wt[:], chunk(we_p, do))
        p.w("tensor", "gpsimd", dma=True)
        for half in range(2):
            psx = (pA, pB)[half]
            p.w("tensor", "scalar")
            for dj in range(NDC):
                p.op("tensor", lambda e, s, psx=psx, wt=wt, dj=dj, half=half: e.matmul(
                    psx[:, 0:512], wt[:, dj * 128:(dj + 1) * 128],
                    xbig[:, dj * T + half * 512: dj * T + half * 512 + 512],
                    start=(dj == 0), stop=(dj == NDC - 1)), tag=(dj == NDC - 1))
            p.w("scalar", "tensor")
            p.op("scalar", lambda e, s, psx=psx, do=do, half=half: e.activation(
                out=ebuf[:, do * T + half * 512: do * T + half * 512 + 512],
                in_=psx[:, 0:512], func=AF.Copy), tag=True)
            it = do * 2 + half
            if it < NBIS:
                emit_bisect_iter(p, it, pG_, sel8f, Tcol, Tb, pbis, junk, cnt, cntb, sel8t)


    p.op("vector", lambda e, s: e.tensor_scalar(
        out=junk[0:16, 0:1024], in0=pmye[:], scalar1=Tcol[0:16, 0:1], scalar2=None,
        op0=ALU.is_gt), tag=True)
    p.w("vector", "vector")
    p.op("vector", lambda e, s: e.tensor_tensor(
        out=junk[0:16, 0:1024], in0=junk[0:16, 0:1024], in1=pmyeb[:], op=ALU.mult), tag=True)
    p.w("tensor", "vector")
    for half in range(2):
        p.op("tensor", lambda e, s, half=half: e.matmul(
            pG_[0:1, 0:512], ones[0:16, 0:1], junk[0:16, half * 512:half * 512 + 512],
            start=True, stop=True), tag=True)
        p.w("vector", "tensor")
        p.op("vector", lambda e, s, half=half: e.tensor_copy(
            out=srow[0:1, half * 512:half * 512 + 512], in_=pG_[0:1, 0:512]), tag=True)
        p.w("tensor", "vector")
    for half in range(2):
        p.op("tensor", lambda e, s, half=half: e.matmul(
            pH_[:, 0:512], onesf[0:1, :], srow[0:1, half * 512:half * 512 + 512],
            start=True, stop=True), tag=True)
        p.w("vector", "tensor")
        p.op("vector", lambda e, s, half=half: e.tensor_copy(
            out=sbc[:, half * 512:half * 512 + 512], in_=pH_[:, 0:512]), tag=True)
        p.w("tensor", "vector")
    p.barrier()

    for do in range(NDC):
        b = (xs0, xs1)[do % 2]
        if do >= 2:
            p.w("sync", "scalar")
        p.dmao("sync", b[:], chunk(x1_in, do))
        p.w("vector", "sync", dma=True)
        p.op("vector", lambda e, s, do=do: e.scalar_tensor_tensor(
            out=scr[:], in0=ebuf[:, do * T:(do + 1) * T], scalar=webb[:, do:do + 1],
            in1=sbc[:], op0=ALU.add, op1=ALU.mult), tag=True)
        p.w("vector", "vector")
        p.op("vector", lambda e, s, b=b: e.tensor_tensor(
            out=b[:], in0=b[:], in1=scr[:], op=ALU.add), tag=True)
        p.w("sync", "vector")
        p.dmao("sync", chunk(x2d, do), b[:])
        p.w("scalar", "vector")
        if do >= 1:
            p.w("scalar", "tensor")
        p.op("scalar", lambda e, s, b=b: e.activation(out=sqs[:], in_=b[:], func=AF.Square))
        p.op("scalar", lambda e, s, b=b: e.activation(out=bfs[:], in_=b[:], func=AF.Copy), tag=True)
        p.w("tensor", "scalar")
        for half in range(2):
            p.op("tensor", lambda e, s, do=do, half=half: e.matmul(
                (pC, pD_)[half][:, 0:512], ones[:], bfs[:, half * 512:half * 512 + 512],
                start=(do == 0), stop=(do == NDC - 1)))
            p.op("tensor", lambda e, s, do=do, half=half: e.matmul(
                (pE, pF)[half][:, 0:512], ones[:], sqs[:, half * 512:half * 512 + 512],
                start=(do == 0), stop=(do == NDC - 1)),
                tag=(do == NDC - 1 and half == 1))
    emit_ln_finish(p, (pC, pD_), (pE, pF), mb, rstdb, scr, epsc=epsc)
    if dbg:
        p.barrier()
        p.dmao("gpsimd", d_sbc, sbc[:])
        for j in range(NDC):
            p.dmao("gpsimd", chunk(d_x2, j), chunk(x2d, j))
        p.barrier()

    p.barrier()
    lanesA = [(xs0[:], "vector"), (xs1[:], "vector"),
              (wt0[:].bitcast(F32), "gpsimd"), (wt1[:].bitcast(F32), "gpsimd")]
    emit_ln_apply(p, NDC, lambda j: chunk(x2d, j), lanesA, mb, rstdb,
                  {"vector": scr[:], "gpsimd": sbc[:]},
                  lambda j: xbig[:, j * T:(j + 1) * T])
    p.barrier()

    myoff_col = None
    for g in range(NKV):
        wt = (wt0, wt1)[g % 2]
        if g >= 2:
            p.w("gpsimd", "tensor")
        p.dmao("gpsimd", wt[:], chunk(wkt_p, g))
        p.w("tensor", "gpsimd", dma=True)
        for half in range(2):
            psx = (pA, pB)[half]
            p.w("tensor", "vector")
            for dj in range(NDC):
                p.op("tensor", lambda e, s, psx=psx, wt=wt, dj=dj, half=half: e.matmul(
                    psx[:, 0:512], wt[:, dj * 128:(dj + 1) * 128],
                    xbig[:, dj * T + half * 512: dj * T + half * 512 + 512],
                    start=(dj == 0), stop=(dj == NDC - 1)), tag=(dj == NDC - 1))
        for half in range(2):
            psx = (pA, pB)[half]
            sl = slice(half * 512, (half + 1) * 512)
            p.w("scalar", "tensor")
            p.w("scalar", "vector")
            p.op("scalar", lambda e, s, psx=psx: e.activation(
                out=junk[:, 0:512], in_=psx[:, 0:512], func=AF.Copy), tag=True)
            p.w("tensor", "scalar")
            p.w("tensor", "vector")
            p.op("tensor", lambda e, s: e.matmul(
                pC[:, 0:512], swapc[:], junk[:, 0:512], start=True, stop=True), tag=True)
            p.w("vector", "tensor")
            p.w("vector", "scalar")
            p.op("vector", lambda e, s, sl=sl: e.tensor_tensor(
                out=scr[:, 0:512], in0=junk[:, 0:512], in1=cfull[:, sl], op=ALU.mult), tag=True)
            p.w("vector", "vector")
            p.op("vector", lambda e, s, sl=sl: e.tensor_tensor(
                out=scr[:, 512:1024], in0=pC[:, 0:512], in1=sfull[:, sl], op=ALU.mult), tag=True)
            p.w("vector", "vector")
            p.op("vector", lambda e, s, g=g, sl=sl: e.tensor_tensor(
                out=kbuf[:, g * 2048 + sl.start: g * 2048 + sl.stop],
                in0=scr[:, 0:512], in1=scr[:, 512:1024], op=ALU.add), tag=True)
    p.w("tensor", "gpsimd", dma=True)
    for tt in range(NTC):
        psx = (pG_, pH_)[tt % 2]
        if tt >= 2:
            p.w("tensor", "scalar")
        for dj in range(NDC):
            p.op("tensor", lambda e, s, psx=psx, tt=tt, dj=dj: e.matmul(
                psx[:, 0:512], xbig[:, dj * T + tt * 128: dj * T + (tt + 1) * 128],
                wvtb[:, dj * 512:(dj + 1) * 512], start=(dj == 0), stop=(dj == NDC - 1)),
                tag=(dj == NDC - 1))
        p.w("scalar", "tensor")
        p.op("scalar", lambda e, s, psx=psx, tt=tt: e.activation(
            out=vbuf[:, tt * 512:(tt + 1) * 512], in_=psx[:, 0:512], func=AF.Copy), tag=True)
    p.barrier()

    for g in range(NKV):
        p.scat("gpsimd", exch[:], exi[:, g:g + 1],
               kbuf[:, g * 2048: g * 2048 + 1024])
    for vb in range(4):
        p.scat("gpsimd", exch[:], exi[:, 4 + vb:5 + vb],
               vbuf[:, vb * 1024: (vb + 1) * 1024])
    p.wv("gpsimd", "gpsimd_dma", p.dma["gpsimd"])
    p.scat("gpsimd", exch[:], exf[:, 0:1], flagst[:])
    p.wv("gpsimd", "gpsimd_dma", p.dma["gpsimd"])
    p.spin_flags("gpsimd", [exch[2048:2049, 0:2].bitcast(I32),
                            exch[2049:2050, 0:2].bitcast(I32)], noncei[0:1, 0:1])
    for g in range(NKV):
        for hf in range(2):
            p.gath("gpsimd", kbuf[:, g * 2048 + hf * 1024: g * 2048 + (hf + 1) * 1024],
                   exch[:], exg[:, g * 2 + hf: g * 2 + hf + 1])
    for vb in range(8):
        p.gath("gpsimd", vbuf[:, vb * 1024:(vb + 1) * 1024], exch[:],
               exg[:, 8 + vb: 9 + vb])
    p.barrier()

    NEXT = [12, 16]
    obank_done = {0: (0, 0), 1: (0, 0)}
    obank_rd = 0
    den_done = {0: 0, 1: 0}
    sco_done = {0: 0, 1: 0}
    panel_mark = [0, 0]
    for grp in range(2):
        p.w("tensor", "vector")
        p.op("tensor", lambda e, s, grp=grp: e.matmul(
            pH_[:, 0:512], onesf[0:1, :], qgx[0:1, grp * 512:(grp + 1) * 512],
            start=True, stop=True), tag=True)
        p.w("vector", "tensor")
        p.op("vector", lambda e, s: e.tensor_copy(out=qgb[:], in_=pH_[:, 0:512]), tag=True)
        p.w("vector", "vector")
        for j in range(NEXT[grp]):
            p.op("vector", lambda e, s, j=j: e.tensor_scalar(
                out=masks[:, j * 512:(j + 1) * 512], in0=qgb[:], scalar1=kgx[:, j:j + 1],
                scalar2=None, op0=ALU.is_ge), tag=(j == NEXT[grp] - 1))
        HORD = list(range(NH))
        p.dmao("gpsimd", wt0[:], chunk(wqt_p, HORD[0]))
        for hi, h in enumerate(HORD):
            g = h // GRP
            wt = (wt0, wt1)[hi % 2]
            psq = (pA, pB)[hi % 2]
            pden = (pF, pG_)[hi % 2]
            if hi + 1 < NH:
                p.w_at("gpsimd", "tensor", panel_mark[(hi + 1) % 2])
                p.dmao("gpsimd", wt1[:] if (hi + 1) % 2 else wt0[:], chunk(wqt_p, HORD[hi + 1]))
            p.w("tensor", "gpsimd", dma=True)
            sm, vm = obank_done[hi % 2]
            p.w_at("tensor", "scalar", sm)
            p.w_at("tensor", "vector", vm)
            for dj in range(NDC):
                p.op("tensor", lambda e, s, psq=psq, wt=wt, dj=dj, grp=grp: e.matmul(
                    psq[:, 0:512], wt[:, dj * 128:(dj + 1) * 128],
                    xbig[:, dj * T + grp * 512: dj * T + grp * 512 + 512],
                    start=(dj == 0), stop=(dj == NDC - 1)), tag=(dj == NDC - 1))
            panel_mark[hi % 2] = p.mark("tensor")
            qs_ = qrope[:, (hi % 2) * 512:(hi % 2) * 512 + 512]
            sl = slice(grp * 512, (grp + 1) * 512)
            p.w("scalar", "tensor")
            p.w("scalar", "vector")
            p.op("scalar", lambda e, s, psq=psq: e.activation(
                out=junk[:, 0:512], in_=psq[:, 0:512], func=AF.Copy), tag=True)
            p.w("tensor", "scalar")
            p.w("tensor", "vector")
            p.op("tensor", lambda e, s: e.matmul(
                pH_[:, 0:512], swapc[:], junk[:, 0:512], start=True, stop=True), tag=True)
            p.w("vector", "tensor")
            p.w("vector", "scalar")
            p.op("vector", lambda e, s, sl=sl: e.tensor_tensor(
                out=scr[:, 0:512], in0=junk[:, 0:512], in1=cfull[:, sl], op=ALU.mult), tag=True)
            p.w("vector", "vector")
            p.op("vector", lambda e, s, sl=sl: e.tensor_tensor(
                out=scr[:, 512:1024], in0=pH_[:, 0:512], in1=sfull[:, sl], op=ALU.mult), tag=True)
            p.w("vector", "vector")
            p.op("vector", lambda e, s, qs_=qs_: e.tensor_tensor(
                out=qs_[:], in0=scr[:, 0:512], in1=scr[:, 512:1024], op=ALU.add), tag=True)
            rope_vm = p.mark("vector")
            nj = NEXT[grp]
            mask_vm = {}
            for j in range(nj + 1):
                if j < nj:
                    psS = (pC, pD_)[j % 2]
                    slot = (j % 2) * 512
                    p.w_at("tensor", "scalar", sco_done[j % 2])
                    if j == 0:
                        p.w_at("tensor", "vector", rope_vm)
                    p.op("tensor", lambda e, s, psS=psS, g=g, j=j, qs_=qs_: e.matmul(
                        psS[:, 0:512], kbuf[:, g * 2048 + j * 128: g * 2048 + (j + 1) * 128],
                        qs_[:, 0:512], start=True, stop=True, skip_group_check=True), tag=True)
                    p.w("scalar", "tensor")
                    p.op("scalar", lambda e, s, psS=psS, slot=slot: e.activation(
                        out=pts2[:, slot:slot + 512], in_=psS[:, 0:512], func=AF.Exp,
                        scale=1.0 / 11.313708498984761), tag=True)
                    sco_done[j % 2] = p.mark("scalar")
                    p.w("vector", "scalar")
                    p.op("vector", lambda e, s, j=j, slot=slot: e.tensor_tensor(
                        out=pts2[:, slot:slot + 512], in0=pts2[:, slot:slot + 512],
                        in1=masks[:, j * 512:(j + 1) * 512], op=ALU.mult), tag=True)
                    mask_vm[j] = p.mark("vector")
                if j >= 1:
                    jj = j - 1
                    slot2 = (jj % 2) * 512
                    p.w_at("tensor", "vector", mask_vm[jj])
                    if jj == 0:
                        p.w_at("tensor", "vector", den_done[hi % 2])
                    if jj == 0:
                        p.w_at("tensor", "vector", obank_rd)
                    p.op("tensor", lambda e, s, jj=jj, g=g, grp=grp, slot2=slot2: e.matmul(
                        pE[:, 0:512], vbuf[:, jj * 512 + g * 128: jj * 512 + (g + 1) * 128],
                        pts2[:, slot2:slot2 + 512], start=(jj == 0), stop=(jj == nj - 1),
                        skip_group_check=True))
                    p.op("tensor", lambda e, s, jj=jj, grp=grp, slot2=slot2, pden=pden: e.matmul(
                        pden[0:1, 0:512], ones[:, 0:1], pts2[:, slot2:slot2 + 512],
                        start=(jj == 0), stop=(jj == nj - 1), skip_group_check=True),
                        tag=True)
            p.w("vector", "tensor")
            p.op("vector", lambda e, s, pden=pden: e.reciprocal(out=recro[:], in_=pden[0:1, 0:512]), tag=True)
            den_done[hi % 2] = p.mark("vector")
            p.w("vector", "vector")
            p.w("tensor", "vector")
            p.op("tensor", lambda e, s: e.matmul(
                pH_[:, 0:512], onesf[0:1, :], recro[:], start=True, stop=True), tag=True)
            p.w("vector", "tensor")
            p.op("vector", lambda e, s: e.tensor_copy(out=recb[:], in_=pH_[:, 0:512]), tag=True)
            p.w("vector", "vector")
            p.op("vector", lambda e, s, h=h, grp=grp: e.tensor_tensor(
                out=ebuf[:, h * T + grp * 512: h * T + grp * 512 + 512],
                in0=pE[:, 0:512], in1=recb[:], op=ALU.mult), tag=True)
            obank_rd = p.mark("vector")
            obank_done[hi % 2] = (p.mark("scalar"), p.mark("vector"))
    p.barrier()

    p.barrier()
    if dbg:
        p.dmao("gpsimd", d_ob, ebuf[:])
        p.dmao("gpsimd", d_kb, kbuf[:])
        p.dmao("gpsimd", d_vb, vbuf[:])
        p.barrier()

    wo2_done = {0: 0, 1: 0}
    for do in range(NDC):
        wt = (wt0, wt1)[do % 2]
        if do >= 2:
            p.w("gpsimd", "tensor")
        p.dmao("gpsimd", wt[:], chunk(wot_p, do))
        b = (xs0, xs1)[do % 2]
        if do >= 2:
            p.w("sync", "scalar")
        p.dmao("sync", b[:], chunk(x2d, do))
        p.w("tensor", "gpsimd", dma=True)
        for half in range(2):
            psx = (pA, pB)[half]
            p.w_at("tensor", "vector", wo2_done[half])
            for dj in range(NDC):
                p.op("tensor", lambda e, s, psx=psx, wt=wt, dj=dj, half=half: e.matmul(
                    psx[:, 0:512], wt[:, dj * 128:(dj + 1) * 128],
                    ebuf[:, dj * T + half * 512: dj * T + half * 512 + 512],
                    start=(dj == 0), stop=(dj == NDC - 1)), tag=(dj == NDC - 1))
        p.w("vector", "tensor")
        p.w("vector", "sync", dma=True)
        for half in range(2):
            psx = (pA, pB)[half]
            sl = slice(half * 512, (half + 1) * 512)
            p.op("vector", lambda e, s, psx=psx, b=b, sl=sl: e.tensor_tensor(
                out=b[:, sl], in0=b[:, sl], in1=psx[:, 0:512], op=ALU.add), tag=True)
            wo2_done[half] = p.mark("vector")
        p.w("sync", "vector")
        p.dmao("sync", chunk(x3_out, do), b[:])
        p.w("scalar", "vector")
        if do >= 1:
            p.w("scalar", "tensor")
        p.op("scalar", lambda e, s, b=b: e.activation(out=sqs[:], in_=b[:], func=AF.Square))
        p.op("scalar", lambda e, s, b=b: e.activation(out=bfs[:], in_=b[:], func=AF.Copy), tag=True)
        p.w("tensor", "scalar")
        for half in range(2):
            p.op("tensor", lambda e, s, do=do, half=half: e.matmul(
                (pC, pD_)[half][:, 0:512], ones[:], bfs[:, half * 512:half * 512 + 512],
                start=(do == 0), stop=(do == NDC - 1)))
            p.op("tensor", lambda e, s, do=do, half=half: e.matmul(
                (pE, pF)[half][:, 0:512], ones[:], sqs[:, half * 512:half * 512 + 512],
                start=(do == 0), stop=(do == NDC - 1)),
                tag=(do == NDC - 1 and half == 1))
    emit_ln_finish(p, (pC, pD_), (pE, pF), mb, rstdb, scr, epsc=epsc)
    p.barrier()

    xln4 = ebuf
    lanesB = [(xs0[:], "vector"), (xs1[:], "vector"),
              (wt0[:].bitcast(F32), "gpsimd"), (wt1[:].bitcast(F32), "gpsimd")]
    emit_ln_apply(p, NDC, lambda j: chunk(x3_out, j), lanesB, mb, rstdb,
                  {"vector": scr[:], "gpsimd": sbc[:]},
                  lambda j: xln4[:, j * T:(j + 1) * T],
                  dma_out_fn=lambda j: chunk(xln4_out, j))

    p.w("tensor", "vector")
    for tt in range(NTC):
        psx = (pG_, pH_)[tt % 2]
        if tt >= 2:
            p.w("tensor", "scalar")
        for dj in range(NDC):
            p.op("tensor", lambda e, s, psx=psx, tt=tt, dj=dj, xln4=xln4: e.matmul(
                psx[:, 0:16], xln4[:, dj * T + tt * 128: dj * T + (tt + 1) * 128],
                rwt[:, dj * 16:(dj + 1) * 16], start=(dj == 0), stop=False))
        p.op("tensor", lambda e, s, psx=psx: e.matmul(
            psx[:, 0:16], ones[0:1, :], rbt[:], start=False, stop=True), tag=True)
        p.w("vector", "tensor")
        p.w("vector", "gpsimd", dma=True)
        p.op("vector", lambda e, s, psx=psx: e.tensor_reduce(
            out=col0[:], in_=psx[:, 0:16], axis=AXX, op=ALU.max, negate=True), tag=True)
        p.w("scalar", "vector")
        p.w("scalar", "gpsimd", dma=True)
        p.op("scalar", lambda e, s, psx=psx: e.activation(
            out=prb[:], in_=psx[:, 0:16], func=AF.Exp, bias=col0[:, 0:1]), tag=True)
        p.w("vector", "scalar")
        p.op("vector", lambda e, s: e.tensor_reduce(
            out=col1[:], in_=prb[:], axis=AXX, op=ALU.add), tag=True)
        p.w("vector", "vector")
        p.op("vector", lambda e, s: e.reciprocal(out=col2[:], in_=col1[:]), tag=True)
        p.w("vector", "vector")
        p.op("vector", lambda e, s: e.tensor_scalar(
            out=prb[:], in0=prb[:], scalar1=col2[:, 0:1], scalar2=None, op0=ALU.mult), tag=True)
        p.w("gpsimd", "vector")
        p.dmao("gpsimd", p2_out[tt * 128:(tt + 1) * 128, :], prb[:])

    p.wv("sync", "sync_dma", p.dma["sync"])
    p.wv("sync", "gpsimd_dma", p.dma["gpsimd"])
    p.wv("gpsimd", "gpsimd_dma", p.dma["gpsimd"])
    p.build(nc, ctx)
    ctx.close()
    return nc




def build_L3():
    nc = bass.Bass("TRN2", target_bir_lowering=False)
    x3_in = nc.dram_tensor("x3i", [H, T], F32, kind="ExternalInput").ap()
    xln4_in = nc.dram_tensor("xln4i", [H, T], BF16, kind="ExternalInput").ap()
    pbis_in = nc.dram_tensor("pbis3", [128, 1024], F32, kind="ExternalInput").ap()
    pmye_in = nc.dram_tensor("pmye3", [16, T], F32, kind="ExternalInput").ap()
    we_p = nc.dram_tensor("we_p3", [NDC * 128, H], F32, kind="ExternalInput").ap()
    web_in = nc.dram_tensor("web3", [128, NDC], F32, kind="ExternalInput").ap()
    onesc = nc.dram_tensor("ones3", [128, 128], BF16, kind="ExternalInput").ap()
    onesfc = nc.dram_tensor("onesf3", [128, 128], F32, kind="ExternalInput").ap()
    sel8fc = nc.dram_tensor("sel8f3", [16, 128], F32, kind="ExternalInput").ap()
    sel8tc = nc.dram_tensor("sel8t3", [128, 16], BF16, kind="ExternalInput").ap()
    out_o = nc.dram_tensor("outx", [H, T], F32, kind="ExternalOutput").ap()

    p = Prog()
    ctx = ExitStack()

    def sb(name, shape, dt):
        return ctx.enter_context(nc.sbuf_tensor("sb_" + name, shape, dt))

    def ps_(name, shape, dt):
        return ctx.enter_context(nc.psum_tensor("ps_" + name, shape, dt))

    xln4 = sb("xln4", [128, NDC * T], BF16)
    ebuf = sb("ebuf", [128, NDC * T], BF16)
    xs0 = sb("xs0", [128, T], F32)
    xs1 = sb("xs1", [128, T], F32)
    scr = sb("scr", [128, T], F32)
    sbc = sb("sbc", [128, T], F32)
    wt0 = sb("wt0", [128, H], BF16)
    wt1 = sb("wt1", [128, H], BF16)
    webb = sb("webb", [128, NDC], F32)
    pbis = sb("pbis", [128, 1024], F32)
    pmye = sb("pmye", [16, T], F32)
    pmyeb = sb("pmyeb", [16, T], BF16)
    Tcol = sb("Tcol", [16, 1], F32)
    Tb = sb("Tb", [128, 1], F32)
    stp = sb("stp", [16, 1], F32)
    cnt = sb("cnt", [128, 1], F32)
    cntb = sb("cntb", [128, 1], BF16)
    cnte = sb("cnte", [16, 1], F32)
    sgn = sb("sgn", [16, 1], F32)
    srow = sb("srow", [1, T], F32)
    ones = sb("ones", [128, 128], BF16)
    onesf = sb("onesf", [128, 128], F32)
    sel8f = sb("sel8f", [16, 128], F32)
    sel8t = sb("sel8t", [128, 16], BF16)
    junk = sb("junk", [128, 1024], BF16)

    pA = ps_("pA", [128, 512], F32)
    pB = ps_("pB", [128, 512], F32)
    pG_ = ps_("pG", [128, 512], F32)
    pH_ = ps_("pH", [128, 512], F32)

    for dst, srcd in [(pbis[:], pbis_in), (pmye[:], pmye_in), (webb[:], web_in),
                      (ones[:], onesc), (onesf[:], onesfc), (sel8f[:], sel8fc),
                      (sel8t[:], sel8tc)]:
        p.dmao("sync", dst, srcd)
    for j in range(NDC):
        p.dmao("sync", xln4[:, j * T:(j + 1) * T], chunk(xln4_in, j))
    p.w("vector", "sync", dma=True)
    p.op("vector", lambda e, s: e.memset(Tcol[:], 0.5))
    p.op("vector", lambda e, s: e.memset(stp[:], 0.25))
    p.op("vector", lambda e, s: e.tensor_copy(out=pmyeb[:], in_=pmye[:]), tag=True)

    p.w("tensor", "sync", dma=True)
    for do in range(NDC):
        wt = (wt0, wt1)[do % 2]
        if do >= 2:
            p.w("gpsimd", "tensor")
        p.dmao("gpsimd", wt[:], chunk(we_p, do))
        p.w("tensor", "gpsimd", dma=True)
        for half in range(2):
            psx = (pA, pB)[half]
            p.w("tensor", "scalar")
            for dj in range(NDC):
                p.op("tensor", lambda e, s, psx=psx, wt=wt, dj=dj, half=half: e.matmul(
                    psx[:, 0:512], wt[:, dj * 128:(dj + 1) * 128],
                    xln4[:, dj * T + half * 512: dj * T + half * 512 + 512],
                    start=(dj == 0), stop=(dj == NDC - 1)), tag=(dj == NDC - 1))
            p.w("scalar", "tensor")
            p.op("scalar", lambda e, s, psx=psx, do=do, half=half: e.activation(
                out=ebuf[:, do * T + half * 512: do * T + half * 512 + 512],
                in_=psx[:, 0:512], func=AF.Copy), tag=True)
            it = do * 2 + half
            if it < NBIS:
                emit_bisect_iter(p, it, pG_, sel8f, Tcol, Tb, pbis, junk, cnt, cntb, sel8t)


    p.op("vector", lambda e, s: e.tensor_scalar(
        out=junk[0:16, 0:1024], in0=pmye[:], scalar1=Tcol[0:16, 0:1], scalar2=None,
        op0=ALU.is_gt), tag=True)
    p.w("vector", "vector")
    p.op("vector", lambda e, s: e.tensor_tensor(
        out=junk[0:16, 0:1024], in0=junk[0:16, 0:1024], in1=pmyeb[:], op=ALU.mult), tag=True)
    p.w("tensor", "vector")
    for half in range(2):
        p.op("tensor", lambda e, s, half=half: e.matmul(
            pG_[0:1, 0:512], ones[0:16, 0:1], junk[0:16, half * 512:half * 512 + 512],
            start=True, stop=True), tag=True)
        p.w("vector", "tensor")
        p.op("vector", lambda e, s, half=half: e.tensor_copy(
            out=srow[0:1, half * 512:half * 512 + 512], in_=pG_[0:1, 0:512]), tag=True)
        p.w("tensor", "vector")
    for half in range(2):
        p.op("tensor", lambda e, s, half=half: e.matmul(
            pH_[:, 0:512], onesf[0:1, :], srow[0:1, half * 512:half * 512 + 512],
            start=True, stop=True), tag=True)
        p.w("vector", "tensor")
        p.op("vector", lambda e, s, half=half: e.tensor_copy(
            out=sbc[:, half * 512:half * 512 + 512], in_=pH_[:, 0:512]), tag=True)
        p.w("tensor", "vector")
    p.barrier()

    for do in range(NDC):
        b = (xs0, xs1)[do % 2]
        if do >= 2:
            p.w("sync", "sync", dma=True)
        p.dmao("sync", b[:], chunk(x3_in, do))
        p.w("vector", "sync", dma=True)
        p.op("vector", lambda e, s, do=do: e.scalar_tensor_tensor(
            out=scr[:], in0=ebuf[:, do * T:(do + 1) * T], scalar=webb[:, do:do + 1],
            in1=sbc[:], op0=ALU.add, op1=ALU.mult), tag=True)
        p.w("vector", "vector")
        p.op("vector", lambda e, s, b=b: e.tensor_tensor(
            out=b[:], in0=b[:], in1=scr[:], op=ALU.add), tag=True)
        p.w("sync", "vector")
        p.dmao("sync", chunk(out_o, do), b[:])
    p.wv("sync", "sync_dma", p.dma["sync"])
    p.wv("gpsimd", "gpsimd_dma", p.dma["gpsimd"])
    p.build(nc, ctx)
    ctx.close()
    return nc



def _panels(W):
    Wr = W.reshape(NDC, 128, NDC, 128)
    pan = Wr.transpose(2, 1, 0, 3)
    return np.ascontiguousarray(pan.reshape(NDC * 128, H))


_PERM = np.concatenate([np.arange(0, 128, 2), np.arange(1, 128, 2)])


def _l1_inmaps(inputs, nonce_bf, nonce_i):
    x = inputs["L0_attn_a_x"]
    wq_p = _panels(np.ascontiguousarray(inputs["L0_attn_a_Wq"]))
    wf_p = _panels(np.ascontiguousarray(inputs["L0_attn_a_Wf"]))
    wi_p = _panels(np.ascontiguousarray(inputs["L0_attn_a_Wi"]))
    wo_p = _panels(np.ascontiguousarray(inputs["L0_attn_a_Wo"]))
    gw = np.ascontiguousarray(inputs["L0_attn_a_g_weight"].reshape(NDC, 128).T)
    rw = np.ascontiguousarray(inputs["L0_ffn_router_w"].reshape(NDC, 128, 16)
                              .transpose(1, 0, 2).reshape(128, NDC * 16))
    rb = inputs["L0_ffn_router_b"].reshape(1, 16)
    u512 = np.ascontiguousarray(
        np.tile(np.triu(np.ones((64, 64), dtype=np.float32)), (2, 8))).astype(BF)
    ident = np.eye(128, dtype=np.float32).astype(BF)
    ones = np.ones((128, 128), dtype=np.float32).astype(BF)
    maps = []
    for c in range(8):
        b, h = divmod(c, 2)
        xs = np.ascontiguousarray(x[b, h * T:(h + 1) * T, :].T)
        role = float(h)
        maps.append({
            "x_fm": xs.astype(np.float32),
            "wq_p": wq_p, "wf_p": wf_p, "wi_p": wi_p, "wo_p": wo_p,
            "gw": gw.astype(np.float32), "rw": rw.astype(np.float32),
            "rb": rb.astype(np.float32),
            "u512": u512, "ident": ident, "ones128": ones,
            "roleflag": np.full((128, 1), role, dtype=np.float32),
            "exidx": (np.arange(128, dtype=np.uint32) + h * 128).reshape(128, 1),
            "exgidx": (np.arange(128, dtype=np.uint32) + (1 - h) * 128).reshape(128, 1),
            "exfidx": np.array([[256 + h], [256 + h]], dtype=np.uint32),
            "noncef": np.tile(nonce_bf, (2, 1024)),
            "noncei": nonce_i,
        })
    return maps




def _panels_g(W, nout):
    ndo = nout
    Wr = W.reshape(NDC, 128, ndo, 128)
    return np.ascontiguousarray(Wr.transpose(2, 1, 0, 3).reshape(ndo * 128, H))


def _perm_cols(W, nheads):
    Wp = W.copy()
    for h in range(nheads):
        Wp[:, h * 128:(h + 1) * 128] = Wp[:, h * 128:(h + 1) * 128][:, _PERM]
    return Wp


def _rw_layout(rw):
    return np.ascontiguousarray(rw.reshape(NDC, 128, 16).transpose(1, 0, 2)
                                .reshape(128, NDC * 16)).astype(np.float32)


def _l2_inmaps(inputs, res1, P_full, nonce_bf, nonce_i):
    WqT = _perm_cols(np.ascontiguousarray(inputs["L0_attn_b_Wq"].T), NH)
    WkT = _perm_cols(np.ascontiguousarray(inputs["L0_attn_b_Wk"].T), NKV)
    WvT = np.ascontiguousarray(inputs["L0_attn_b_Wv"].T)
    WoT = np.ascontiguousarray(inputs["L0_attn_b_Wo"].T)
    wqt_p = _panels_g(WqT, NDC)
    wkt_p = _panels_g(WkT, 4)
    wot_p = _panels_g(WoT, NDC)
    we_p = _panels_g(np.ascontiguousarray(inputs["L0_ffn_expert_w"]), NDC)
    web = np.ascontiguousarray(inputs["L0_ffn_expert_b"].reshape(NDC, 128).T).astype(np.float32)
    rw = _rw_layout(inputs["L0_ffn_router_w"])
    rb = inputs["L0_ffn_router_b"].reshape(1, 16).astype(np.float32)
    pbis = np.ascontiguousarray(P_full.T.reshape(16, 8, 1024).reshape(128, 1024)).astype(np.float32)
    ident = np.eye(128, dtype=np.float32).astype(BF)
    ones = np.ones((128, 128), dtype=np.float32).astype(BF)
    onesf = np.ones((128, 128), dtype=np.float32)
    sel8f = np.zeros((16, 128), dtype=np.float32)
    for e in range(16):
        sel8f[e, e * 8:(e + 1) * 8] = 1.0
    sel8t = sel8f.T.copy().astype(BF)
    theta = 10000.0
    freqs = 1.0 / theta ** (np.arange(0, 128, 2) / 128.0)
    SWAPM = np.zeros((128, 128), dtype=np.float32)
    for i in range(64):
        SWAPM[64 + i, i] = 1.0
        SWAPM[i, 64 + i] = 1.0
    SWAPM = SWAPM.astype(BF)
    kg = (np.arange(16)[None, :] * 128 + np.arange(128)[:, None]).astype(np.float32)
    maps = []
    for c in range(8):
        b, h = divmod(c, 2)
        pos = np.arange(T) + h * T
        ang = pos[None, :] * freqs[:, None]
        exi = np.zeros((128, 8), dtype=np.uint32)
        exg = np.zeros((128, 16), dtype=np.uint32)
        r = np.arange(128, dtype=np.uint32)
        for g in range(4):
            exi[:, g] = h * 1024 + g * 128 + r
        for vb in range(4):
            exi[:, 4 + vb] = h * 1024 + 512 + vb * 128 + r
        for g in range(4):
            for hf in range(2):
                exg[:, g * 2 + hf] = hf * 1024 + g * 128 + r
        for vbp in range(8):
            hf, vbb = divmod(vbp, 4)
            exg[:, 8 + vbp] = hf * 1024 + 512 + vbb * 128 + r
        maps.append({
            "x1i": res1[c]["x1"], "xln2i": res1[c]["xln2"],
            "pbis": pbis, "pmye": np.ascontiguousarray(P_full.T[:, c * T:(c + 1) * T]).astype(np.float32),
            "we_p": we_p, "web": web, "wqt_p": wqt_p, "wkt_p": wkt_p,
            "wvt": np.ascontiguousarray(WvT.reshape(16, 128, 512).transpose(1, 0, 2)
                                         .reshape(128, 16 * 512)).astype(np.float32),
            "wot_p": wot_p,
            "rw2": rw, "rb2": rb,
            "cosin": np.vstack([np.cos(ang), np.cos(ang)]).astype(np.float32),
            "sinin": np.vstack([-np.sin(ang), np.sin(ang)]).astype(np.float32),
            "swapin": SWAPM,
            "qgidx": pos.reshape(1, T).astype(np.float32), "kgidx": kg,
            "ident2": ident, "ones2": ones, "onesf2": onesf,
            "sel8f": sel8f, "sel8t": sel8t,
            "exidx2": exi, "exgidx2": exg,
            "exfidx2": np.array([[2048 + h], [2048 + h]], dtype=np.uint32),
            "noncef2": np.tile(nonce_bf, (2, 8)),
            "noncei2": nonce_i,
        })
    return maps


def _l3_inmaps(inputs, res2, P2_full):
    we_p = _panels_g(np.ascontiguousarray(inputs["L0_ffn_expert_w"]), NDC)
    web = np.ascontiguousarray(inputs["L0_ffn_expert_b"].reshape(NDC, 128).T).astype(np.float32)
    pbis = np.ascontiguousarray(P2_full.T.reshape(16, 8, 1024).reshape(128, 1024)).astype(np.float32)
    ones = np.ones((128, 128), dtype=np.float32).astype(BF)
    onesf = np.ones((128, 128), dtype=np.float32)
    sel8f = np.zeros((16, 128), dtype=np.float32)
    for e in range(16):
        sel8f[e, e * 8:(e + 1) * 8] = 1.0
    sel8t = sel8f.T.copy().astype(BF)
    maps = []
    for c in range(8):
        maps.append({
            "x3i": res2[c]["x3"], "xln4i": res2[c]["xln4"],
            "pbis3": pbis,
            "pmye3": np.ascontiguousarray(P2_full.T[:, c * T:(c + 1) * T]).astype(np.float32),
            "we_p3": we_p, "web3": web, "ones3": ones, "onesf3": onesf,
            "sel8f3": sel8f, "sel8t3": sel8t,
        })
    return maps


_NC_CACHE = {}


def kernel(**inputs):
    inputs = {k: np.asarray(v) for k, v in inputs.items()}
    nv = np.random.randint(1, 2 ** 31 - 1, dtype=np.int32)
    nonce_i = np.array([[nv]], dtype=np.int32)
    nonce_bf = np.frombuffer(nonce_i.tobytes(), dtype=BF).reshape(1, 2).copy()

    if "L1" not in _NC_CACHE:
        _NC_CACHE["L1"] = build_L1()
        _NC_CACHE["L2"] = build_L2()
        _NC_CACHE["L3"] = build_L3()
    res1 = run_bass_kernel_spmd(_NC_CACHE["L1"], _l1_inmaps(inputs, nonce_bf, nonce_i),
                                core_ids=list(range(8))).results
    P_full = np.concatenate([res1[c]["p1"] for c in range(8)], axis=0)
    nv2 = np.array([[np.int32(nv ^ 0x1234567)]], dtype=np.int32)
    nonce_bf2 = np.frombuffer(nv2.tobytes(), dtype=BF).reshape(1, 2).copy()
    res2 = run_bass_kernel_spmd(_NC_CACHE["L2"],
                                _l2_inmaps(inputs, res1, P_full, nonce_bf2, nv2),
                                core_ids=list(range(8))).results
    P2_full = np.concatenate([res2[c]["p2"] for c in range(8)], axis=0)
    res3 = run_bass_kernel_spmd(_NC_CACHE["L3"], _l3_inmaps(inputs, res2, P2_full),
                                core_ids=list(range(8))).results
    out = np.zeros((4, 2048, 2048), dtype=np.float32)
    for c in range(8):
        b, h = divmod(c, 2)
        out[b, h * T:(h + 1) * T, :] = res3[c]["outx"].T
    return out
